# version 02
# baseline (speedup 1.0000x reference)
import contextlib
import numpy as np
import ml_dtypes
import concourse.bass as bass
import concourse.mybir as mybir
from concourse.bass_utils import run_bass_kernel_spmd

F32 = mybir.dt.float32
BF16 = mybir.dt.bfloat16
I32 = mybir.dt.int32
I8 = mybir.dt.int8
AF = mybir.ActivationFunctionType
ALU = mybir.AluOpType
AX = mybir.AxisListType
BF = ml_dtypes.bfloat16

NCORE = 8
S = 8192
T = 16384
TSL = T // NCORE
NTT = TSL // 128

ENGS = ("sync", "vector", "scalar", "tensor", "gpsimd")


class Sem:
    def __init__(self, h):
        self.h = h
        self.v = 0


class Prog:

    def __init__(self):
        self.q = {k: [] for k in ENGS}

    def emit(self, eng, fn, inc=None, n=1):
        if inc is None:
            self.q[eng].append(fn)
        else:
            inc.v += n
            h = inc.h
            self.q[eng].append(lambda e, fn=fn, h=h, n=n:
                               fn(e).then_inc(h, n))

    def wait(self, eng, sem, val=None):
        v = sem.v if val is None else val
        self.q[eng].append(lambda e, s=sem.h, v=v: e.wait_ge(s, v))

    def play(self, blk):
        @blk.sync
        def _(e):
            for f in self.q["sync"]:
                f(e)

        @blk.vector
        def _(e):
            for f in self.q["vector"]:
                f(e)

        @blk.scalar
        def _(e):
            for f in self.q["scalar"]:
                f(e)

        @blk.tensor
        def _(e):
            for f in self.q["tensor"]:
                f(e)

        @blk.gpsimd
        def _(e):
            for f in self.q["gpsimd"]:
                f(e)



def _bucket(rel):
    rel = np.asarray(rel)
    nb, me = 16, 8
    ret = (rel > 0).astype(np.int64) * nb
    arel = np.abs(rel)
    small = arel < me
    rp = me + (np.log(np.maximum(arel, 1).astype(np.float64) / me)
               / np.log(128.0 / me) * (nb - me)).astype(np.int64)
    rp = np.minimum(rp, nb - 1)
    return ret + np.where(small, arel, rp)


def make_rel_ohT():
    t = np.arange(1280)
    bk = _bucket(639 - t)
    oh = np.zeros((32, 1280), np.float32)
    oh[bk, t] = 1.0
    return oh


L0_NEAR = (-128, 0, 128, 256, 384, 512)


def chunk128(w):
    w = np.asarray(w)
    k = w.shape[0] // 128
    return np.ascontiguousarray(
        w.reshape(k, 128, w.shape[1]).transpose(1, 0, 2)
        .reshape(128, k * w.shape[1]))



def sexp(P, out, in_, scale=1.0, bias=0.0, Sp=None):
    P.emit("scalar", lambda e: e.activation(out=out, in_=in_, func=AF.Exp,
                                            scale=scale, bias=bias), Sp)


NLE_SET = 6


def force_table(P, idx=NLE_SET):
    def em(e):
        return e.add_instruction(mybir.InstLoadActFuncSet(
            name=e.bass.get_next_instruction_name(), act_func_set_id=idx,
            ins=[], outs=[]))
    P.emit("scalar", em)


LN2_32 = float(np.float32(32 * np.log(2.0)))


def recip_lnexp(P, Sp, out, in_):
    P.emit("scalar", lambda e: e.activation(out=out, in_=in_, func=AF.Ln),
           Sp)
    P.wait("scalar", Sp)
    P.emit("scalar", lambda e: e.activation(out=out, in_=out, func=AF.Exp,
                                            scale=-1.0), Sp)


def recip_big(P, Sp, out, in_, nb):
    P.emit("scalar", lambda e: e.activation(out=out, in_=in_, func=AF.Ln,
                                            scale=float(2.0 ** -32)), Sp)
    P.wait("scalar", Sp)
    P.emit("scalar", lambda e: e.activation(out=out, in_=out, func=AF.Exp,
                                            scale=-1.0, bias=nb), Sp)


def ln_tile(P, Vp, Sp, xt, xn_out, sq, cols, want_rstd2=False):
    def V(fn):
        P.emit("vector", fn, Vp)
        P.wait("vector", Vp)

    P.emit("scalar", lambda e: e.activation(out=sq[:], in_=xt[:],
                                            func=AF.Square,
                                            accum_out=cols[:, 1:2]), Sp)
    P.wait("vector", Sp)
    V(lambda e: e.reduce_sum(out=cols[:, 0:1], in_=xt[:], axis=AX.X))
    V(lambda e: e.tensor_single_scalar(
        out=cols[:, 2:3], in_=cols[:, 0:1], scalar=1.0 / 1024, op=ALU.mult))
    V(lambda e: e.tensor_tensor(
        out=cols[:, 7:8], in0=cols[:, 2:3], in1=cols[:, 2:3], op=ALU.mult))
    V(lambda e: e.tensor_single_scalar(
        out=cols[:, 3:4], in_=cols[:, 1:2], scalar=1.0 / 1024, op=ALU.mult))
    V(lambda e: e.tensor_tensor(
        out=cols[:, 3:4], in0=cols[:, 3:4], in1=cols[:, 7:8],
        op=ALU.subtract))
    V(lambda e: e.tensor_single_scalar(
        out=cols[:, 7:8], in_=cols[:, 3:4], scalar=1e-5, op=ALU.add))
    P.wait("scalar", Vp)
    P.emit("scalar", lambda e: e.activation(out=cols[:, 4:5],
                                            in_=cols[:, 7:8], func=AF.Ln),
           Sp)
    P.wait("scalar", Sp)
    P.emit("scalar", lambda e: e.activation(out=cols[:, 4:5],
                                            in_=cols[:, 4:5], func=AF.Exp,
                                            scale=-0.5), Sp)
    P.wait("vector", Sp)
    V(lambda e: e.tensor_tensor(
        out=cols[:, 5:6], in0=cols[:, 2:3], in1=cols[:, 4:5], op=ALU.mult))
    V(lambda e: e.tensor_single_scalar(
        out=cols[:, 5:6], in_=cols[:, 5:6], scalar=-1.0, op=ALU.mult))
    P.wait("scalar", Vp)
    P.emit("scalar", lambda e: e.activation(out=xn_out[:], in_=xt[:],
                                            func=AF.Identity,
                                            scale=cols[:, 4:5],
                                            bias=cols[:, 5:6]), Sp)
    if want_rstd2:
        P.emit("scalar", lambda e: e.activation(out=cols[:, 6:7],
                                                in_=cols[:, 7:8],
                                                func=AF.Ln), Sp)
        P.wait("scalar", Sp)
        P.emit("scalar", lambda e: e.activation(out=cols[:, 6:7],
                                                in_=cols[:, 6:7],
                                                func=AF.Exp, scale=-1.0), Sp)
        P.wait("vector", Sp)
        V(lambda e: e.tensor_tensor(
            out=cols[:, 6:7], in0=cols[:, 3:4], in1=cols[:, 6:7],
            op=ALU.mult))
        V(lambda e: e.tensor_single_scalar(
            out=cols[:, 6:7], in_=cols[:, 6:7], scalar=1e-6, op=ALU.add))
        P.wait("scalar", Vp)
        P.emit("scalar", lambda e: e.activation(out=cols[:, 6:7],
                                                in_=cols[:, 6:7],
                                                func=AF.Ln), Sp)
        P.wait("scalar", Sp)
        P.emit("scalar", lambda e: e.activation(out=cols[:, 6:7],
                                                in_=cols[:, 6:7],
                                                func=AF.Exp, scale=-0.5), Sp)



def build_pre0():
    nc = bass.Bass("TRN2", target_bir_lowering=False, debug=False)
    x = nc.dram_tensor("x", [TSL, 1024], F32, kind="ExternalInput").ap()
    nw8 = nc.dram_tensor("nw8", [128, 8], F32, kind="ExternalInput").ap()
    rw64 = nc.dram_tensor("rw64", [128, 64], F32, kind="ExternalInput").ap()
    xnfm = nc.dram_tensor("xnfm", [1024, TSL], BF16,
                          kind="ExternalOutput").ap()
    r2o = nc.dram_tensor("r2o", [128, NTT], F32, kind="ExternalOutput").ap()
    cmo = nc.dram_tensor("cmo", [128, NTT * 8], F32,
                         kind="ExternalOutput").ap()

    es = contextlib.ExitStack()
    with es:
        sb = lambda n, s, d: es.enter_context(nc.sbuf_tensor(n, s, d))
        psr = lambda n, s, d: es.enter_context(nc.psum_tensor(n, s, d))
        sem = lambda n: Sem(es.enter_context(nc.semaphore(n)))
        xt = sb("xt", [128, 1024], F32)
        sq = sb("sq", [128, 1024], F32)
        xnb = sb("xnb", [128, 1024], BF16)
        xnfmt = sb("xnfmt", [128, 128], BF16)
        cols = sb("cols", [128, 8], F32)
        r2 = sb("r2", [128, NTT], F32)
        cmb = sb("cmb", [128, NTT * 8], F32)
        lgt = sb("lgt", [128, 8], F32)
        w8 = sb("w8", [128, 40], F32)
        rwf = sb("rwf", [128, 64], F32)
        rwb = sb("rwb", [128, 64], BF16)
        nwt = sb("nwt", [128, 8], F32)
        ones_t = sb("ones", [128, 128], BF16)
        idn = sb("idn", [128, 128], BF16)
        ptr = psr("ptr", [128, 128], BF16)
        plg = psr("plg", [128, 8], F32)
        lda = sem("s_lda")
        lxt = sem("s_lxt")
        sxn = sem("s_sxn")
        sfin = sem("s_sfin")
        Vp = sem("s_v")
        Sp = sem("s_s")
        Tm = sem("s_t")
        G = sem("s_g")
        blk = es.enter_context(nc.Block())
        P = Prog()
        force_table(P)

        P.emit("sync", lambda e: e.dma_start(out=nwt[:], in_=nw8), lda, 16)
        P.emit("sync", lambda e: e.dma_start(out=rwf[:], in_=rw64), lda, 16)
        lda_w = lda.v

        P.emit("vector", lambda e: e.memset(ones_t[:], 1.0))
        P.wait("vector", lda, lda_w)
        for k in range(8):
            a, b2, nwc = (rwb[:, 8 * k:8 * k + 8], rwf[:, 8 * k:8 * k + 8],
                          nwt[:, k:k + 1])
            P.emit("vector", lambda e, a=a, b2=b2, nwc=nwc:
                   e.tensor_single_scalar(out=a, in_=b2, scalar=nwc,
                                          op=ALU.mult),
                   Vp if k == 7 else None)
        v_rwb = Vp.v
        P.wait("gpsimd", Vp, v_rwb)
        P.emit("gpsimd", lambda e: e.affine_select(
            out=idn[:], in_=ones_t[:], pattern=[[-1, 128]],
            compare_op=ALU.is_equal, fill=0.0, base=0, channel_multiplier=1),
            G)
        g_idn = G.v
        P.wait("tensor", G, g_idn)

        for i in range(NTT):
            P.wait("sync", Sp)
            src = x[i * 128:(i + 1) * 128, :]
            P.emit("sync", lambda e, src=src: e.dma_start(out=xt[:],
                                                          in_=src), lxt, 16)
            lx = lxt.v
            P.wait("scalar", lxt, lx)
            P.wait("scalar", Tm)
            P.wait("vector", lxt, lx)
            ln_tile(P, Vp, Sp, xt, xnb, sq, cols, want_rstd2=True)
            P.wait("vector", Sp)
            dst = r2[:, i:i + 1]
            P.emit("vector", lambda e, dst=dst: e.tensor_copy(
                out=dst, in_=cols[:, 6:7]), Vp)

            for k in range(8):
                P.wait("tensor", Sp)
                P.wait("tensor", Vp)
                src = xnb[:, 128 * k:128 * (k + 1)]
                P.emit("tensor", lambda e, src=src: e.transpose(
                    ptr[:], src, idn[:]), Tm)
                P.wait("vector", Tm)
                P.wait("vector", sxn)
                P.emit("vector", lambda e: e.tensor_copy(out=xnfmt[:],
                                                         in_=ptr[:]), Vp)
                v_ev = Vp.v
                P.wait("sync", Vp, v_ev)
                dst = xnfm[128 * k:128 * (k + 1), i * 128:(i + 1) * 128]
                P.emit("sync", lambda e, dst=dst: e.dma_start(
                    out=dst, in_=xnfmt[:]), sxn, 16)
                P.wait("tensor", Vp, v_ev)
                rw_k = rwb[:, 8 * k:8 * k + 8]
                P.emit("tensor", lambda e, rw_k=rw_k, k=k: e.matmul(
                    plg[:], xnfmt[:], rw_k, start=(k == 0), stop=(k == 7)),
                    Tm)

            P.wait("scalar", Tm)
            P.wait("scalar", Sp)
            P.emit("scalar", lambda e: e.activation(
                out=lgt[:], in_=plg[:], func=AF.Identity,
                scale=cols[:, 6:7]), Sp)
            P.wait("vector", Sp)
            P.emit("vector", lambda e: e.reduce_max(
                out=w8[:, 0:1], in_=lgt[:], axis=AX.X, negate=True), Vp)
            P.wait("scalar", Vp)
            P.emit("scalar", lambda e: e.activation(
                out=w8[:, 8:16], in_=lgt[:], func=AF.Exp, bias=w8[:, 0:1]),
                Sp)
            P.wait("vector", Sp)

            def V(fn):
                P.emit("vector", fn, Vp)
                P.wait("vector", Vp)
            V(lambda e: e.tensor_single_scalar(
                out=w8[:, 16:24], in_=w8[:, 8:16], scalar=1.0, op=ALU.is_ge))
            V(lambda e: e.tensor_tensor(
                out=w8[:, 24:32], in0=w8[:, 8:16], in1=w8[:, 16:24],
                op=ALU.mult))
            V(lambda e: e.tensor_tensor(
                out=w8[:, 24:32], in0=w8[:, 8:16], in1=w8[:, 24:32],
                op=ALU.subtract))
            V(lambda e: e.reduce_max(
                out=w8[:, 1:2], in_=w8[:, 24:32], axis=AX.X))
            V(lambda e: e.tensor_single_scalar(
                out=w8[:, 32:40], in_=w8[:, 24:32], scalar=w8[:, 1:2],
                op=ALU.is_ge))
            V(lambda e: e.tensor_tensor(
                out=w8[:, 16:24], in0=w8[:, 16:24], in1=w8[:, 32:40],
                op=ALU.add))
            V(lambda e: e.tensor_tensor(
                out=w8[:, 16:24], in0=w8[:, 8:16], in1=w8[:, 16:24],
                op=ALU.mult))
            V(lambda e: e.reduce_sum(
                out=w8[:, 2:3], in_=w8[:, 16:24], axis=AX.X))
            P.wait("scalar", Vp)
            recip_lnexp(P, Sp, w8[:, 3:4], w8[:, 2:3])
            P.wait("vector", Sp)
            dst = cmb[:, 8 * i:8 * i + 8]
            P.emit("vector", lambda e, dst=dst: e.tensor_single_scalar(
                out=dst, in_=w8[:, 16:24], scalar=w8[:, 3:4], op=ALU.mult),
                Vp)

        P.wait("sync", Vp)
        P.emit("sync", lambda e: e.dma_start(out=r2o, in_=r2[:]), sfin, 16)
        P.emit("sync", lambda e: e.dma_start(out=cmo, in_=cmb[:]), sfin, 16)
        P.wait("sync", sfin)
        P.wait("sync", sxn)
        P.play(blk)
    return nc



def build_A0(stage=3, dbg=False):
    nc = bass.Bass("TRN2", target_bir_lowering=False, debug=False)
    xn = nc.dram_tensor("xn", [1024, T], BF16, kind="ExternalInput").ap()
    wq = nc.dram_tensor("wq", [128, 8 * 768], F32, kind="ExternalInput").ap()
    rel4 = nc.dram_tensor("rel4", [32, 4], F32, kind="ExternalInput").ap()
    relohT = nc.dram_tensor("relohT", [32, 1280], F32,
                            kind="ExternalInput").ap()
    gup = nc.dram_tensor("gup", [128, 8 * 5632], F32,
                         kind="ExternalInput").ap()
    dwn = nc.dram_tensor("dwn", [128, 22 * 1024], F32,
                         kind="ExternalInput").ap()
    combr = nc.dram_tensor("combr", [1, T], F32, kind="ExternalInput").ap()
    alpha = nc.dram_tensor("alpha", [1, 1], F32, kind="ExternalInput").ap()
    r2r = nc.dram_tensor("r2r", [1, T], F32, kind="ExternalInput").ap()
    nw8 = nc.dram_tensor("nw8", [128, 8], F32, kind="ExternalInput").ap()
    yo = nc.dram_tensor("yo", [256, S], BF16, kind="ExternalOutput").ap()
    mpo = nc.dram_tensor("mpo", [1024, T], BF16, kind="ExternalOutput").ap()
    if dbg:
        dqo = nc.dram_tensor("dqo", [128, S], BF16,
                             kind="ExternalOutput").ap()
        dko = nc.dram_tensor("dko", [128, S], BF16,
                             kind="ExternalOutput").ap()
        dvo = nc.dram_tensor("dvo", [128, 65 * 64], BF16,
                             kind="ExternalOutput").ap()

    NKT = S // 128
    NQC = S // 512
    es = contextlib.ExitStack()
    with es:
        sb = lambda n, s, d: es.enter_context(nc.sbuf_tensor(n, s, d))
        psr = lambda n, s, d: es.enter_context(nc.psum_tensor(n, s, d))
        sem = lambda n: Sem(es.enter_context(nc.semaphore(n)))
        xch = [sb("xch0", [128, 8 * 512], BF16),
               sb("xch1", [128, 8 * 512], BF16)]
        ones1 = sb("ones1", [1, 128], BF16)
        ones1f = sb("ones1f", [1, 128], F32)
        alf = sb("alf", [1, 128], BF16)
        alfs = sb("alfs", [1, 4], F32)
        stg = sb("stg", [1, 1024], F32)
        stgb = sb("stgb", [1, 512], BF16)
        farb = sb("farb", [128, 8], F32)
        nwt = sb("nwt", [128, 8], F32)
        psA = [psr("psA0", [128, 512], F32), psr("psA1", [128, 512], F32)]
        psB = [psr("psB0", [128, 512], F32), psr("psB1", [128, 512], F32)]
        outA = psr("outA", [128, 512], F32)
        outB = psr("outB", [128, 512], F32)
        paux = psr("paux", [128, 512], F32)
        paux2 = psr("paux2", [128, 512], F32)
        lda = sem("s_lda")
        ldg = sem("s_ldg")
        ldm = sem("s_ldm")
        lx = [sem("s_lx0"), sem("s_lx1")]
        lsg = sem("s_lsg")
        lws = sem("s_lws")
        sty = sem("s_sty")
        smo = sem("s_smo")
        sdbg = sem("s_sdbg")
        Vp = sem("s_v")
        Sp = sem("s_s")
        Mm = sem("s_m")
        Pv = sem("s_p")
        ph = sem("s_ph")

        blk = es.enter_context(nc.Block())

        with contextlib.ExitStack() as es1:
            sb1 = lambda n, s, d: es1.enter_context(nc.sbuf_tensor(n, s, d))
            qf = sb1("qf", [128, S], BF16)
            kf = sb1("kf", [128, S], BF16)
            vA = sb1("vA", [128, 65 * 64], BF16)
            vB = sb1("vB", [128, 65 * 64], BF16)
            wqb = sb1("wqb", [128, 8 * 768], BF16)
            megaA = sb1("megaA", [128, 1152], F32)
            megaB = sb1("megaB", [128, 1152], F32)
            relst = sb1("relst", [32, 1284], F32)
            diagR = sb1("diagR", [4, 1280], F32)
            dgr = sb1("dgr", [1, 4 * 1280], F32)
            eA = sb1("eA", [128, 2 * 512], BF16)
            eB = sb1("eB", [128, 2 * 512], BF16)
            sf32 = sb1("sf32", [128, 512], F32)
            rnrow = sb1("rnrow", [1, 1024], F32)
            rnb = sb1("rnb", [64, 512], F32)
            ych = sb1("ych", [128, 512], BF16)
            wst = sb1("wst", [128, 6144], F32)
            P = Prog()
            force_table(P)

            P.emit("sync", lambda e: e.dma_start(out=wst[:], in_=wq),
                   lda, 16)
            P.emit("sync", lambda e: e.dma_start(out=relst[:, 0:4],
                                                 in_=rel4), lda, 16)
            P.emit("sync", lambda e: e.dma_start(out=relst[:, 4:1284],
                                                 in_=relohT), lda, 16)
            P.emit("sync", lambda e: e.dma_start(out=alfs[0:1, 0:1],
                                                 in_=alpha), lda, 16)
            P.emit("sync", lambda e: e.dma_start(out=nwt[:], in_=nw8),
                   lda, 16)
            lda_all = lda.v

            P.emit("vector", lambda e: e.memset(ones1[:], 1.0))
            P.emit("vector", lambda e: e.memset(ones1f[:], 1.0))
            va_ones = vA[:].rearrange("p (t c) -> p t c", c=65)[:, :, 64:65]
            vb_ones = vB[:].rearrange("p (t c) -> p t c", c=65)[:, :, 64:65]
            P.emit("vector", lambda e: e.memset(va_ones, 1.0))
            P.emit("vector", lambda e: e.memset(vb_ones, 1.0))
            P.wait("vector", lda, lda_all)
            P.emit("vector", lambda e: e.tensor_copy(out=wqb[:],
                                                     in_=wst[:]), Vp)
            P.emit("vector", lambda e: e.tensor_single_scalar(
                out=alfs[0:1, 1:2], in_=alfs[0:1, 0:1], scalar=-0.5,
                op=ALU.max), Vp)
            P.wait("vector", Vp)
            P.emit("vector", lambda e: e.tensor_single_scalar(
                out=alfs[0:1, 1:2], in_=alfs[0:1, 1:2], scalar=2.0,
                op=ALU.min), Vp)
            v_setup = Vp.v
            P.wait("scalar", Vp, v_setup)
            P.emit("scalar", lambda e: e.activation(
                out=alf[:], in_=ones1[:], func=AF.Identity,
                scale=alfs[0:1, 1:2]), Sp)

            P.wait("tensor", lda, lda_all)
            for c0 in range(0, 1280, 320):
                rhs = relst[0:32, 4 + c0:4 + c0 + 320]
                P.emit("tensor", lambda e, rhs=rhs: e.matmul(
                    paux[0:4, 0:320], relst[0:32, 0:4], rhs,
                    start=True, stop=True), Mm)
                P.wait("vector", Mm)
                dst = diagR[0:4, c0:c0 + 320]
                P.emit("vector", lambda e, dst=dst: e.tensor_copy(
                    out=dst, in_=paux[0:4, 0:320]), Vp)
                P.wait("tensor", Vp)
            v_diag = Vp.v
            P.wait("sync", Vp, v_diag)
            for hh in range(4):
                dst = dgr[0:1, hh * 1280:(hh + 1) * 1280]
                srcr = diagR[hh:hh + 1, :]
                P.emit("sync", lambda e, dst=dst, srcr=srcr: e.dma_start(
                    out=dst, in_=srcr), ldg, 16)
            ldg_all = ldg.v

            P.wait("tensor", ldg, ldg_all)
            for hh in range(4):
                for side in range(2):
                    tpos = 1278 if side == 0 else 0
                    outp = paux2[:, hh * 2 + side:hh * 2 + side + 1]
                    rhs = dgr[0:1, hh * 1280 + tpos:hh * 1280 + tpos + 1]
                    P.emit("tensor", lambda e, outp=outp, rhs=rhs: e.matmul(
                        outp, ones1f[0:1, :], rhs, start=True, stop=True),
                        Mm if (hh, side) == (3, 1) else None)
            m_far = Mm.v
            P.wait("vector", Mm, m_far)
            P.emit("vector", lambda e: e.tensor_copy(out=farb[:, 0:8],
                                                     in_=paux2[:, 0:8]), Vp)
            v_far = Vp.v

            for pair in range(2):
                P.wait("tensor", Vp)
                P.wait("gpsimd", ldg, ldg_all)
                P.wait("gpsimd", Sp)
                for i in range(128):
                    for mg, hh in ((megaA, 2 * pair), (megaB, 2 * pair + 1)):
                        dst = mg[i:i + 1, :]
                        src = dgr[0:1, hh * 1280 + 127 - i:
                                  hh * 1280 + 127 - i + 1152]
                        P.emit("gpsimd", lambda e, dst=dst, src=src:
                               e.dma_start(out=dst, in_=src), ldm, 16)
                ldm_pair = ldm.v

                for tc in range(NQC):
                    par = tc & 1
                    xb = xch[par]
                    P.wait("sync", Mm)
                    for ch in range(8):
                        dst = xb[:, ch * 512:(ch + 1) * 512]
                        src = xn[ch * 128:(ch + 1) * 128,
                                 tc * 512:(tc + 1) * 512]
                        P.emit("sync", lambda e, dst=dst, src=src:
                               e.dma_start(out=dst, in_=src), lx[par], 16)
                    lx_t = lx[par].v
                    P.wait("tensor", lx[par], lx_t)
                    for which, base in (("q", 0), ("k", 256)):
                        P.wait("tensor", Sp)
                        for ch in range(8):
                            lhsT = wqb[:, ch * 768 + base + pair * 128:
                                       ch * 768 + base + pair * 128 + 128]
                            rhs = xb[:, ch * 512:(ch + 1) * 512]
                            P.emit("tensor",
                                   lambda e, lhsT=lhsT, rhs=rhs, ch=ch:
                                   e.matmul(paux[:], lhsT, rhs,
                                            start=(ch == 0), stop=(ch == 7)),
                                   Mm if ch == 7 else None)
                        m_qk = Mm.v
                        dstf = (qf if which == "q"
                                else kf)[:, tc * 512:(tc + 1) * 512]
                        P.wait("scalar", Mm, m_qk)
                        P.emit("scalar", lambda e, dstf=dstf: e.activation(
                            out=dstf, in_=paux[:], func=AF.Identity), Sp)
                    for sub in range(4):
                        kt = tc * 4 + sub
                        P.wait("tensor", Vp)
                        for ch in range(8):
                            lhsT = xb[:, ch * 512 + sub * 128:
                                      ch * 512 + sub * 128 + 128]
                            rhs = wqb[:, ch * 768 + 512 + pair * 128:
                                      ch * 768 + 512 + pair * 128 + 128]
                            P.emit("tensor",
                                   lambda e, lhsT=lhsT, rhs=rhs, ch=ch:
                                   e.matmul(paux2[:, 0:128], lhsT, rhs,
                                            start=(ch == 0), stop=(ch == 7)),
                                   Mm if ch == 7 else None)
                        m_v = Mm.v
                        dstA = vA[:, kt * 65:kt * 65 + 64]
                        dstB = vB[:, kt * 65:kt * 65 + 64]
                        P.wait("vector", Mm, m_v)
                        P.emit("vector", lambda e, dstA=dstA: e.tensor_copy(
                            out=dstA, in_=paux2[:, 0:64]))
                        P.emit("vector", lambda e, dstB=dstB: e.tensor_copy(
                            out=dstB, in_=paux2[:, 64:128]), Vp)

                if dbg and pair == 0:
                    P.wait("sync", Vp)
                    P.wait("sync", Sp)
                    P.emit("sync", lambda e: e.dma_start(out=dqo,
                                                         in_=qf[:]),
                           sdbg, 16)
                    P.emit("sync", lambda e: e.dma_start(out=dko,
                                                         in_=kf[:]),
                           sdbg, 16)
                    P.emit("sync", lambda e: e.dma_start(out=dvo,
                                                         in_=vA[:]),
                           sdbg, 16)
                if stage < 2:
                    break

                P.wait("tensor", Vp)
                P.wait("scalar", ldm, ldm_pair)
                P.wait("scalar", Vp, v_far)
                P.wait("vector", ldm, ldm_pair)
                for qc in range(NQC):
                    sp_qc0 = Sp.v
                    mm_qc0 = Mm.v
                    pv_qc0 = Pv.v
                    q0 = qc * 512
                    qslA = qf[0:64, q0:q0 + 512]
                    qslB = qf[64:128, q0:q0 + 512]
                    for kt in range(NKT):
                        par = kt & 1
                        P.wait("tensor", Sp,
                               sp_qc0 + (2 * (kt - 1) if kt >= 2 else 0))
                        kslA = kf[0:64, kt * 128:(kt + 1) * 128]
                        kslB = kf[64:128, kt * 128:(kt + 1) * 128]
                        pA, pB = psA[par], psB[par]
                        P.emit("tensor",
                               lambda e, kslA=kslA, qslA=qslA, pA=pA:
                               e.matmul(pA[:], kslA, qslA, start=True,
                                        stop=True, tile_position=(0, 0)))
                        P.emit("tensor",
                               lambda e, kslB=kslB, qslB=qslB, pB=pB:
                               e.matmul(pB[:], kslB, qslB, start=True,
                                        stop=True, tile_position=(64, 0)),
                               Mm)
                        if kt >= 1:
                            ktp = kt - 1
                            parp = ktp & 1
                            P.wait("tensor", Sp, sp_qc0 + 2 * (ktp + 1))
                            vslA = vA[:, ktp * 65:(ktp + 1) * 65]
                            vslB = vB[:, ktp * 65:(ktp + 1) * 65]
                            eslA = eA[:, parp * 512:parp * 512 + 512]
                            eslB = eB[:, parp * 512:parp * 512 + 512]
                            P.emit("tensor",
                                   lambda e, vslA=vslA, eslA=eslA, ktp=ktp:
                                   e.matmul(outA[0:65, :], vslA, eslA,
                                            start=(ktp == 0), stop=False))
                            P.emit("tensor",
                                   lambda e, vslB=vslB, eslB=eslB, ktp=ktp:
                                   e.matmul(outB[0:65, :], vslB, eslB,
                                            start=(ktp == 0), stop=False),
                                   Pv)
                        d0 = kt * 128 - q0
                        P.wait("scalar", Mm, mm_qc0 + (kt + 1))
                        P.wait("scalar", Pv,
                               pv_qc0 + (kt - 1 if kt >= 2 else 0))
                        for hh in range(2):
                            pX = pA if hh == 0 else pB
                            eX = eA if hh == 0 else eB
                            mg = megaA if hh == 0 else megaB
                            edst = eX[:, par * 512:par * 512 + 512]
                            if d0 in L0_NEAR:
                                s0 = 512 - d0
                                msl = mg[:, s0:s0 + 512]
                                P.wait("vector", Mm, mm_qc0 + (kt + 1))
                                P.wait("vector", Sp)
                                P.emit("vector",
                                       lambda e, pX=pX, msl=msl:
                                       e.tensor_tensor(out=sf32[:],
                                                       in0=pX[:], in1=msl,
                                                       op=ALU.add), Vp)
                                P.wait("scalar", Vp)
                                P.emit("scalar", lambda e, edst=edst:
                                       e.activation(out=edst, in_=sf32[:],
                                                    func=AF.Exp), Sp)
                            else:
                                side = 0 if d0 < 0 else 1
                                bcol = farb[:, (2 * pair + hh) * 2 + side:
                                            (2 * pair + hh) * 2 + side + 1]
                                P.emit("scalar",
                                       lambda e, edst=edst, pX=pX,
                                       bcol=bcol: e.activation(
                                           out=edst, in_=pX[:], func=AF.Exp,
                                           bias=bcol), Sp)
                    ktp = NKT - 1
                    parp = ktp & 1
                    P.wait("tensor", Sp, sp_qc0 + 2 * (ktp + 1))
                    vslA = vA[:, ktp * 65:(ktp + 1) * 65]
                    vslB = vB[:, ktp * 65:(ktp + 1) * 65]
                    eslA = eA[:, parp * 512:parp * 512 + 512]
                    eslB = eB[:, parp * 512:parp * 512 + 512]
                    P.emit("tensor", lambda e, vslA=vslA, eslA=eslA:
                           e.matmul(outA[0:65, :], vslA, eslA, start=False,
                                    stop=True))
                    P.emit("tensor", lambda e, vslB=vslB, eslB=eslB:
                           e.matmul(outB[0:65, :], vslB, eslB, start=False,
                                    stop=True), Pv)
                    pv_done = Pv.v

                    P.wait("vector", Pv, pv_done)
                    P.emit("vector", lambda e: e.reciprocal(
                        out=rnrow[0:1, 0:512], in_=outA[64:65, :]), Vp)
                    P.wait("vector", Vp)
                    P.emit("vector", lambda e: e.reciprocal(
                        out=rnrow[0:1, 512:1024], in_=outB[64:65, :]), Vp)
                    P.wait("vector", Vp)
                    s_rn = Vp.v
                    for hh in range(2):
                        P.wait("tensor", Vp, s_rn if hh == 0 else Vp.v)
                        rsl = rnrow[0:1, hh * 512:hh * 512 + 512]
                        P.emit("tensor", lambda e, rsl=rsl: e.matmul(
                            paux[0:64, :], ones1f[0:1, 0:64], rsl,
                            start=True, stop=True), Mm)
                        m_bc = Mm.v
                        P.wait("vector", Mm, m_bc)
                        P.emit("vector", lambda e: e.tensor_copy(
                            out=rnb[:], in_=paux[0:64, :]), Vp)
                        P.wait("vector", Vp)
                        P.wait("vector", sty)
                        oX = outA if hh == 0 else outB
                        ysl = ych[hh * 64:hh * 64 + 64, :]
                        P.emit("vector", lambda e, oX=oX, ysl=ysl:
                               e.tensor_tensor(out=ysl, in0=oX[0:64, :],
                                               in1=rnb[:], op=ALU.mult), Vp)
                    v_y = Vp.v
                    P.wait("scalar", Vp, v_y)
                    dst = yo[pair * 128:(pair + 1) * 128, q0:q0 + 512]
                    P.emit("scalar", lambda e, dst=dst: e.dma_start(
                        out=dst, in_=ych[:]), sty, 16)
                    P.wait("tensor", Vp, v_y)

            P.play(blk)
            PB = Prog()
            for eng in ENGS:
                PB.emit(eng, lambda e: e.nop(), ph)
            for eng in ENGS:
                PB.wait(eng, ph, 5 * (1 if True else 1))
            PB.play(blk)

        if stage < 3:
            PF = Prog()
            PF.wait("sync", sty)
            PF.wait("sync", sdbg)
            PF.wait("gpsimd", ldm)
            PF.play(blk)
            return nc

        with contextlib.ExitStack() as es2:
            sb2 = lambda n, s, d: es2.enter_context(nc.sbuf_tensor(n, s, d))
            gupb = sb2("gupb", [128, 8 * 5632], BF16)
            dwnb = sb2("dwnb", [128, 22 * 1024], BF16)
            P = Prog()
            with contextlib.ExitStack() as es2a:
                wst2 = es2a.enter_context(
                    nc.sbuf_tensor("wst2", [128, 2816], F32))
                force_table(P, 18)
                for ch in range(8):
                    for hlf in range(2):
                        src = gup[:, ch * 5632 + hlf * 2816:
                                  ch * 5632 + (hlf + 1) * 2816]
                        P.wait("sync", Vp)
                        P.emit("sync", lambda e, src=src: e.dma_start(
                            out=wst2[:], in_=src), lws, 16)
                        lw_c = lws.v
                        P.wait("vector", lws, lw_c)
                        dst = gupb[:, ch * 5632 + hlf * 2816:
                                   ch * 5632 + (hlf + 1) * 2816]
                        nwc = nwt[:, ch:ch + 1]
                        P.emit("vector", lambda e, dst=dst, nwc=nwc:
                               e.tensor_single_scalar(out=dst, in_=wst2[:],
                                                      scalar=nwc,
                                                      op=ALU.mult), Vp)
                for ch in range(11):
                    src = dwn[:, ch * 2048:(ch + 1) * 2048]
                    P.wait("sync", Vp)
                    P.emit("sync", lambda e, src=src: e.dma_start(
                        out=wst2[:, 0:2048], in_=src), lws, 16)
                    lw_c = lws.v
                    P.wait("vector", lws, lw_c)
                    dst = dwnb[:, ch * 2048:(ch + 1) * 2048]
                    P.emit("vector", lambda e, dst=dst: e.tensor_copy(
                        out=dst, in_=wst2[:, 0:2048]), Vp)
                P.play(blk)
                P = Prog()

            x2b = sb2("x2b", [128, 8 * 512], BF16)
            rbb = sb2("rbb", [128, 512], BF16)
            cbb = sb2("cbb", [128, 512], F32)
            gs = sb2("gs", [128, 512], BF16)
            ups = sb2("ups", [128, 512], BF16)
            hid = sb2("hid", [128, 22 * 512], BF16)
            mo = sb2("mo", [128, 512], BF16)

            v_w = Vp.v
            P.wait("tensor", Vp, v_w)
            NTC = T // 512
            for tc in range(NTC):
                par = tc & 1
                xb = xch[par]
                P.wait("sync", Mm)
                for ch in range(8):
                    dst = xb[:, ch * 512:(ch + 1) * 512]
                    src = xn[ch * 128:(ch + 1) * 128,
                             tc * 512:(tc + 1) * 512]
                    P.emit("sync", lambda e, dst=dst, src=src: e.dma_start(
                        out=dst, in_=src), lx[par], 16)
                lx_t = lx[par].v
                srcr = r2r[0:1, tc * 512:(tc + 1) * 512]
                P.emit("sync", lambda e, srcr=srcr: e.dma_start(
                    out=stg[0:1, 0:512], in_=srcr), lsg, 16)
                srcc = combr[0:1, tc * 512:(tc + 1) * 512]
                P.emit("sync", lambda e, srcc=srcc: e.dma_start(
                    out=stg[0:1, 512:1024], in_=srcc), lsg, 16)
                lsg_t = lsg.v
                P.wait("vector", lsg, lsg_t)
                P.emit("vector", lambda e: e.tensor_copy(
                    out=stgb[:], in_=stg[0:1, 512:1024]), Vp)
                v_st = Vp.v
                P.wait("tensor", Vp, v_st)
                P.emit("tensor", lambda e: e.matmul(
                    paux[:], ones1f[0:1, :], stg[0:1, 0:512], start=True,
                    stop=True), Mm)
                P.wait("tensor", Sp)
                P.emit("tensor", lambda e: e.matmul(
                    paux2[:], alf[0:1, :], stgb[0:1, :], start=True,
                    stop=True), Mm)
                m_bc = Mm.v
                P.wait("vector", Mm, m_bc)
                P.emit("vector", lambda e: e.tensor_copy(out=rbb[:],
                                                         in_=paux[:]))
                P.emit("vector", lambda e: e.tensor_copy(out=cbb[:],
                                                         in_=paux2[:]), Vp)
                v_bc = Vp.v
                P.wait("vector", lx[par], lx_t)
                for ch in range(8):
                    a = x2b[:, ch * 512:(ch + 1) * 512]
                    b2 = xb[:, ch * 512:(ch + 1) * 512]
                    P.emit("vector", lambda e, a=a, b2=b2: e.tensor_tensor(
                        out=a, in0=b2, in1=rbb[:], op=ALU.mult),
                        Vp if ch == 7 else None)
                v_x2 = Vp.v

                for hp in range(22):
                    P.wait("tensor", Vp, v_x2)
                    P.wait("tensor", Sp)
                    for ch in range(8):
                        lhsT = gupb[:, ch * 5632 + hp * 128:
                                    ch * 5632 + hp * 128 + 128]
                        rhs = x2b[:, ch * 512:(ch + 1) * 512]
                        P.emit("tensor", lambda e, lhsT=lhsT, rhs=rhs, ch=ch:
                               e.matmul(paux[:], lhsT, rhs, start=(ch == 0),
                                        stop=(ch == 7)))
                    for ch in range(8):
                        lhsT = gupb[:, ch * 5632 + (22 + hp) * 128:
                                    ch * 5632 + (22 + hp) * 128 + 128]
                        rhs = x2b[:, ch * 512:(ch + 1) * 512]
                        P.emit("tensor", lambda e, lhsT=lhsT, rhs=rhs, ch=ch:
                               e.matmul(paux2[:], lhsT, rhs,
                                        start=(ch == 0), stop=(ch == 7)),
                               Mm if ch == 7 else None)
                    m_gu = Mm.v
                    P.wait("scalar", Mm, m_gu)
                    P.wait("scalar", Vp)
                    P.emit("scalar", lambda e: e.activation(
                        out=gs[:], in_=paux[:], func=AF.Silu), Sp)
                    P.emit("scalar", lambda e: e.activation(
                        out=ups[:], in_=paux2[:], func=AF.Identity), Sp)
                    s_gu = Sp.v
                    P.wait("vector", Sp, s_gu)
                    hsl = hid[:, hp * 512:(hp + 1) * 512]
                    P.emit("vector", lambda e, hsl=hsl: e.tensor_tensor(
                        out=hsl, in0=gs[:], in1=ups[:], op=ALU.mult), Vp)
                    P.wait("tensor", Sp, s_gu)
                v_hid = Vp.v

                for dch in range(8):
                    P.wait("tensor", Vp, v_hid)
                    for hch in range(22):
                        lhsT = dwnb[:, hch * 1024 + dch * 128:
                                    hch * 1024 + dch * 128 + 128]
                        rhs = hid[:, hch * 512:(hch + 1) * 512]
                        P.emit("tensor",
                               lambda e, lhsT=lhsT, rhs=rhs, hch=hch:
                               e.matmul(paux[:], lhsT, rhs,
                                        start=(hch == 0), stop=(hch == 21)),
                               Mm if hch == 21 else None)
                    m_dn = Mm.v
                    P.wait("vector", Mm, m_dn)
                    P.wait("vector", Vp, v_bc)
                    P.wait("vector", smo)
                    P.emit("vector", lambda e: e.tensor_tensor(
                        out=mo[:], in0=paux[:], in1=cbb[:], op=ALU.mult),
                        Vp)
                    v_mo = Vp.v
                    P.wait("tensor", Vp, v_mo)
                    P.wait("sync", Vp, v_mo)
                    dst = mpo[dch * 128:(dch + 1) * 128,
                              tc * 512:(tc + 1) * 512]
                    P.emit("sync", lambda e, dst=dst: e.dma_start(
                        out=dst, in_=mo[:]), smo, 16)

            P.wait("sync", smo)
            P.wait("sync", sty)
            P.wait("gpsimd", ldm)
            if dbg:
                P.wait("sync", sdbg)
            P.play(blk)
    return nc



def build_B0():
    nc = bass.Bass("TRN2", target_bir_lowering=False, debug=False)
    x0 = nc.dram_tensor("x0", [TSL, 1024], F32, kind="ExternalInput").ap()
    yfm = nc.dram_tensor("yfm", [1024, TSL], BF16,
                         kind="ExternalInput").ap()
    wout = nc.dram_tensor("wout", [128, 8 * 1024], F32,
                          kind="ExternalInput").ap()
    mp = nc.dram_tensor("mp", [8 * TSL, 1024], BF16,
                        kind="ExternalInput").ap()
    x1o = nc.dram_tensor("x1o", [TSL, 1024], F32,
                         kind="ExternalOutput").ap()
    xnfm = nc.dram_tensor("xnfm", [1024, TSL], BF16,
                          kind="ExternalOutput").ap()

    es = contextlib.ExitStack()
    with es:
        sb = lambda n, s, d: es.enter_context(nc.sbuf_tensor(n, s, d))
        psr = lambda n, s, d: es.enter_context(nc.psum_tensor(n, s, d))
        sem = lambda n: Sem(es.enter_context(nc.semaphore(n)))
        xt = sb("xt", [128, 1024], F32)
        acc = sb("acc", [128, 1024], F32)
        sq = sb("sq", [128, 1024], F32)
        xn0 = sb("xn0", [128, 1024], F32)
        xnb = sb("xnb", [128, 1024], BF16)
        xnfmt = sb("xnfmt", [128, 128], BF16)
        cols = sb("cols", [128, 8], F32)
        mptb = sb("mptb", [128, 1024], BF16)
        mptf = sb("mptf", [128, 1024], F32)
        ysl = sb("ysl", [128, 8 * TSL], BF16)
        wof = sb("wof", [128, 8 * 1024], F32)
        wob = sb("wob", [128, 8 * 1024], BF16)
        ones_t = sb("ones", [128, 128], BF16)
        idn = sb("idn", [128, 128], BF16)
        ptr = psr("ptr", [128, 128], BF16)
        pj = [psr("pj0", [128, 512], F32), psr("pj1", [128, 512], F32)]
        lda = sem("s_lda")
        lxt = sem("s_lxt")
        lmp = sem("s_lmp")
        sx1 = sem("s_sx1")
        sxn = sem("s_sxn")
        Vp = sem("s_v")
        Sp = sem("s_s")
        Tm = sem("s_t")
        G = sem("s_g")
        blk = es.enter_context(nc.Block())
        P = Prog()
        force_table(P)

        P.emit("sync", lambda e: e.dma_start(out=wof[:], in_=wout), lda, 16)
        for ch in range(8):
            dsty = ysl[:, ch * TSL:(ch + 1) * TSL]
            srcy = yfm[ch * 128:(ch + 1) * 128, :]
            P.emit("sync", lambda e, dsty=dsty, srcy=srcy: e.dma_start(
                out=dsty, in_=srcy), lda, 16)
        lda_all = lda.v
        P.emit("vector", lambda e: e.memset(ones_t[:], 1.0))
        P.wait("vector", lda, lda_all)
        P.emit("vector", lambda e: e.tensor_copy(out=wob[:], in_=wof[:]),
               Vp)
        v_w = Vp.v
        P.wait("gpsimd", Vp, v_w)
        P.emit("gpsimd", lambda e: e.affine_select(
            out=idn[:], in_=ones_t[:], pattern=[[-1, 128]],
            compare_op=ALU.is_equal, fill=0.0, base=0, channel_multiplier=1),
            G)
        g_idn = G.v
        P.wait("tensor", G, g_idn)
        P.wait("tensor", Vp, v_w)

        for i in range(NTT):
            P.wait("sync", Vp)
            P.wait("sync", Sp)
            src = x0[i * 128:(i + 1) * 128, :]
            P.emit("sync", lambda e, src=src: e.dma_start(out=xt[:],
                                                          in_=src), lxt, 16)
            lx_i = lxt.v
            P.wait("scalar", lxt, lx_i)
            P.wait("scalar", Tm)
            P.wait("vector", lxt, lx_i)
            ln_tile(P, Vp, Sp, xt, xn0, sq, cols, want_rstd2=False)
            s_ln0 = Sp.v
            P.wait("tensor", Vp)
            for half in range(2):
                for ch in range(8):
                    lhsT = ysl[:, ch * TSL + i * 128:
                               ch * TSL + (i + 1) * 128]
                    rhs = wob[:, ch * 1024 + half * 512:
                              ch * 1024 + half * 512 + 512]
                    pjh = pj[half]
                    P.emit("tensor",
                           lambda e, lhsT=lhsT, rhs=rhs, ch=ch, pjh=pjh:
                           e.matmul(pjh[:], lhsT, rhs, start=(ch == 0),
                                    stop=(ch == 7)),
                           Tm if ch == 7 else None)
            m_pj = Tm.v
            P.wait("vector", Sp, s_ln0)
            P.wait("vector", sx1)
            P.emit("vector", lambda e: e.tensor_tensor(
                out=acc[:], in0=xt[:], in1=xn0[:], op=ALU.add), Vp)
            P.wait("vector", Tm, m_pj)
            P.emit("vector", lambda e: e.tensor_tensor(
                out=acc[:, 0:512], in0=acc[:, 0:512], in1=pj[0][:],
                op=ALU.add))
            P.emit("vector", lambda e: e.tensor_tensor(
                out=acc[:, 512:1024], in0=acc[:, 512:1024], in1=pj[1][:],
                op=ALU.add), Vp)
            for ee in range(8):
                src = mp[ee * TSL + i * 128:ee * TSL + (i + 1) * 128, :]
                P.wait("sync", Vp)
                P.emit("sync", lambda e, src=src: e.dma_start(out=mptb[:],
                                                              in_=src),
                       lmp, 16)
                lm_i = lmp.v
                P.wait("vector", lmp, lm_i)
                P.emit("vector", lambda e: e.tensor_copy(out=mptf[:],
                                                         in_=mptb[:]), Vp)
                P.wait("vector", Vp)
                P.emit("vector", lambda e: e.tensor_tensor(
                    out=acc[:], in0=acc[:], in1=mptf[:], op=ALU.add), Vp)
            v_acc = Vp.v
            P.wait("sync", Vp, v_acc)
            dst = x1o[i * 128:(i + 1) * 128, :]
            P.emit("sync", lambda e, dst=dst: e.dma_start(out=dst,
                                                          in_=acc[:]),
                   sx1, 16)
            P.wait("scalar", Vp, v_acc)
            P.wait("scalar", Tm)
            ln_tile(P, Vp, Sp, acc, xnb, sq, cols, want_rstd2=False)
            for k in range(8):
                P.wait("tensor", Sp)
                P.wait("tensor", Vp)
                src = xnb[:, 128 * k:128 * (k + 1)]
                P.emit("tensor", lambda e, src=src: e.transpose(
                    ptr[:], src, idn[:]), Tm)
                P.wait("vector", Tm)
                P.wait("vector", sxn)
                P.emit("vector", lambda e: e.tensor_copy(out=xnfmt[:],
                                                         in_=ptr[:]), Vp)
                v_ev = Vp.v
                P.wait("sync", Vp, v_ev)
                dst = xnfm[128 * k:128 * (k + 1), i * 128:(i + 1) * 128]
                P.emit("sync", lambda e, dst=dst: e.dma_start(
                    out=dst, in_=xnfmt[:]), sxn, 16)

        P.wait("sync", sxn)
        P.wait("sync", sx1)
        P.play(blk)
    return nc



def build_A1():
    nc = bass.Bass("TRN2", target_bir_lowering=False, debug=False)
    xn = nc.dram_tensor("xn", [1024, T], BF16, kind="ExternalInput").ap()
    xnf = nc.dram_tensor("xnf", [1024, TSL], BF16, kind="ExternalInput").ap()
    wq = nc.dram_tensor("wq", [128, 8 * 768], F32, kind="ExternalInput").ap()
    qkb = nc.dram_tensor("qkb", [128, 4], F32, kind="ExternalInput").ap()
    vbr = nc.dram_tensor("vbr", [1, 256], F32, kind="ExternalInput").ap()
    wg8 = nc.dram_tensor("wg8", [128, 8 * 3072], I8,
                         kind="ExternalInput").ap()
    wu8 = nc.dram_tensor("wu8", [128, 8 * 3072], I8,
                         kind="ExternalInput").ap()
    wd8 = nc.dram_tensor("wd8", [128, 24 * 1024], I8,
                         kind="ExternalInput").ap()
    yo = nc.dram_tensor("yo", [256, S], F32, kind="ExternalOutput").ap()
    oo = nc.dram_tensor("oo", [1024, TSL], F32, kind="ExternalOutput").ap()

    NKT = S // 128
    NQC = S // 512
    es = contextlib.ExitStack()
    with es:
        sb = lambda n, s, d: es.enter_context(nc.sbuf_tensor(n, s, d))
        psr = lambda n, s, d: es.enter_context(nc.psum_tensor(n, s, d))
        sem = lambda n: Sem(es.enter_context(nc.semaphore(n)))
        xch = [sb("xch0", [128, 8 * 512], BF16),
               sb("xch1", [128, 8 * 512], BF16)]
        ones1 = sb("ones1", [1, 128], BF16)
        ones1f = sb("ones1f", [1, 128], F32)
        ones_t = sb("onest", [128, 128], BF16)
        idn = sb("idn", [128, 128], BF16)
        qkbs = sb("qkbs", [128, 8], F32)
        vbb = sb("vbb", [1, 256], BF16)
        vbf = sb("vbf", [1, 256], F32)
        psA = [psr("psA0", [128, 512], F32), psr("psA1", [128, 512], F32)]
        psB = [psr("psB0", [128, 512], F32), psr("psB1", [128, 512], F32)]
        outA = psr("outA", [128, 512], F32)
        outB = psr("outB", [128, 512], F32)
        paux = psr("paux", [128, 512], F32)
        paux2 = psr("paux2", [128, 512], F32)
        lda = sem("s_lda")
        lx = [sem("s_lx0"), sem("s_lx1")]
        lxf = sem("s_lxf")
        lws = sem("s_lws")
        sty = sem("s_sty")
        soo = sem("s_soo")
        Vp = sem("s_v")
        Sp = sem("s_s")
        Mm = sem("s_m")
        Pv = sem("s_p")
        G = sem("s_g")
        ph = sem("s_ph")
        blk = es.enter_context(nc.Block())

        with contextlib.ExitStack() as es1:
            sb1 = lambda n, s, d: es1.enter_context(nc.sbuf_tensor(n, s, d))
            qha = sb1("qha", [128, S], BF16)
            qhb = sb1("qhb", [128, S], BF16)
            kha = sb1("kha", [128, S], BF16)
            khb = sb1("khb", [128, S], BF16)
            vA = sb1("vA", [128, 65 * 64], BF16)
            vB = sb1("vB", [128, 65 * 64], BF16)
            wqb = sb1("wqb", [128, 8 * 768], BF16)
            eA = sb1("eA", [128, 2 * 512], BF16)
            eB = sb1("eB", [128, 2 * 512], BF16)
            nmx = sb1("nmx", [128, 128], F32)
            nmb = sb1("nmb", [128, 1], BF16)
            rnrow = sb1("rnrow", [1, 1024], F32)
            rnb = sb1("rnb", [64, 512], F32)
            ych = sb1("ych", [128, 512], F32)
            wst = sb1("wst", [128, 6144], F32)
            P = Prog()
            force_table(P)

            P.emit("sync", lambda e: e.dma_start(out=wst[:], in_=wq),
                   lda, 16)
            P.emit("sync", lambda e: e.dma_start(out=qkbs[:, 0:4],
                                                 in_=qkb), lda, 16)
            P.emit("sync", lambda e: e.dma_start(out=vbf[:], in_=vbr),
                   lda, 16)
            lda_all = lda.v
            P.emit("vector", lambda e: e.memset(ones1[:], 1.0))
            P.emit("vector", lambda e: e.memset(ones1f[:], 1.0))
            P.emit("vector", lambda e: e.memset(ones_t[:], 1.0))
            va_ones = vA[:].rearrange("p (t c) -> p t c", c=65)[:, :, 64:65]
            vb_ones = vB[:].rearrange("p (t c) -> p t c", c=65)[:, :, 64:65]
            P.emit("vector", lambda e: e.memset(va_ones, 1.0))
            P.emit("vector", lambda e: e.memset(vb_ones, 1.0))
            P.emit("vector", lambda e: e.memset(kha[64:65, :], 1.0))
            P.emit("vector", lambda e: e.memset(khb[64:65, :], 1.0))
            P.wait("vector", lda, lda_all)
            P.emit("vector", lambda e: e.tensor_copy(out=wqb[:],
                                                     in_=wst[:]))
            P.emit("vector", lambda e: e.tensor_copy(out=vbb[:],
                                                     in_=vbf[:]))
            P.emit("vector", lambda e: e.tensor_single_scalar(
                out=qkbs[:, 4:6], in_=qkbs[:, 0:2], scalar=8.0,
                op=ALU.mult), Vp)
            v_setup = Vp.v
            P.wait("gpsimd", Vp, v_setup)
            P.emit("gpsimd", lambda e: e.affine_select(
                out=idn[:], in_=ones_t[:], pattern=[[-1, 128]],
                compare_op=ALU.is_equal, fill=0.0, base=0,
                channel_multiplier=1), G)
            g_idn = G.v

            for pair in range(2):
                P.wait("tensor", Vp)
                for tc in range(NQC):
                    par = tc & 1
                    xb = xch[par]
                    P.wait("sync", Mm)
                    for ch in range(8):
                        dst = xb[:, ch * 512:(ch + 1) * 512]
                        src = xn[ch * 128:(ch + 1) * 128,
                                 tc * 512:(tc + 1) * 512]
                        P.emit("sync", lambda e, dst=dst, src=src:
                               e.dma_start(out=dst, in_=src), lx[par], 16)
                    lx_t = lx[par].v
                    P.wait("tensor", lx[par], lx_t)
                    for which, base in (("q", 0), ("k", 256)):
                        P.wait("tensor", Sp)
                        for ch in range(8):
                            lhsT = wqb[:, ch * 768 + base + pair * 128:
                                       ch * 768 + base + pair * 128 + 128]
                            rhs = xb[:, ch * 512:(ch + 1) * 512]
                            P.emit("tensor",
                                   lambda e, lhsT=lhsT, rhs=rhs, ch=ch:
                                   e.matmul(paux[:], lhsT, rhs,
                                            start=(ch == 0),
                                            stop=(ch == 7)),
                                   Mm if ch == 7 else None)
                        m_qk = Mm.v
                        P.wait("scalar", Mm, m_qk)
                        if which == "q":
                            dA = qha[0:64, tc * 512:(tc + 1) * 512]
                            dB = qhb[0:64, tc * 512:(tc + 1) * 512]
                            bidx, scl = 4, 8.0
                        else:
                            dA = kha[0:64, tc * 512:(tc + 1) * 512]
                            dB = khb[0:64, tc * 512:(tc + 1) * 512]
                            bidx, scl = 2, 1.0
                        bA = qkbs[0:64, bidx + pair:bidx + pair + 1]
                        bB = qkbs[64:128, bidx + pair:bidx + pair + 1]
                        P.emit("scalar", lambda e, dA=dA, bA=bA, scl=scl:
                               e.activation(out=dA, in_=paux[0:64, :],
                                            func=AF.Identity, scale=scl,
                                            bias=bA), Sp)
                        P.emit("scalar", lambda e, dB=dB, bB=bB, scl=scl:
                               e.activation(out=dB, in_=paux[64:128, :],
                                            func=AF.Identity, scale=scl,
                                            bias=bB), Sp)
                    for sub in range(4):
                        kt = tc * 4 + sub
                        P.wait("tensor", Vp)
                        for ch in range(8):
                            lhsT = xb[:, ch * 512 + sub * 128:
                                      ch * 512 + sub * 128 + 128]
                            rhs = wqb[:, ch * 768 + 512 + pair * 128:
                                      ch * 768 + 512 + pair * 128 + 128]
                            P.emit("tensor",
                                   lambda e, lhsT=lhsT, rhs=rhs, ch=ch:
                                   e.matmul(paux2[:, 0:128], lhsT, rhs,
                                            start=(ch == 0), stop=False))
                        vbsl = vbb[0:1, pair * 128:pair * 128 + 128]
                        P.emit("tensor", lambda e, vbsl=vbsl: e.matmul(
                            paux2[:, 0:128], ones1[0:1, :], vbsl,
                            start=False, stop=True), Mm)
                        m_v = Mm.v
                        dstA = vA[:, kt * 65:kt * 65 + 64]
                        dstB = vB[:, kt * 65:kt * 65 + 64]
                        P.wait("vector", Mm, m_v)
                        P.emit("vector", lambda e, dstA=dstA: e.tensor_copy(
                            out=dstA, in_=paux2[:, 0:64]))
                        P.emit("vector", lambda e, dstB=dstB: e.tensor_copy(
                            out=dstB, in_=paux2[:, 64:128]), Vp)

                P.wait("tensor", Sp)
                for qt in range(NKT):
                    qslA = qha[0:64, qt * 128:(qt + 1) * 128]
                    qslB = qhb[0:64, qt * 128:(qt + 1) * 128]
                    ca = nmx[:, 0:1]
                    cb = nmx[:, 1:2]
                    for kt2 in range(NQC):
                        par = kt2 & 1
                        P.wait("tensor", Vp)
                        kslA = kha[0:64, kt2 * 512:(kt2 + 1) * 512]
                        kslB = khb[0:64, kt2 * 512:(kt2 + 1) * 512]
                        pA, pB = psA[par], psB[par]
                        P.emit("tensor",
                               lambda e, qslA=qslA, kslA=kslA, pA=pA:
                               e.matmul(pA[:], qslA, kslA, start=True,
                                        stop=True))
                        P.emit("tensor",
                               lambda e, qslB=qslB, kslB=kslB, pB=pB:
                               e.matmul(pB[:], qslB, kslB, start=True,
                                        stop=True), Mm)
                        m_s = Mm.v
                        P.wait("vector", Mm, m_s)
                        if kt2 == 0:
                            P.emit("vector", lambda e, ca=ca, pA=pA:
                                   e.reduce_max(out=ca, in_=pA[:],
                                                axis=AX.X, negate=True), Vp)
                            P.wait("vector", Vp)
                            P.emit("vector", lambda e, cb=cb, pB=pB:
                                   e.reduce_max(out=cb, in_=pB[:],
                                                axis=AX.X, negate=True), Vp)
                            P.wait("vector", Vp)
                        else:
                            P.emit("vector", lambda e, pA=pA:
                                   e.reduce_max(out=nmx[:, 2:3], in_=pA[:],
                                                axis=AX.X, negate=True), Vp)
                            P.wait("vector", Vp)
                            P.emit("vector", lambda e, ca=ca:
                                   e.tensor_tensor(out=ca, in0=ca,
                                                   in1=nmx[:, 2:3],
                                                   op=ALU.min), Vp)
                            P.wait("vector", Vp)
                            P.emit("vector", lambda e, pB=pB:
                                   e.reduce_max(out=nmx[:, 2:3], in_=pB[:],
                                                axis=AX.X, negate=True), Vp)
                            P.wait("vector", Vp)
                            P.emit("vector", lambda e, cb=cb:
                                   e.tensor_tensor(out=cb, in0=cb,
                                                   in1=nmx[:, 2:3],
                                                   op=ALU.min), Vp)
                            P.wait("vector", Vp)
                    for hh in range(2):
                        col = nmx[:, hh:hh + 1]
                        P.wait("vector", Mm)
                        P.emit("vector", lambda e, col=col: e.tensor_copy(
                            out=nmb[:], in_=col), Vp)
                        v_nm = Vp.v
                        P.wait("tensor", Vp, v_nm)
                        P.wait("tensor", G, g_idn)
                        P.wait("tensor", Sp)
                        P.emit("tensor", lambda e: e.matmul(
                            paux[0:1, 0:128], nmb[:], idn[:], start=True,
                            stop=True), Mm)
                        m_t = Mm.v
                        qh = qha if hh == 0 else qhb
                        dst = qh[64:65, qt * 128:(qt + 1) * 128]
                        P.wait("scalar", Mm, m_t)
                        P.emit("scalar", lambda e, dst=dst: e.activation(
                            out=dst, in_=paux[0:1, 0:128],
                            func=AF.Identity), Sp)

                P.wait("tensor", Sp)
                for qc in range(NQC):
                    sp_qc0 = Sp.v
                    mm_qc0 = Mm.v
                    pv_qc0 = Pv.v
                    q0 = qc * 512
                    for kt in range(NKT):
                        par = kt & 1
                        P.wait("tensor", Sp,
                               sp_qc0 + (2 * (kt - 1) if kt >= 2 else 0))
                        pA, pB = psA[par], psB[par]
                        kslA = kha[0:65, kt * 128:(kt + 1) * 128]
                        kslB = khb[0:65, kt * 128:(kt + 1) * 128]
                        qslA = qha[0:65, q0:q0 + 512]
                        qslB = qhb[0:65, q0:q0 + 512]
                        P.emit("tensor",
                               lambda e, kslA=kslA, qslA=qslA, pA=pA:
                               e.matmul(pA[:], kslA, qslA, start=True,
                                        stop=True))
                        P.emit("tensor",
                               lambda e, kslB=kslB, qslB=qslB, pB=pB:
                               e.matmul(pB[:], kslB, qslB, start=True,
                                        stop=True), Mm)
                        if kt >= 1:
                            ktp = kt - 1
                            parp = ktp & 1
                            P.wait("tensor", Sp, sp_qc0 + 2 * (ktp + 1))
                            vslA = vA[:, ktp * 65:(ktp + 1) * 65]
                            vslB = vB[:, ktp * 65:(ktp + 1) * 65]
                            eslA = eA[:, parp * 512:parp * 512 + 512]
                            eslB = eB[:, parp * 512:parp * 512 + 512]
                            P.emit("tensor",
                                   lambda e, vslA=vslA, eslA=eslA, ktp=ktp:
                                   e.matmul(outA[0:65, :], vslA, eslA,
                                            start=(ktp == 0), stop=False))
                            P.emit("tensor",
                                   lambda e, vslB=vslB, eslB=eslB, ktp=ktp:
                                   e.matmul(outB[0:65, :], vslB, eslB,
                                            start=(ktp == 0), stop=False),
                                   Pv)
                        P.wait("scalar", Mm, mm_qc0 + (kt + 1))
                        P.wait("scalar", Pv,
                               pv_qc0 + (kt - 1 if kt >= 2 else 0))
                        for hh in range(2):
                            pX = pA if hh == 0 else pB
                            eX = eA if hh == 0 else eB
                            edst = eX[:, par * 512:par * 512 + 512]
                            P.emit("scalar", lambda e, edst=edst, pX=pX:
                                   e.activation(out=edst, in_=pX[:],
                                                func=AF.Exp), Sp)
                    ktp = NKT - 1
                    parp = ktp & 1
                    P.wait("tensor", Sp, sp_qc0 + 2 * (ktp + 1))
                    vslA = vA[:, ktp * 65:(ktp + 1) * 65]
                    vslB = vB[:, ktp * 65:(ktp + 1) * 65]
                    eslA = eA[:, parp * 512:parp * 512 + 512]
                    eslB = eB[:, parp * 512:parp * 512 + 512]
                    P.emit("tensor", lambda e, vslA=vslA, eslA=eslA:
                           e.matmul(outA[0:65, :], vslA, eslA, start=False,
                                    stop=True))
                    P.emit("tensor", lambda e, vslB=vslB, eslB=eslB:
                           e.matmul(outB[0:65, :], vslB, eslB, start=False,
                                    stop=True), Pv)
                    pv_done = Pv.v
                    P.wait("vector", Pv, pv_done)
                    P.emit("vector", lambda e: e.reciprocal(
                        out=rnrow[0:1, 0:512], in_=outA[64:65, :]), Vp)
                    P.wait("vector", Vp)
                    P.emit("vector", lambda e: e.reciprocal(
                        out=rnrow[0:1, 512:1024], in_=outB[64:65, :]), Vp)
                    P.wait("vector", Vp)
                    for hh in range(2):
                        P.wait("tensor", Vp)
                        rsl = rnrow[0:1, hh * 512:hh * 512 + 512]
                        P.emit("tensor", lambda e, rsl=rsl: e.matmul(
                            paux[0:64, :], ones1f[0:1, 0:64], rsl,
                            start=True, stop=True), Mm)
                        m_bc = Mm.v
                        P.wait("vector", Mm, m_bc)
                        P.emit("vector", lambda e: e.tensor_copy(
                            out=rnb[:], in_=paux[0:64, :]), Vp)
                        P.wait("vector", Vp)
                        P.wait("vector", sty)
                        oX = outA if hh == 0 else outB
                        ysl = ych[hh * 64:hh * 64 + 64, :]
                        P.emit("vector", lambda e, oX=oX, ysl=ysl:
                               e.tensor_tensor(out=ysl, in0=oX[0:64, :],
                                               in1=rnb[:], op=ALU.mult), Vp)
                    v_y = Vp.v
                    P.wait("scalar", Vp, v_y)
                    dst = yo[pair * 128:(pair + 1) * 128, q0:q0 + 512]
                    P.emit("scalar", lambda e, dst=dst: e.dma_start(
                        out=dst, in_=ych[:]), sty, 16)
                    P.wait("tensor", Vp, v_y)

            P.play(blk)
            PB = Prog()
            for eng in ENGS:
                PB.emit(eng, lambda e: e.nop(), ph)
            for eng in ENGS:
                PB.wait(eng, ph, 5)
            PB.play(blk)

        with contextlib.ExitStack() as es2:
            sb2 = lambda n, s, d: es2.enter_context(nc.sbuf_tensor(n, s, d))
            wgb = sb2("wgb", [128, 8 * 3072], BF16)
            wub = sb2("wub", [128, 8 * 3072], BF16)
            wdb = sb2("wdb", [128, 24 * 1024], BF16)
            wsti = sb2("wsti", [128, 6144], I8)
            gs = sb2("gs", [128, 512], BF16)
            ups = sb2("ups", [128, 512], BF16)
            hid = sb2("hid", [128, 24 * 512], BF16)
            mo = sb2("mo", [128, 512], F32)
            P = Prog()
            force_table(P, 18)
            for (wsrc, wdst2) in ((wg8, wgb), (wu8, wub), (wd8, wdb)):
                for r0 in range(0, 24576, 6144):
                    srcw = wsrc[:, r0:r0 + 6144]
                    P.wait("sync", Vp)
                    P.emit("sync", lambda e, srcw=srcw: e.dma_start(
                        out=wsti[:], in_=srcw), lws, 16)
                    lw_c = lws.v
                    P.wait("vector", lws, lw_c)
                    dstw = wdst2[:, r0:r0 + 6144]
                    P.emit("vector", lambda e, dstw=dstw: e.tensor_copy(
                        out=dstw, in_=wsti[:]), Vp)
            v_w = Vp.v
            P.wait("tensor", Vp, v_w)

            for tcf in range(4):
                par = tcf & 1
                xb = xch[par]
                P.wait("sync", Mm)
                for ch in range(8):
                    dst = xb[:, ch * 512:(ch + 1) * 512]
                    src = xnf[ch * 128:(ch + 1) * 128,
                              tcf * 512:(tcf + 1) * 512]
                    P.emit("sync", lambda e, dst=dst, src=src: e.dma_start(
                        out=dst, in_=src), lx[par], 16)
                lx_t = lx[par].v
                P.wait("tensor", lx[par], lx_t)
                for hp in range(24):
                    P.wait("tensor", Sp)
                    for ch in range(8):
                        lhsT = wgb[:, ch * 3072 + hp * 128:
                                   ch * 3072 + hp * 128 + 128]
                        rhs = xb[:, ch * 512:(ch + 1) * 512]
                        P.emit("tensor", lambda e, lhsT=lhsT, rhs=rhs, ch=ch:
                               e.matmul(paux[:], lhsT, rhs, start=(ch == 0),
                                        stop=(ch == 7)))
                    for ch in range(8):
                        lhsT = wub[:, ch * 3072 + hp * 128:
                                   ch * 3072 + hp * 128 + 128]
                        rhs = xb[:, ch * 512:(ch + 1) * 512]
                        P.emit("tensor", lambda e, lhsT=lhsT, rhs=rhs, ch=ch:
                               e.matmul(paux2[:], lhsT, rhs,
                                        start=(ch == 0), stop=(ch == 7)),
                               Mm if ch == 7 else None)
                    m_gu = Mm.v
                    P.wait("scalar", Mm, m_gu)
                    P.wait("scalar", Vp)
                    P.emit("scalar", lambda e: e.activation(
                        out=gs[:], in_=paux[:], func=AF.Silu,
                        scale=1.0 / 127.0), Sp)
                    P.emit("scalar", lambda e: e.activation(
                        out=ups[:], in_=paux2[:], func=AF.Identity), Sp)
                    s_gu = Sp.v
                    P.wait("vector", Sp, s_gu)
                    hsl = hid[:, hp * 512:(hp + 1) * 512]
                    P.emit("vector", lambda e, hsl=hsl: e.tensor_tensor(
                        out=hsl, in0=gs[:], in1=ups[:], op=ALU.mult), Vp)
                    P.wait("tensor", Sp, s_gu)
                v_hid = Vp.v
                for dch in range(8):
                    P.wait("tensor", Vp, v_hid)
                    P.wait("tensor", Sp)
                    for hch in range(24):
                        lhsT = wdb[:, hch * 1024 + dch * 128:
                                   hch * 1024 + dch * 128 + 128]
                        rhs = hid[:, hch * 512:(hch + 1) * 512]
                        P.emit("tensor",
                               lambda e, lhsT=lhsT, rhs=rhs, hch=hch:
                               e.matmul(paux[:], lhsT, rhs,
                                        start=(hch == 0), stop=(hch == 23)),
                               Mm if hch == 23 else None)
                    m_dn = Mm.v
                    P.wait("scalar", Mm, m_dn)
                    P.wait("scalar", soo)
                    P.emit("scalar", lambda e: e.activation(
                        out=mo[:], in_=paux[:], func=AF.Identity,
                        scale=1.0 / 16129.0), Sp)
                    s_mo = Sp.v
                    P.wait("scalar", Sp, s_mo)
                    dst = oo[dch * 128:(dch + 1) * 128,
                             tcf * 512:(tcf + 1) * 512]
                    P.emit("scalar", lambda e, dst=dst: e.dma_start(
                        out=dst, in_=mo[:]), soo, 16)
            P.wait("sync", soo)
            P.wait("sync", sty)
            P.play(blk)
    return nc



def build_B_add(n_f32, n_bf16, ln_out=True):
    nc = bass.Bass("TRN2", target_bir_lowering=False, debug=False)
    x0 = nc.dram_tensor("x0", [TSL, 1024], F32, kind="ExternalInput").ap()
    tf = (nc.dram_tensor("tf", [n_f32 * TSL, 1024], F32,
                         kind="ExternalInput").ap() if n_f32 else None)
    tb = (nc.dram_tensor("tb", [n_bf16 * TSL, 1024], BF16,
                         kind="ExternalInput").ap() if n_bf16 else None)
    x1o = nc.dram_tensor("x1o", [TSL, 1024], F32, kind="ExternalOutput").ap()
    if ln_out:
        xnfm = nc.dram_tensor("xnfm", [1024, TSL], BF16,
                              kind="ExternalOutput").ap()

    es = contextlib.ExitStack()
    with es:
        sb = lambda n, s, d: es.enter_context(nc.sbuf_tensor(n, s, d))
        psr = lambda n, s, d: es.enter_context(nc.psum_tensor(n, s, d))
        sem = lambda n: Sem(es.enter_context(nc.semaphore(n)))
        xt = sb("xt", [128, 1024], F32)
        acc = sb("acc", [128, 1024], F32)
        sq = sb("sq", [128, 1024], F32)
        xnb = sb("xnb", [128, 1024], BF16)
        xnfmt = sb("xnfmt", [128, 128], BF16)
        cols = sb("cols", [128, 8], F32)
        tft = sb("tft", [128, 1024], F32)
        tbt = sb("tbt", [128, 1024], BF16)
        tbf = sb("tbf", [128, 1024], F32)
        ones_t = sb("ones", [128, 128], BF16)
        idn = sb("idn", [128, 128], BF16)
        ptr = psr("ptr", [128, 128], BF16)
        lxt = sem("s_lxt")
        ltf = sem("s_ltf")
        ltb = sem("s_ltb")
        sx1 = sem("s_sx1")
        sxn = sem("s_sxn")
        Vp = sem("s_v")
        Sp = sem("s_s")
        Tm = sem("s_t")
        G = sem("s_g")
        blk = es.enter_context(nc.Block())
        P = Prog()
        force_table(P)
        P.emit("vector", lambda e: e.memset(ones_t[:], 1.0), Vp)
        v0 = Vp.v
        P.wait("gpsimd", Vp, v0)
        P.emit("gpsimd", lambda e: e.affine_select(
            out=idn[:], in_=ones_t[:], pattern=[[-1, 128]],
            compare_op=ALU.is_equal, fill=0.0, base=0, channel_multiplier=1),
            G)
        g_idn = G.v
        P.wait("tensor", G, g_idn)

        for i in range(NTT):
            P.wait("sync", Vp)
            src = x0[i * 128:(i + 1) * 128, :]
            P.emit("sync", lambda e, src=src: e.dma_start(out=xt[:],
                                                          in_=src), lxt, 16)
            lx_i = lxt.v
            P.wait("vector", lxt, lx_i)
            P.wait("vector", sx1)
            P.emit("vector", lambda e: e.tensor_copy(out=acc[:],
                                                     in_=xt[:]), Vp)
            for ee in range(n_f32):
                src = tf[ee * TSL + i * 128:ee * TSL + (i + 1) * 128, :]
                P.wait("sync", Vp)
                P.emit("sync", lambda e, src=src: e.dma_start(out=tft[:],
                                                              in_=src),
                       ltf, 16)
                lt_i = ltf.v
                P.wait("vector", ltf, lt_i)
                P.emit("vector", lambda e: e.tensor_tensor(
                    out=acc[:], in0=acc[:], in1=tft[:], op=ALU.add), Vp)
            for ee in range(n_bf16):
                src = tb[ee * TSL + i * 128:ee * TSL + (i + 1) * 128, :]
                P.wait("sync", Vp)
                P.emit("sync", lambda e, src=src: e.dma_start(out=tbt[:],
                                                              in_=src),
                       ltb, 16)
                lt_i = ltb.v
                P.wait("vector", ltb, lt_i)
                P.emit("vector", lambda e: e.tensor_copy(out=tbf[:],
                                                         in_=tbt[:]), Vp)
                P.wait("vector", Vp)
                P.emit("vector", lambda e: e.tensor_tensor(
                    out=acc[:], in0=acc[:], in1=tbf[:], op=ALU.add), Vp)
            v_acc = Vp.v
            P.wait("sync", Vp, v_acc)
            dst = x1o[i * 128:(i + 1) * 128, :]
            P.emit("sync", lambda e, dst=dst: e.dma_start(out=dst,
                                                          in_=acc[:]),
                   sx1, 16)
            if ln_out:
                P.wait("scalar", Vp, v_acc)
                P.wait("scalar", Tm)
                ln_tile(P, Vp, Sp, acc, xnb, sq, cols, want_rstd2=False)
                for k in range(8):
                    P.wait("tensor", Sp)
                    P.wait("tensor", Vp)
                    src = xnb[:, 128 * k:128 * (k + 1)]
                    P.emit("tensor", lambda e, src=src: e.transpose(
                        ptr[:], src, idn[:]), Tm)
                    P.wait("vector", Tm)
                    P.wait("vector", sxn)
                    P.emit("vector", lambda e: e.tensor_copy(
                        out=xnfmt[:], in_=ptr[:]), Vp)
                    v_ev = Vp.v
                    P.wait("sync", Vp, v_ev)
                    dst = xnfm[128 * k:128 * (k + 1),
                               i * 128:(i + 1) * 128]
                    P.emit("sync", lambda e, dst=dst: e.dma_start(
                        out=dst, in_=xnfmt[:]), sxn, 16)
            else:
                pass

        P.wait("sync", sx1)
        if ln_out:
            P.wait("sync", sxn)
        P.play(blk)
    return nc



def bisect_mask(P, Vp, acts, scr, cms, k, iters=40):
    def V(fn):
        P.emit("vector", fn, Vp)
        P.wait("vector", Vp)

    V(lambda e: e.reduce_max(out=scr[:, 1:2], in_=acts, axis=AX.X))
    V(lambda e: e.tensor_single_scalar(
        out=scr[:, 1:2], in_=scr[:, 1:2], scalar=1.0, op=ALU.add))
    V(lambda e: e.reduce_max(out=scr[:, 0:1], in_=acts, axis=AX.X,
                             negate=True))
    V(lambda e: e.tensor_single_scalar(
        out=scr[:, 0:1], in_=scr[:, 0:1], scalar=-1.0, op=ALU.mult))
    V(lambda e: e.tensor_single_scalar(
        out=scr[:, 0:1], in_=scr[:, 0:1], scalar=-1.0, op=ALU.add))
    for _ in range(iters):
        V(lambda e: e.tensor_tensor(out=scr[:, 2:3], in0=scr[:, 0:1],
                                    in1=scr[:, 1:2], op=ALU.add))
        V(lambda e: e.tensor_single_scalar(
            out=scr[:, 2:3], in_=scr[:, 2:3], scalar=0.5, op=ALU.mult))
        V(lambda e: e.tensor_single_scalar(
            out=cms, in_=acts, scalar=scr[:, 2:3], op=ALU.is_ge))
        V(lambda e: e.reduce_sum(out=scr[:, 3:4], in_=cms, axis=AX.X))
        V(lambda e: e.tensor_single_scalar(
            out=scr[:, 4:5], in_=scr[:, 3:4], scalar=float(k) - 0.5,
            op=ALU.is_ge))
        V(lambda e: e.tensor_tensor(out=scr[:, 5:6], in0=scr[:, 2:3],
                                    in1=scr[:, 0:1], op=ALU.subtract))
        V(lambda e: e.tensor_tensor(out=scr[:, 5:6], in0=scr[:, 5:6],
                                    in1=scr[:, 4:5], op=ALU.mult))
        V(lambda e: e.tensor_tensor(out=scr[:, 0:1], in0=scr[:, 0:1],
                                    in1=scr[:, 5:6], op=ALU.add))
        V(lambda e: e.tensor_tensor(out=scr[:, 5:6], in0=scr[:, 2:3],
                                    in1=scr[:, 1:2], op=ALU.subtract))
        V(lambda e: e.tensor_tensor(out=scr[:, 6:7], in0=scr[:, 5:6],
                                    in1=scr[:, 4:5], op=ALU.mult))
        V(lambda e: e.tensor_tensor(out=scr[:, 1:2], in0=scr[:, 1:2],
                                    in1=scr[:, 5:6], op=ALU.add))
        V(lambda e: e.tensor_tensor(out=scr[:, 1:2], in0=scr[:, 1:2],
                                    in1=scr[:, 6:7], op=ALU.subtract))
    V(lambda e: e.tensor_single_scalar(
        out=cms, in_=acts, scalar=scr[:, 0:1], op=ALU.is_ge))



def build_A2():
    nc = bass.Bass("TRN2", target_bir_lowering=False, debug=False)
    xn = nc.dram_tensor("xn", [1024, T], BF16, kind="ExternalInput").ap()
    wq = nc.dram_tensor("wq", [128, 8 * 768], F32, kind="ExternalInput").ap()
    qkb = nc.dram_tensor("qkb", [128, 6], F32, kind="ExternalInput").ap()
    vbr = nc.dram_tensor("vbr", [1, 256], F32, kind="ExternalInput").ap()
    gw1c = nc.dram_tensor("gw1c", [128, 8 * 512], F32,
                          kind="ExternalInput").ap()
    gw2c = nc.dram_tensor("gw2c", [128, 4 * 16], F32,
                          kind="ExternalInput").ap()
    gb1 = nc.dram_tensor("gb1", [128, 4], F32, kind="ExternalInput").ap()
    gb2 = nc.dram_tensor("gb2", [16, 1], F32, kind="ExternalInput").ap()
    lam = nc.dram_tensor("lam", [128, 128], F32, kind="ExternalInput").ap()
    gsel = nc.dram_tensor("gsel", [128, 4 * 4096], F32,
                          kind="ExternalInput").ap()
    sel2 = nc.dram_tensor("sel2", [128, 32 * 64], F32,
                          kind="ExternalInput").ap()
    wsta = nc.dram_tensor("wsta", [64, 4 * 64], F32,
                          kind="ExternalInput").ap()
    hsel = nc.dram_tensor("hsel", [16, 4 * 64], F32,
                          kind="ExternalInput").ap()
    cmsk = nc.dram_tensor("cmsk", [128, 4 * 512], F32,
                          kind="ExternalInput").ap()
    ipc = nc.dram_tensor("ipc", [128, 8 * 1024], F32,
                         kind="ExternalInput").ap()
    yo = nc.dram_tensor("yo", [256, S], BF16, kind="ExternalOutput").ap()
    iao = nc.dram_tensor("iao", [128, 16], F32, kind="ExternalOutput").ap()

    NQC = S // 512
    es = contextlib.ExitStack()
    with es:
        sb = lambda n, s, d: es.enter_context(nc.sbuf_tensor(n, s, d))
        psr = lambda n, s, d: es.enter_context(nc.psum_tensor(n, s, d))
        sem = lambda n: Sem(es.enter_context(nc.semaphore(n)))
        xch = [sb("xch0", [128, 8 * 512], BF16),
               sb("xch1", [128, 8 * 512], BF16)]
        ones1 = sb("ones1", [1, 128], BF16)
        ones1f = sb("ones1f", [1, 128], F32)
        alf = sb("alf", [16, S], BF16)
        rcols = sb("rcols", [128, 128], F32)
        qkbs = sb("qkbs", [128, 6], F32)
        vbf = sb("vbf", [1, 256], F32)
        vbb = sb("vbb", [1, 256], BF16)
        gb2t = sb("gb2t", [16, 1], F32)
        psA = [psr("psA0", [128, 512], F32), psr("psA1", [128, 512], F32)]
        psB = [psr("psB0", [128, 512], F32), psr("psB1", [128, 512], F32)]
        outA = psr("outA", [128, 512], F32)
        outB = psr("outB", [128, 512], F32)
        paux = psr("paux", [128, 512], F32)
        paux2 = psr("paux2", [128, 512], F32)
        lda = sem("s_lda")
        lx = [sem("s_lx0"), sem("s_lx1")]
        lgs = sem("s_lgs")
        sty = sem("s_sty")
        sia = sem("s_sia")
        Vp = sem("s_v")
        Sp = sem("s_s")
        Mm = sem("s_m")
        Pv = sem("s_p")
        ph = sem("s_ph")
        blk = es.enter_context(nc.Block())

        with contextlib.ExitStack() as es0:
            sb0 = lambda n, s, d: es0.enter_context(nc.sbuf_tensor(n, s, d))
            wstg = sb0("wstg", [128, 8192], F32)
            gw1b = sb0("gw1b", [128, 8 * 512], BF16)
            gw2b = sb0("gw2b", [128, 4 * 16], BF16)
            ipb = sb0("ipb", [128, 8 * 1024], BF16)
            gch = sb0("gch", [128, 4 * 512], BF16)
            alog = sb0("alog", [16, S], F32)
            iacc = sb0("iacc", [128, 16], F32)
            iact = sb0("iact", [128, 16], F32)
            gb1t = sb0("gb1t", [128, 4], F32)
            gdump = sb0("gdump", [128, 512], F32)
            P = Prog()
            force_table(P, 10)
            P.emit("sync", lambda e: e.dma_start(out=wstg[:, 0:4096],
                                                 in_=gw1c), lda, 16)
            P.emit("sync", lambda e: e.dma_start(out=wstg[:, 4096:4160],
                                                 in_=gw2c), lda, 16)
            P.emit("sync", lambda e: e.dma_start(out=gb1t[:], in_=gb1),
                   lda, 16)
            P.emit("sync", lambda e: e.dma_start(out=gb2t[:], in_=gb2),
                   lda, 16)
            P.emit("sync", lambda e: e.dma_start(out=rcols[:], in_=lam),
                   lda, 16)
            P.emit("sync", lambda e: e.dma_start(out=qkbs[:], in_=qkb),
                   lda, 16)
            P.emit("sync", lambda e: e.dma_start(out=vbf[:], in_=vbr),
                   lda, 16)
            lda_1 = lda.v
            P.emit("vector", lambda e: e.memset(ones1[:], 1.0))
            P.emit("vector", lambda e: e.memset(ones1f[:], 1.0))
            P.emit("vector", lambda e: e.memset(iacc[:], 0.0))
            P.wait("vector", lda, lda_1)
            P.emit("vector", lambda e: e.tensor_copy(
                out=gw1b[:], in_=wstg[:, 0:4096]))
            P.emit("vector", lambda e: e.tensor_copy(
                out=gw2b[:], in_=wstg[:, 4096:4160]))
            P.emit("vector", lambda e: e.tensor_copy(
                out=vbb[:], in_=vbf[:]), Vp)
            v_g1 = Vp.v
            for half in range(2):
                src = ipc[:, half * 4096:(half + 1) * 4096]
                P.wait("sync", Vp, v_g1 if half == 0 else Vp.v)
                P.emit("sync", lambda e, src=src: e.dma_start(
                    out=wstg[:, 0:4096], in_=src), lda, 16)
                ld_c = lda.v
                P.wait("vector", lda, ld_c)
                dst = ipb[:, half * 4096:(half + 1) * 4096]
                P.emit("vector", lambda e, dst=dst: e.tensor_copy(
                    out=dst, in_=wstg[:, 0:4096]), Vp)
            v_ip = Vp.v

            for tc in range(T // 512):
                par = tc & 1
                xb = xch[par]
                P.wait("sync", Mm)
                for ch in range(8):
                    dst = xb[:, ch * 512:(ch + 1) * 512]
                    src = xn[ch * 128:(ch + 1) * 128,
                             tc * 512:(tc + 1) * 512]
                    P.emit("sync", lambda e, dst=dst, src=src: e.dma_start(
                        out=dst, in_=src), lx[par], 16)
                lx_t = lx[par].v
                P.wait("tensor", lx[par], lx_t)
                P.wait("tensor", Vp, v_ip)
                half = 0 if tc < 16 else 1
                for pt in range(8):
                    P.wait("tensor", Sp)
                    for ch in range(8):
                        lhsT = ipb[:, ch * 1024 + pt * 128:
                                   ch * 1024 + pt * 128 + 128]
                        rhs = xb[:, ch * 512:(ch + 1) * 512]
                        P.emit("tensor", lambda e, lhsT=lhsT, rhs=rhs, ch=ch:
                               e.matmul(paux[:], lhsT, rhs, start=(ch == 0),
                                        stop=(ch == 7)),
                               Mm if ch == 7 else None)
                    m_ia = Mm.v
                    P.wait("scalar", Mm, m_ia)
                    P.wait("scalar", Vp)
                    acol = iact[:, pt * 2 + half:pt * 2 + half + 1]
                    P.emit("scalar", lambda e, acol=acol: e.activation(
                        out=gdump[:], in_=paux[:], func=AF.Gelu,
                        accum_out=acol), Sp)
                    s_ia = Sp.v
                    col = iacc[:, pt * 2 + half:pt * 2 + half + 1]
                    P.wait("vector", Sp, s_ia)
                    P.emit("vector", lambda e, col=col, acol=acol:
                           e.tensor_tensor(out=col, in0=col, in1=acol,
                                           op=ALU.add), Vp)
                if tc < 16:
                    for gt in range(4):
                        P.wait("tensor", Sp)
                        for ch in range(8):
                            lhsT = gw1b[:, ch * 512 + gt * 128:
                                        ch * 512 + gt * 128 + 128]
                            rhs = xb[:, ch * 512:(ch + 1) * 512]
                            P.emit("tensor",
                                   lambda e, lhsT=lhsT, rhs=rhs, ch=ch:
                                   e.matmul(paux2[:], lhsT, rhs,
                                            start=(ch == 0), stop=(ch == 7)),
                                   Mm if ch == 7 else None)
                        m_g = Mm.v
                        P.wait("scalar", Mm, m_g)
                        dstg = gch[:, gt * 512:(gt + 1) * 512]
                        bcol = gb1t[:, gt:gt + 1]
                        P.emit("scalar", lambda e, dstg=dstg, bcol=bcol:
                               e.activation(out=dstg, in_=paux2[:],
                                            func=AF.Gelu, bias=bcol), Sp)
                    s_g = Sp.v
                    P.wait("tensor", Sp, s_g)
                    for gt in range(4):
                        lhsT = gw2b[:, gt * 16:gt * 16 + 16]
                        rhs = gch[:, gt * 512:(gt + 1) * 512]
                        P.emit("tensor", lambda e, lhsT=lhsT, rhs=rhs, gt=gt:
                               e.matmul(paux[0:16, :], lhsT, rhs,
                                        start=(gt == 0), stop=(gt == 3)),
                               Mm if gt == 3 else None)
                    m_al = Mm.v
                    P.wait("scalar", Mm, m_al)
                    dsta = alog[0:16, tc * 512:(tc + 1) * 512]
                    P.emit("scalar", lambda e, dsta=dsta: e.activation(
                        out=dsta, in_=paux[0:16, :], func=AF.Identity,
                        bias=gb2t[:, 0:1]), Sp)
            v_iac = Vp.v
            P.wait("sync", Vp, v_iac)
            P.emit("sync", lambda e: e.dma_start(out=iao, in_=iacc[:]),
                   sia, 16)
            P.wait("scalar", Sp)
            P.wait("scalar", Vp)
            force_table(P, 2)
            P.emit("scalar", lambda e: e.activation(
                out=alf[0:16, 0:4096], in_=alog[0:16, 0:4096],
                func=AF.Sigmoid), Sp)
            P.emit("scalar", lambda e: e.activation(
                out=alf[0:16, 4096:8192], in_=alog[0:16, 4096:8192],
                func=AF.Sigmoid), Sp)
            P.emit("scalar", lambda e: e.activation(
                out=rcols[:], in_=rcols[:], func=AF.Sigmoid, scale=-1.0),
                Sp)
            s_sig = Sp.v
            for eng in ENGS:
                P.wait(eng, Sp, s_sig)
            P.play(blk)
            PB = Prog()
            for eng in ENGS:
                PB.emit(eng, lambda e: e.nop(), ph)
            bar1 = ph.v
            for eng in ENGS:
                PB.wait(eng, ph, bar1)
            PB.play(blk)

        with contextlib.ExitStack() as es1:
            sb1 = lambda n, s, d: es1.enter_context(nc.sbuf_tensor(n, s, d))
            qf = sb1("qf", [128, S], BF16)
            kf = sb1("kf", [128, S], BF16)
            vfm = sb1("vfm", [128, S], BF16)
            vA = sb1("vA", [128, 65 * 64], BF16)
            vB = sb1("vB", [128, 65 * 64], BF16)
            wqb = sb1("wqb", [128, 8 * 768], BF16)
            ysoft = sb1("ysoft", [128, S], BF16)
            eA = sb1("eA", [128, 2 * 512], BF16)
            eB = sb1("eB", [128, 2 * 512], BF16)
            sf32 = sb1("sf32", [128, 512], F32)
            rnrow = sb1("rnrow", [1, 1024], F32)
            rnb = sb1("rnb", [64, 512], F32)
            cmt = sb1("cmt", [128, 4 * 512], F32)
            gselb = sb1("gselb", [128, 4096], BF16)
            sel2b = sb1("sel2b", [128, 2048], BF16)
            kdup = sb1("kdup", [128, 512], BF16)
            qdup = sb1("qdup", [128, 512], BF16)
            wstf = sb1("wstf", [64, 256], F32)
            wstab = sb1("wstab", [64, 256], BF16)
            hself = sb1("hself", [16, 256], F32)
            hselb = sb1("hselb", [16, 256], BF16)
            wst = sb1("wst", [128, 6144], F32)
            ones5f = sb1("ones5f", [128, 512], F32)
            K2c = sb1("K2c", [128, 512], F32)
            Q2c = sb1("Q2c", [128, 512], F32)
            rbt = sb1("rbt", [128, 512], F32)
            uf = sb1("uf", [128, 512], F32)
            ut = sb1("ut", [128, 512], F32)
            sc = sb1("scn", [128, 512], F32)
            sqb = sb1("sqb", [128, 512], BF16)
            carr = sb1("carr", [128, 32], F32)
            alb = sb1("alb", [64, 512], F32)
            ychs = sb1("ychs", [64, 512], BF16)
            ysf0 = ut[0:64, :]
            dtt = uf[0:64, :]
            P = Prog()
            force_table(P, NLE_SET)

            P.emit("sync", lambda e: e.dma_start(out=wst[:], in_=wq),
                   lda, 16)
            P.emit("sync", lambda e: e.dma_start(out=cmt[:], in_=cmsk),
                   lda, 16)
            P.emit("sync", lambda e: e.dma_start(out=wstf[:], in_=wsta),
                   lda, 16)
            P.emit("sync", lambda e: e.dma_start(out=hself[:], in_=hsel),
                   lda, 16)
            lda_2 = lda.v
            P.emit("vector", lambda e: e.memset(ones5f[:], 1.0))
            va_ones = vA[:].rearrange("p (t c) -> p t c", c=65)[:, :, 64:65]
            vb_ones = vB[:].rearrange("p (t c) -> p t c", c=65)[:, :, 64:65]
            P.emit("vector", lambda e: e.memset(va_ones, 1.0))
            P.emit("vector", lambda e: e.memset(vb_ones, 1.0))
            P.wait("vector", lda, lda_2)
            P.emit("vector", lambda e: e.tensor_copy(out=wqb[:],
                                                     in_=wst[:]))
            P.emit("vector", lambda e: e.tensor_copy(out=wstab[:],
                                                     in_=wstf[:]))
            P.emit("vector", lambda e: e.tensor_copy(out=hselb[:],
                                                     in_=hself[:]), Vp)
            v_wqc = Vp.v
            P.wait("sync", Vp, v_wqc)
            P.emit("sync", lambda e: e.dma_start(out=wst[:, 0:2048],
                                                 in_=sel2), lda, 16)
            lda_s2 = lda.v
            P.wait("vector", lda, lda_s2)
            P.emit("vector", lambda e: e.tensor_copy(
                out=sel2b[:], in_=wst[:, 0:2048]), Vp)
            v_cv = Vp.v

            for pair in range(2):
                P.wait("tensor", Vp)
                P.wait("scalar", Vp)
                for tc in range(NQC):
                    par = tc & 1
                    xb = xch[par]
                    P.wait("sync", Mm)
                    for ch in range(8):
                        dst = xb[:, ch * 512:(ch + 1) * 512]
                        src = xn[ch * 128:(ch + 1) * 128,
                                 tc * 512:(tc + 1) * 512]
                        P.emit("sync", lambda e, dst=dst, src=src:
                               e.dma_start(out=dst, in_=src), lx[par], 16)
                    lx_t = lx[par].v
                    P.wait("tensor", lx[par], lx_t)
                    for wi, base in (("q", 0), ("k", 256), ("v", 512)):
                        P.wait("tensor", Sp)
                        for ch in range(8):
                            lhsT = wqb[:, ch * 768 + base + pair * 128:
                                       ch * 768 + base + pair * 128 + 128]
                            rhs = xb[:, ch * 512:(ch + 1) * 512]
                            P.emit("tensor",
                                   lambda e, lhsT=lhsT, rhs=rhs, ch=ch:
                                   e.matmul(paux[:], lhsT, rhs,
                                            start=(ch == 0), stop=(ch == 7)),
                                   Mm if ch == 7 else None)
                        m_qk = Mm.v
                        dstf = {"q": qf, "k": kf, "v": vfm}[wi][
                            :, tc * 512:(tc + 1) * 512]
                        bci = {"q": 0, "k": 2, "v": 4}[wi] + pair
                        bcol = qkbs[:, bci:bci + 1]
                        P.wait("scalar", Mm, m_qk)
                        P.emit("scalar", lambda e, dstf=dstf, bcol=bcol:
                               e.activation(out=dstf, in_=paux[:],
                                            func=AF.Identity, bias=bcol),
                               Sp)
                    for sub in range(4):
                        kt = tc * 4 + sub
                        P.wait("tensor", Vp)
                        for ch in range(8):
                            lhsT = xb[:, ch * 512 + sub * 128:
                                      ch * 512 + sub * 128 + 128]
                            rhs = wqb[:, ch * 768 + 512 + pair * 128:
                                      ch * 768 + 512 + pair * 128 + 128]
                            P.emit("tensor",
                                   lambda e, lhsT=lhsT, rhs=rhs, ch=ch:
                                   e.matmul(paux2[:, 0:128], lhsT, rhs,
                                            start=(ch == 0), stop=False))
                        vbsl = vbb[0:1, pair * 128:pair * 128 + 128]
                        P.emit("tensor", lambda e, vbsl=vbsl: e.matmul(
                            paux2[:, 0:128], ones1[0:1, :], vbsl,
                            start=False, stop=True), Mm)
                        m_v = Mm.v
                        dstA = vA[:, kt * 65:kt * 65 + 64]
                        dstB = vB[:, kt * 65:kt * 65 + 64]
                        P.wait("vector", Mm, m_v)
                        P.emit("vector", lambda e, dstA=dstA: e.tensor_copy(
                            out=dstA, in_=paux2[:, 0:64]))
                        P.emit("vector", lambda e, dstB=dstB: e.tensor_copy(
                            out=dstB, in_=paux2[:, 64:128]), Vp)

                P.wait("tensor", Vp)
                P.wait("vector", Sp)
                P.wait("scalar", Vp)
                for qc in range(NQC):
                    sp_qc0 = Sp.v
                    mm_qc0 = Mm.v
                    pv_qc0 = Pv.v
                    q0 = qc * 512
                    nktq = 4 * qc + 4
                    qslA = qf[0:64, q0:q0 + 512]
                    qslB = qf[64:128, q0:q0 + 512]
                    for kt in range(nktq):
                        par = kt & 1
                        P.wait("tensor", Sp,
                               sp_qc0 + (2 * (kt - 1) if kt >= 2 else 0))
                        kslA = kf[0:64, kt * 128:(kt + 1) * 128]
                        kslB = kf[64:128, kt * 128:(kt + 1) * 128]
                        pA, pB = psA[par], psB[par]
                        P.emit("tensor",
                               lambda e, kslA=kslA, qslA=qslA, pA=pA:
                               e.matmul(pA[:], kslA, qslA, start=True,
                                        stop=True, tile_position=(0, 0)))
                        P.emit("tensor",
                               lambda e, kslB=kslB, qslB=qslB, pB=pB:
                               e.matmul(pB[:], kslB, qslB, start=True,
                                        stop=True, tile_position=(64, 0)),
                               Mm)
                        if kt >= 1:
                            ktp = kt - 1
                            parp = ktp & 1
                            P.wait("tensor", Sp, sp_qc0 + 2 * (ktp + 1))
                            vslA = vA[:, ktp * 65:(ktp + 1) * 65]
                            vslB = vB[:, ktp * 65:(ktp + 1) * 65]
                            eslA = eA[:, parp * 512:parp * 512 + 512]
                            eslB = eB[:, parp * 512:parp * 512 + 512]
                            P.emit("tensor",
                                   lambda e, vslA=vslA, eslA=eslA, ktp=ktp:
                                   e.matmul(outA[0:65, :], vslA, eslA,
                                            start=(ktp == 0), stop=False))
                            P.emit("tensor",
                                   lambda e, vslB=vslB, eslB=eslB, ktp=ktp:
                                   e.matmul(outB[0:65, :], vslB, eslB,
                                            start=(ktp == 0), stop=False),
                                   Pv)
                        mi = kt - 4 * qc
                        P.wait("scalar", Mm, mm_qc0 + (kt + 1))
                        P.wait("scalar", Pv,
                               pv_qc0 + (kt - 1 if kt >= 2 else 0))
                        for hh in range(2):
                            pX = pA if hh == 0 else pB
                            eX = eA if hh == 0 else eB
                            edst = eX[:, par * 512:par * 512 + 512]
                            if mi >= 0:
                                msl = cmt[:, mi * 512:(mi + 1) * 512]
                                P.wait("vector", Mm, mm_qc0 + (kt + 1))
                                P.wait("vector", Sp)
                                P.emit("vector",
                                       lambda e, pX=pX, msl=msl:
                                       e.tensor_tensor(out=sf32[:],
                                                       in0=pX[:], in1=msl,
                                                       op=ALU.add), Vp)
                                P.wait("scalar", Vp)
                                P.emit("scalar", lambda e, edst=edst:
                                       e.activation(out=edst, in_=sf32[:],
                                                    func=AF.Exp,
                                                    scale=0.125), Sp)
                            else:
                                P.emit("scalar",
                                       lambda e, edst=edst, pX=pX:
                                       e.activation(out=edst, in_=pX[:],
                                                    func=AF.Exp,
                                                    scale=0.125), Sp)
                    ktp = nktq - 1
                    parp = ktp & 1
                    P.wait("tensor", Sp, sp_qc0 + 2 * (ktp + 1))
                    vslA = vA[:, ktp * 65:(ktp + 1) * 65]
                    vslB = vB[:, ktp * 65:(ktp + 1) * 65]
                    eslA = eA[:, parp * 512:parp * 512 + 512]
                    eslB = eB[:, parp * 512:parp * 512 + 512]
                    P.emit("tensor", lambda e, vslA=vslA, eslA=eslA, ktp=ktp:
                           e.matmul(outA[0:65, :], vslA, eslA,
                                    start=(ktp == 0), stop=True))
                    P.emit("tensor", lambda e, vslB=vslB, eslB=eslB, ktp=ktp:
                           e.matmul(outB[0:65, :], vslB, eslB,
                                    start=(ktp == 0), stop=True), Pv)
                    pv_done = Pv.v

                    P.wait("vector", Pv, pv_done)
                    P.emit("vector", lambda e: e.reciprocal(
                        out=rnrow[0:1, 0:512], in_=outA[64:65, :]), Vp)
                    P.wait("vector", Vp)
                    P.emit("vector", lambda e: e.reciprocal(
                        out=rnrow[0:1, 512:1024], in_=outB[64:65, :]), Vp)
                    P.wait("vector", Vp)
                    s_rn = Vp.v
                    for hh in range(2):
                        P.wait("tensor", Vp, s_rn if hh == 0 else Vp.v)
                        rsl = rnrow[0:1, hh * 512:hh * 512 + 512]
                        P.emit("tensor", lambda e, rsl=rsl: e.matmul(
                            paux[0:64, :], ones1f[0:1, 0:64], rsl,
                            start=True, stop=True), Mm)
                        m_bc = Mm.v
                        P.wait("vector", Mm, m_bc)
                        P.emit("vector", lambda e: e.tensor_copy(
                            out=rnb[:], in_=paux[0:64, :]), Vp)
                        P.wait("vector", Vp)
                        oX = outA if hh == 0 else outB
                        ysl = ysoft[hh * 64:hh * 64 + 64, q0:q0 + 512]
                        P.emit("vector", lambda e, oX=oX, ysl=ysl:
                               e.tensor_tensor(out=ysl, in0=oX[0:64, :],
                                               in1=rnb[:], op=ALU.mult), Vp)
                    v_y = Vp.v
                    P.wait("tensor", Vp, v_y)

                for hh in range(2):
                    hsl_ = pair * 2 + hh
                    P.wait("sync", Vp)
                    src = gsel[:, hsl_ * 4096:(hsl_ + 1) * 4096]
                    P.emit("sync", lambda e, src=src: e.dma_start(
                        out=wst[:, 0:4096], in_=src), lgs, 16)
                    lg_h = lgs.v
                    P.wait("vector", lgs, lg_h)
                    P.emit("vector", lambda e: e.tensor_copy(
                        out=gselb[:], in_=wst[:, 0:4096]), Vp)
                    v_gs = Vp.v
                    P.wait("tensor", Vp, v_gs)
                    P.wait("scalar", Vp, v_gs)

                    def V(fn):
                        P.emit("vector", fn, Vp)
                        P.wait("vector", Vp)

                    for ch in range(16):
                        c0 = ch * 512
                        ksl = kf[hh * 64:hh * 64 + 64, c0:c0 + 512]
                        qsl = qf[hh * 64:hh * 64 + 64, c0:c0 + 512]
                        P.wait("gpsimd", Vp)
                        P.wait("gpsimd", Sp)
                        P.wait("gpsimd", Mm)
                        P.emit("gpsimd", lambda e, ksl=ksl: e.dma_start(
                            out=kdup[0:64, :], in_=ksl), lgs, 16)
                        P.emit("gpsimd", lambda e, ksl=ksl: e.dma_start(
                            out=kdup[64:128, :], in_=ksl), lgs, 16)
                        P.emit("gpsimd", lambda e, qsl=qsl: e.dma_start(
                            out=qdup[0:64, :], in_=qsl), lgs, 16)
                        P.emit("gpsimd", lambda e, qsl=qsl: e.dma_start(
                            out=qdup[64:128, :], in_=qsl), lgs, 16)
                        lk_c = lgs.v
                        P.wait("scalar", lgs, lk_c)
                        P.wait("scalar", Vp)
                        P.emit("scalar", lambda e: e.activation(
                            out=K2c[:], in_=kdup[:], func=AF.Identity,
                            scale=0.125))
                        P.emit("scalar", lambda e: e.activation(
                            out=Q2c[:], in_=qdup[:], func=AF.Identity),
                            Sp)
                        s_kq = Sp.v
                        P.wait("tensor", Sp, s_kq)
                        hsl_b = hselb[:, hsl_ * 64:(hsl_ + 1) * 64]
                        asl = alf[0:16, c0:c0 + 512]
                        P.emit("tensor", lambda e, hsl_b=hsl_b, asl=asl:
                               e.matmul(paux2[0:64, :], hsl_b, asl,
                                        start=True, stop=True), Mm)
                        m_alb = Mm.v
                        P.wait("scalar", Mm, m_alb)
                        P.emit("scalar", lambda e: e.activation(
                            out=alb[:], in_=paux2[0:64, :],
                            func=AF.Identity), Sp)
                        s_alb = Sp.v
                        P.wait("vector", Sp, s_alb)

                        for t in range(32):
                            rcol = rcols[:, hsl_ * 32 + t:hsl_ * 32 + t + 1]
                            P.wait("scalar", Vp)
                            P.emit("scalar", lambda e, rcol=rcol:
                                   e.activation(out=rbt[:], in_=ones5f[:],
                                                func=AF.Identity,
                                                scale=rcol))
                            vsl = vfm[hh * 64:hh * 64 + 64, c0:c0 + 512]
                            gsl = gselb[hh * 64:hh * 64 + 64,
                                        t * 128:(t + 1) * 128]
                            P.wait("tensor", Sp)
                            P.wait("tensor", Vp)
                            P.emit("tensor", lambda e, gsl=gsl, vsl=vsl:
                                   e.matmul(paux[:], gsl, vsl, start=True,
                                            stop=True), Mm)
                            m_ve = Mm.v
                            P.wait("scalar", Mm, m_ve)
                            P.emit("scalar", lambda e: e.activation(
                                out=uf[:], in_=paux[:], func=AF.Identity),
                                Sp)
                            s_uf = Sp.v
                            P.wait("vector", Sp, s_uf)
                            V(lambda e: e.tensor_tensor(
                                out=ut[:], in0=uf[:], in1=K2c[:],
                                op=ALU.mult))
                            ini = 0.0 if ch == 0 else carr[:, t:t + 1]
                            V(lambda e, ini=ini: e.tensor_tensor_scan(
                                out=sc[:], data0=rbt[:], data1=ut[:],
                                initial=ini, op0=ALU.mult, op1=ALU.add))
                            V(lambda e: e.tensor_copy(
                                out=carr[:, t:t + 1], in_=sc[:, 511:512]))
                            V(lambda e: e.tensor_tensor(
                                out=sqb[:], in0=sc[:], in1=Q2c[:],
                                op=ALU.mult))
                            v_sq = Vp.v
                            P.wait("tensor", Vp, v_sq)
                            ssl = sel2b[:, t * 64:(t + 1) * 64]
                            P.emit("tensor", lambda e, ssl=ssl, t=t:
                                   e.matmul(outA[0:64, :], ssl, sqb[:],
                                            start=(t == 0), stop=False))
                        wsl = wstab[:, hsl_ * 64:(hsl_ + 1) * 64]
                        P.emit("tensor", lambda e, wsl=wsl:
                               e.matmul(outA[0:64, :], wsl, qdup[0:64, :],
                                        start=False, stop=True), Mm)
                        m_yst = Mm.v
                        yssl = ysoft[hh * 64:hh * 64 + 64, c0:c0 + 512]
                        P.wait("gpsimd", Sp)
                        P.emit("gpsimd", lambda e, yssl=yssl: e.dma_start(
                            out=kdup[0:64, :], in_=yssl), lgs, 16)
                        lk_y = lgs.v
                        P.wait("scalar", lgs, lk_y)
                        P.wait("scalar", Vp)
                        P.emit("scalar", lambda e: e.activation(
                            out=ysf0, in_=kdup[0:64, :],
                            func=AF.Identity), Sp)
                        s_ys = Sp.v
                        P.wait("vector", Sp, s_ys)
                        P.wait("vector", Mm, m_yst)
                        V(lambda e: e.tensor_tensor(
                            out=dtt, in0=ysf0, in1=outA[0:64, :],
                            op=ALU.subtract))
                        V(lambda e: e.tensor_tensor(
                            out=dtt, in0=dtt, in1=alb[:],
                            op=ALU.mult))
                        P.wait("vector", sty)
                        V(lambda e: e.tensor_tensor(
                            out=ychs[:], in0=dtt, in1=outA[0:64, :],
                            op=ALU.add))
                        v_yc = Vp.v
                        P.wait("scalar", Vp, v_yc)
                        dst = yo[hsl_ * 64:hsl_ * 64 + 64, c0:c0 + 512]
                        P.emit("scalar", lambda e, dst=dst: e.dma_start(
                            out=dst, in_=ychs[:]), sty, 16)
                        P.wait("tensor", Vp, v_yc)

            P.wait("sync", sty)
            P.wait("sync", sia)
            P.play(blk)
    return nc



def build_B2a():
    nc = bass.Bass("TRN2", target_bir_lowering=False, debug=False)
    x0 = nc.dram_tensor("x0", [TSL, 1024], F32, kind="ExternalInput").ap()
    yfm = nc.dram_tensor("yfm", [1024, TSL], BF16,
                         kind="ExternalInput").ap()
    wout = nc.dram_tensor("wout", [128, 8 * 1024], F32,
                          kind="ExternalInput").ap()
    bo = nc.dram_tensor("bo", [1, 1024], F32, kind="ExternalInput").ap()
    iaf = nc.dram_tensor("iaf", [2, 8192], F32, kind="ExternalInput").ap()
    piw = nc.dram_tensor("piw", [128, 8 * 4096], F32,
                         kind="ExternalInput").ap()
    x3p = nc.dram_tensor("x3p", [TSL, 1024], F32,
                         kind="ExternalOutput").ap()
    pactp = nc.dram_tensor("pactp", [128, 64], F32,
                           kind="ExternalOutput").ap()

    es = contextlib.ExitStack()
    with es:
        sb = lambda n, s, d: es.enter_context(nc.sbuf_tensor(n, s, d))
        psr = lambda n, s, d: es.enter_context(nc.psum_tensor(n, s, d))
        sem = lambda n: Sem(es.enter_context(nc.semaphore(n)))
        ones1 = sb("ones1", [1, 128], BF16)
        ones_t = sb("ones", [128, 128], BF16)
        idn = sb("idn", [128, 128], BF16)
        pj = [psr("pj0", [128, 512], F32), psr("pj1", [128, 512], F32)]
        ptr2 = psr("ptr2", [128, 2], F32)
        pp = psr("pp", [128, 2], F32)
        lda = sem("s_lda")
        lxt = sem("s_lxt")
        lws = sem("s_lws")
        sx1 = sem("s_sx1")
        sfin = sem("s_sfin")
        Vp = sem("s_v")
        Sp = sem("s_s")
        Tm = sem("s_t")
        G = sem("s_g")
        ph = sem("s_ph")
        blk = es.enter_context(nc.Block())

        with contextlib.ExitStack() as es1:
            sb1 = lambda n, s, d: es1.enter_context(nc.sbuf_tensor(n, s, d))
            xt = sb1("xt", [128, 1024], F32)
            acc = sb1("acc", [128, 1024], F32)
            ysl = sb1("ysl", [128, 8 * TSL], BF16)
            wof = sb1("wof", [128, 8 * 1024], F32)
            wob = sb1("wob", [128, 8 * 1024], BF16)
            bot = sb1("bot", [1, 1024], F32)
            bob = sb1("bob", [1, 1024], BF16)
            P = Prog()
            force_table(P, NLE_SET)
            P.emit("sync", lambda e: e.dma_start(out=wof[:], in_=wout),
                   lda, 16)
            P.emit("sync", lambda e: e.dma_start(out=bot[:], in_=bo),
                   lda, 16)
            for ch in range(8):
                dsty = ysl[:, ch * TSL:(ch + 1) * TSL]
                srcy = yfm[ch * 128:(ch + 1) * 128, :]
                P.emit("sync", lambda e, dsty=dsty, srcy=srcy: e.dma_start(
                    out=dsty, in_=srcy), lda, 16)
            lda_all = lda.v
            P.emit("vector", lambda e: e.memset(ones_t[:], 1.0))
            P.emit("vector", lambda e: e.memset(ones1[:], 1.0))
            P.wait("vector", lda, lda_all)
            P.emit("vector", lambda e: e.tensor_copy(out=wob[:],
                                                     in_=wof[:]))
            P.emit("vector", lambda e: e.tensor_copy(out=bob[:],
                                                     in_=bot[:]), Vp)
            v_w = Vp.v
            P.wait("gpsimd", Vp, v_w)
            P.emit("gpsimd", lambda e: e.affine_select(
                out=idn[:], in_=ones_t[:], pattern=[[-1, 128]],
                compare_op=ALU.is_equal, fill=0.0, base=0,
                channel_multiplier=1), G)
            g_idn = G.v
            P.wait("tensor", G, g_idn)
            P.wait("tensor", Vp, v_w)

            for i in range(NTT):
                P.wait("sync", Vp)
                src = x0[i * 128:(i + 1) * 128, :]
                P.emit("sync", lambda e, src=src: e.dma_start(
                    out=xt[:], in_=src), lxt, 16)
                lx_i = lxt.v
                P.wait("tensor", Vp)
                for half in range(2):
                    for ch in range(8):
                        lhsT = ysl[:, ch * TSL + i * 128:
                                   ch * TSL + (i + 1) * 128]
                        rhs = wob[:, ch * 1024 + half * 512:
                                  ch * 1024 + half * 512 + 512]
                        pjh = pj[half]
                        P.emit("tensor",
                               lambda e, lhsT=lhsT, rhs=rhs, ch=ch, pjh=pjh:
                               e.matmul(pjh[:], lhsT, rhs, start=(ch == 0),
                                        stop=False))
                    bsl = bob[0:1, half * 512:half * 512 + 512]
                    pjh = pj[half]
                    P.emit("tensor", lambda e, bsl=bsl, pjh=pjh: e.matmul(
                        pjh[:], ones1[0:1, :], bsl, start=False, stop=True),
                        Tm if half == 1 else None)
                m_pj = Tm.v
                P.wait("vector", lxt, lx_i)
                P.wait("vector", Tm, m_pj)
                P.wait("vector", sx1)
                P.emit("vector", lambda e: e.tensor_tensor(
                    out=acc[:, 0:512], in0=xt[:, 0:512], in1=pj[0][:],
                    op=ALU.add))
                P.emit("vector", lambda e: e.tensor_tensor(
                    out=acc[:, 512:1024], in0=xt[:, 512:1024], in1=pj[1][:],
                    op=ALU.add), Vp)
                v_acc = Vp.v
                P.wait("sync", Vp, v_acc)
                dst = x3p[i * 128:(i + 1) * 128, :]
                P.emit("sync", lambda e, dst=dst: e.dma_start(
                    out=dst, in_=acc[:]), sx1, 16)
            P.wait("sync", sx1)
            for eng in ENGS:
                P.wait(eng, sx1)
            P.play(blk)
            PB = Prog()
            for eng in ENGS:
                PB.emit(eng, lambda e: e.nop(), ph)
            bar1 = ph.v
            for eng in ENGS:
                PB.wait(eng, ph, bar1)
            PB.play(blk)

        with contextlib.ExitStack() as es2:
            sb2 = lambda n, s, d: es2.enter_context(nc.sbuf_tensor(n, s, d))
            iat = sb2("iat", [2, 8192], F32)
            cms = sb2("cms", [2, 8192], F32)
            irb = sb2("irb", [2, 8192], BF16)
            scr = sb2("scr", [2, 8], F32)
            irT = sb2("irT", [128, 16], BF16)
            pacts = sb2("pacts", [128, 64], F32)
            wst = sb2("wst", [128, 8192], F32)
            piwb = sb2("piwb", [128, 8 * 4096], BF16)
            P = Prog()
            P.emit("sync", lambda e: e.dma_start(out=iat[:], in_=iaf),
                   lda, 16)
            lda_ia = lda.v
            for rd in range(4):
                src = piw[:, rd * 8192:(rd + 1) * 8192]
                P.wait("sync", Vp)
                P.emit("sync", lambda e, src=src: e.dma_start(
                    out=wst[:], in_=src), lws, 16)
                lw_c = lws.v
                P.wait("vector", lws, lw_c)
                dst = piwb[:, rd * 8192:(rd + 1) * 8192]
                P.emit("vector", lambda e, dst=dst: e.tensor_copy(
                    out=dst, in_=wst[:]), Vp)
            v_piw = Vp.v

            def V(fn):
                P.emit("vector", fn, Vp)
                P.wait("vector", Vp)

            P.wait("vector", lda, lda_ia)
            V(lambda e: e.tensor_single_scalar(
                out=iat[:], in_=iat[:], scalar=1.0 / 8192, op=ALU.mult))
            bisect_mask(P, Vp, iat[:], scr, cms[:], 1024)
            V(lambda e: e.tensor_tensor(out=iat[:], in0=iat[:], in1=cms[:],
                                        op=ALU.mult))
            V(lambda e: e.tensor_copy(out=irb[:], in_=iat[:]))
            v_ir = Vp.v
            P.wait("tensor", Vp, v_ir)
            for chv in range(8):
                lhsT = irb[0:2, chv * 128:(chv + 1) * 128]
                P.emit("tensor", lambda e, lhsT=lhsT: e.matmul(
                    ptr2[:], lhsT, idn[0:2, 0:2], start=True, stop=True),
                    Tm)
                m_tr = Tm.v
                P.wait("vector", Tm, m_tr)
                dst = irT[:, chv * 2:(chv + 1) * 2]
                P.emit("vector", lambda e, dst=dst: e.tensor_copy(
                    out=dst, in_=ptr2[:]), Vp)
                P.wait("tensor", Vp)
            for pt in range(32):
                P.wait("tensor", Vp)
                for chv in range(8):
                    lhsT = piwb[:, chv * 4096 + pt * 128:
                                chv * 4096 + pt * 128 + 128]
                    rhs = irT[:, chv * 2:(chv + 1) * 2]
                    P.emit("tensor", lambda e, lhsT=lhsT, rhs=rhs, chv=chv:
                           e.matmul(pp[:], lhsT, rhs, start=(chv == 0),
                                    stop=(chv == 7)),
                           Tm if chv == 7 else None)
                m_pp = Tm.v
                P.wait("vector", Tm, m_pp)
                dst = pacts[:, pt * 2:(pt + 1) * 2]
                P.emit("vector", lambda e, dst=dst: e.tensor_copy(
                    out=dst, in_=pp[:]), Vp)
            v_pa = Vp.v
            P.wait("sync", Vp, v_pa)
            P.emit("sync", lambda e: e.dma_start(out=pactp, in_=pacts[:]),
                   sfin, 16)
            P.wait("sync", sfin)
            P.play(blk)
    return nc



def build_B2b():
    nc = bass.Bass("TRN2", target_bir_lowering=False, debug=False)
    pac8 = nc.dram_tensor("pac8", [2, 8 * 4096], F32,
                          kind="ExternalInput").ap()
    pv = nc.dram_tensor("pv", [128, 4 * 512], F32,
                        kind="ExternalInput").ap()
    aggp = nc.dram_tensor("aggp", [128, 8], F32, kind="ExternalOutput").ap()

    es = contextlib.ExitStack()
    with es:
        sb = lambda n, s, d: es.enter_context(nc.sbuf_tensor(n, s, d))
        psr = lambda n, s, d: es.enter_context(nc.psum_tensor(n, s, d))
        sem = lambda n: Sem(es.enter_context(nc.semaphore(n)))
        stg2 = sb("stg2", [2, 4096], F32)
        pacc = sb("pacc", [2, 4096], F32)
        pe = sb("pe", [2, 4096], F32)
        cms2 = sb("cms2", [2, 4096], F32)
        web = sb("web", [2, 4096], BF16)
        scr = sb("scr", [2, 8], F32)
        nmc = sb("nmc", [2, 4], F32)
        wtT = sb("wtT", [128, 8], BF16)
        aggs = sb("aggs", [128, 8], F32)
        pvf = sb("pvf", [128, 2048], F32)
        pvb = sb("pvb", [128, 2048], BF16)
        ones_t = sb("ones", [128, 128], BF16)
        idn = sb("idn", [128, 128], BF16)
        ptr2 = psr("ptr2", [128, 2], F32)
        pp = psr("pp", [128, 2], F32)
        lda = sem("s_lda")
        sfin = sem("s_sfin")
        Vp = sem("s_v")
        Sp = sem("s_s")
        Tm = sem("s_t")
        G = sem("s_g")
        blk = es.enter_context(nc.Block())
        P = Prog()
        force_table(P, 10)

        P.emit("sync", lambda e: e.dma_start(out=pvf[:], in_=pv), lda, 16)
        for j in range(8):
            src = pac8[0:2, j * 4096:(j + 1) * 4096]
            P.wait("sync", Vp)
            P.emit("sync", lambda e, src=src: e.dma_start(
                out=stg2[:], in_=src), lda, 16)
            ld_j = lda.v
            P.wait("vector", lda, ld_j)
            if j == 0:
                P.emit("vector", lambda e: e.tensor_copy(
                    out=pacc[:], in_=stg2[:]), Vp)
            else:
                P.emit("vector", lambda e: e.tensor_tensor(
                    out=pacc[:], in0=pacc[:], in1=stg2[:], op=ALU.add), Vp)
        v_sum = Vp.v
        P.emit("vector", lambda e: e.memset(ones_t[:], 1.0), Vp)
        v_one = Vp.v
        P.wait("gpsimd", Vp, v_one)
        P.emit("gpsimd", lambda e: e.affine_select(
            out=idn[:], in_=ones_t[:], pattern=[[-1, 128]],
            compare_op=ALU.is_equal, fill=0.0, base=0,
            channel_multiplier=1), G)
        g_idn = G.v
        P.emit("vector", lambda e: e.tensor_copy(out=pvb[:], in_=pvf[:]),
               Vp)
        P.wait("scalar", Vp, v_sum)
        P.emit("scalar", lambda e: e.activation(
            out=pacc[:], in_=pacc[:], func=AF.Gelu), Sp)
        s_gl = Sp.v
        P.wait("scalar", Sp, s_gl)
        force_table(P, NLE_SET)

        def V(fn):
            P.emit("vector", fn, Vp)
            P.wait("vector", Vp)

        P.wait("vector", Sp, s_gl)
        bisect_mask(P, Vp, pacc[:], scr, cms2[:], 512)
        V(lambda e: e.reduce_max(out=nmc[:, 0:1], in_=pacc[:], axis=AX.X,
                                 negate=True))
        P.wait("scalar", Vp)
        P.emit("scalar", lambda e: e.activation(
            out=pe[:], in_=pacc[:], func=AF.Exp, bias=nmc[:, 0:1]), Sp)
        s_e = Sp.v
        P.wait("vector", Sp, s_e)
        V(lambda e: e.tensor_tensor(out=pe[:], in0=pe[:], in1=cms2[:],
                                    op=ALU.mult))
        V(lambda e: e.reduce_sum(out=nmc[:, 1:2], in_=pe[:], axis=AX.X))
        V(lambda e: e.reciprocal(out=nmc[:, 2:3], in_=nmc[:, 1:2]))
        V(lambda e: e.tensor_single_scalar(
            out=pe[:], in_=pe[:], scalar=nmc[:, 2:3], op=ALU.mult))
        V(lambda e: e.tensor_copy(out=web[:], in_=pe[:]))
        v_wt = Vp.v
        P.wait("tensor", Vp, v_wt)
        P.wait("tensor", G, g_idn)
        for chv in range(4):
            lhsT = web[0:2, chv * 128:(chv + 1) * 128]
            P.emit("tensor", lambda e, lhsT=lhsT: e.matmul(
                ptr2[:], lhsT, idn[0:2, 0:2], start=True, stop=True), Tm)
            m_tr = Tm.v
            P.wait("vector", Tm, m_tr)
            dst = wtT[:, chv * 2:(chv + 1) * 2]
            P.emit("vector", lambda e, dst=dst: e.tensor_copy(
                out=dst, in_=ptr2[:]), Vp)
            P.wait("tensor", Vp)
        for cg in range(4):
            P.wait("tensor", Vp)
            for chv in range(4):
                lhsT = pvb[:, chv * 512 + cg * 128:
                           chv * 512 + cg * 128 + 128]
                rhs = wtT[:, chv * 2:(chv + 1) * 2]
                P.emit("tensor", lambda e, lhsT=lhsT, rhs=rhs, chv=chv:
                       e.matmul(pp[:], lhsT, rhs, start=(chv == 0),
                                stop=(chv == 3)), Tm if chv == 3 else None)
            m_pp = Tm.v
            P.wait("vector", Tm, m_pp)
            dst = aggs[:, cg * 2:(cg + 1) * 2]
            P.emit("vector", lambda e, dst=dst: e.tensor_copy(
                out=dst, in_=pp[:]), Vp)
        v_ag = Vp.v
        P.wait("sync", Vp, v_ag)
        P.emit("sync", lambda e: e.dma_start(out=aggp, in_=aggs[:]),
               sfin, 16)
        P.wait("sync", sfin)
        P.play(blk)
    return nc



def build_B2c():
    nc = bass.Bass("TRN2", target_bir_lowering=False, debug=False)
    agg8 = nc.dram_tensor("agg8", [128, 64], F32, kind="ExternalInput").ap()
    oiw = nc.dram_tensor("oiw", [128, 4 * 1024], F32,
                         kind="ExternalInput").ap()
    oactp = nc.dram_tensor("oactp", [128, 16], F32,
                           kind="ExternalOutput").ap()

    es = contextlib.ExitStack()
    with es:
        sb = lambda n, s, d: es.enter_context(nc.sbuf_tensor(n, s, d))
        psr = lambda n, s, d: es.enter_context(nc.psum_tensor(n, s, d))
        sem = lambda n: Sem(es.enter_context(nc.semaphore(n)))
        at8 = sb("at8", [128, 64], F32)
        aggT = sb("aggT", [128, 8], F32)
        aggb = sb("aggb", [128, 8], BF16)
        oiwf = sb("oiwf", [128, 4096], F32)
        oiwb = sb("oiwb", [128, 4096], BF16)
        oacts = sb("oacts", [128, 16], F32)
        pp = psr("pp", [128, 2], F32)
        lda = sem("s_lda")
        sfin = sem("s_sfin")
        Vp = sem("s_v")
        Sp = sem("s_s")
        Tm = sem("s_t")
        blk = es.enter_context(nc.Block())
        P = Prog()
        force_table(P, 10)
        P.emit("sync", lambda e: e.dma_start(out=at8[:], in_=agg8), lda, 16)
        P.emit("sync", lambda e: e.dma_start(out=oiwf[:], in_=oiw), lda, 16)
        lda_all = lda.v

        def V(fn):
            P.emit("vector", fn, Vp)
            P.wait("vector", Vp)

        P.wait("vector", lda, lda_all)
        V(lambda e: e.tensor_copy(out=aggT[:], in_=at8[:, 0:8]))
        for j in range(1, 8):
            sl = at8[:, j * 8:(j + 1) * 8]
            V(lambda e, sl=sl: e.tensor_tensor(out=aggT[:], in0=aggT[:],
                                               in1=sl, op=ALU.add))
        V(lambda e: e.tensor_copy(out=aggb[:], in_=aggT[:]))
        V(lambda e: e.tensor_copy(out=oiwb[:], in_=oiwf[:]))
        v_cv = Vp.v
        P.wait("tensor", Vp, v_cv)
        for ot in range(8):
            P.wait("tensor", Sp)
            for chv in range(4):
                lhsT = oiwb[:, chv * 1024 + ot * 128:
                            chv * 1024 + ot * 128 + 128]
                rhs = aggb[:, chv * 2:(chv + 1) * 2]
                P.emit("tensor", lambda e, lhsT=lhsT, rhs=rhs, chv=chv:
                       e.matmul(pp[:], lhsT, rhs, start=(chv == 0),
                                stop=(chv == 3)), Tm if chv == 3 else None)
            m_pp = Tm.v
            P.wait("scalar", Tm, m_pp)
            dst = oacts[:, ot * 2:(ot + 1) * 2]
            P.emit("scalar", lambda e, dst=dst: e.activation(
                out=dst, in_=pp[:], func=AF.Gelu), Sp)
        s_all = Sp.v
        P.wait("sync", Sp, s_all)
        P.emit("sync", lambda e: e.dma_start(out=oactp, in_=oacts[:]),
               sfin, 16)
        P.wait("sync", sfin)
        P.play(blk)
    return nc



def build_B2c2():
    nc = bass.Bass("TRN2", target_bir_lowering=False, debug=False)
    oact = nc.dram_tensor("oact", [2, 8192], F32, kind="ExternalInput").ap()
    opat = nc.dram_tensor("opat", [128, 8 * 1024], F32,
                          kind="ExternalInput").ap()
    outbp = nc.dram_tensor("outbp", [128, 16], F32,
                           kind="ExternalOutput").ap()

    es = contextlib.ExitStack()
    with es:
        sb = lambda n, s, d: es.enter_context(nc.sbuf_tensor(n, s, d))
        psr = lambda n, s, d: es.enter_context(nc.psum_tensor(n, s, d))
        sem = lambda n: Sem(es.enter_context(nc.semaphore(n)))
        oat = sb("oat", [2, 8192], F32)
        cms = sb("cms", [2, 8192], F32)
        scr = sb("scr", [2, 8], F32)
        mb = sb("mb", [2, 1024], BF16)
        mT = sb("mT", [128, 16], BF16)
        wst = sb("wst", [128, 8192], F32)
        opb = sb("opb", [128, 8192], BF16)
        obs = sb("obs", [128, 16], F32)
        ones_t = sb("ones", [128, 128], BF16)
        idn = sb("idn", [128, 128], BF16)
        ptr2 = psr("ptr2", [128, 2], F32)
        pp = psr("pp", [128, 2], F32)
        lda = sem("s_lda")
        sfin = sem("s_sfin")
        Vp = sem("s_v")
        Tm = sem("s_t")
        G = sem("s_g")
        blk = es.enter_context(nc.Block())
        P = Prog()
        force_table(P, NLE_SET)
        P.emit("sync", lambda e: e.dma_start(out=oat[:], in_=oact), lda, 16)
        P.emit("sync", lambda e: e.dma_start(out=wst[:], in_=opat), lda, 16)
        lda_all = lda.v
        P.emit("vector", lambda e: e.memset(ones_t[:], 1.0), Vp)
        v_one = Vp.v
        P.wait("gpsimd", Vp, v_one)
        P.emit("gpsimd", lambda e: e.affine_select(
            out=idn[:], in_=ones_t[:], pattern=[[-1, 128]],
            compare_op=ALU.is_equal, fill=0.0, base=0,
            channel_multiplier=1), G)
        g_idn = G.v

        def V(fn):
            P.emit("vector", fn, Vp)
            P.wait("vector", Vp)

        P.wait("vector", lda, lda_all)
        V(lambda e: e.tensor_copy(out=opb[:], in_=wst[:]))
        bisect_mask(P, Vp, oat[:], scr, cms[:], 1024)
        V(lambda e: e.tensor_tensor(out=oat[:], in0=oat[:], in1=cms[:],
                                    op=ALU.mult))
        V(lambda e: e.tensor_copy(out=mb[:], in_=oat[0:2, 0:1024]))
        v_mb = Vp.v
        P.wait("tensor", Vp, v_mb)
        P.wait("tensor", G, g_idn)
        for chv in range(8):
            lhsT = mb[0:2, chv * 128:(chv + 1) * 128]
            P.emit("tensor", lambda e, lhsT=lhsT: e.matmul(
                ptr2[:], lhsT, idn[0:2, 0:2], start=True, stop=True), Tm)
            m_tr = Tm.v
            P.wait("vector", Tm, m_tr)
            dst = mT[:, chv * 2:(chv + 1) * 2]
            P.emit("vector", lambda e, dst=dst: e.tensor_copy(
                out=dst, in_=ptr2[:]), Vp)
            P.wait("tensor", Vp)
        for dt in range(8):
            P.wait("tensor", Vp)
            for chv in range(8):
                lhsT = opb[:, chv * 1024 + dt * 128:
                           chv * 1024 + dt * 128 + 128]
                rhs = mT[:, chv * 2:(chv + 1) * 2]
                P.emit("tensor", lambda e, lhsT=lhsT, rhs=rhs, chv=chv:
                       e.matmul(pp[:], lhsT, rhs, start=(chv == 0),
                                stop=(chv == 7)), Tm if chv == 7 else None)
            m_pp = Tm.v
            P.wait("vector", Tm, m_pp)
            dst = obs[:, dt * 2:(dt + 1) * 2]
            P.emit("vector", lambda e, dst=dst: e.tensor_copy(
                out=dst, in_=pp[:]), Vp)
        v_ob = Vp.v
        P.wait("sync", Vp, v_ob)
        P.emit("sync", lambda e: e.dma_start(out=outbp, in_=obs[:]),
               sfin, 16)
        P.wait("sync", sfin)
        P.play(blk)
    return nc



def build_B2d():
    nc = bass.Bass("TRN2", target_bir_lowering=False, debug=False)
    x0 = nc.dram_tensor("x0", [TSL, 1024], F32, kind="ExternalInput").ap()
    obp = nc.dram_tensor("obp", [1, 8 * 1024], F32,
                         kind="ExternalInput").ap()
    rw = nc.dram_tensor("rw", [128, 8 * 14], F32, kind="ExternalInput").ap()
    x1o = nc.dram_tensor("x1o", [TSL, 1024], F32, kind="ExternalOutput").ap()
    xnfm = nc.dram_tensor("xnfm", [1024, TSL], BF16,
                          kind="ExternalOutput").ap()
    cmo = nc.dram_tensor("cmo", [128, NTT * 14], F32,
                         kind="ExternalOutput").ap()

    es = contextlib.ExitStack()
    with es:
        sb = lambda n, s, d: es.enter_context(nc.sbuf_tensor(n, s, d))
        psr = lambda n, s, d: es.enter_context(nc.psum_tensor(n, s, d))
        sem = lambda n: Sem(es.enter_context(nc.semaphore(n)))
        xt = sb("xt", [128, 1024], F32)
        acc = sb("acc", [128, 1024], F32)
        sq = sb("sq", [128, 1024], F32)
        xnb = sb("xnb", [128, 1024], BF16)
        xnfmt = sb("xnfmt", [128, 128], BF16)
        cols = sb("cols", [128, 8], F32)
        obt = sb("obt", [1, 8192], F32)
        mem = sb("mem", [1, 1024], F32)
        membc = sb("membc", [128, 1024], F32)
        rwf = sb("rwf", [128, 112], F32)
        rwb = sb("rwb", [128, 112], BF16)
        lgt = sb("lgt", [128, 14], F32)
        w8 = sb("w8", [128, 48], F32)
        cmb = sb("cmb", [128, NTT * 14], F32)
        ones_t = sb("ones", [128, 128], BF16)
        ones1f = sb("ones1f", [1, 128], F32)
        idn = sb("idn", [128, 128], BF16)
        ptr = psr("ptr", [128, 128], BF16)
        plg = psr("plg", [128, 14], F32)
        paux = psr("paux", [128, 512], F32)
        lda = sem("s_lda")
        lxt = sem("s_lxt")
        sx1 = sem("s_sx1")
        sxn = sem("s_sxn")
        sfin = sem("s_sfin")
        Vp = sem("s_v")
        Sp = sem("s_s")
        Tm = sem("s_t")
        G = sem("s_g")
        blk = es.enter_context(nc.Block())
        P = Prog()
        force_table(P, NLE_SET)
        P.emit("sync", lambda e: e.dma_start(out=obt[:], in_=obp), lda, 16)
        P.emit("sync", lambda e: e.dma_start(out=rwf[:], in_=rw), lda, 16)
        lda_all = lda.v
        P.emit("vector", lambda e: e.memset(ones_t[:], 1.0))
        P.emit("vector", lambda e: e.memset(ones1f[:], 1.0))
        P.wait("vector", lda, lda_all)
        P.emit("vector", lambda e: e.tensor_copy(out=rwb[:], in_=rwf[:]))

        def V(fn):
            P.emit("vector", fn, Vp)
            P.wait("vector", Vp)

        V(lambda e: e.tensor_copy(out=mem[:], in_=obt[0:1, 0:1024]))
        for j in range(1, 8):
            sl = obt[0:1, j * 1024:(j + 1) * 1024]
            V(lambda e, sl=sl: e.tensor_tensor(out=mem[:], in0=mem[:],
                                               in1=sl, op=ALU.add))
        v_mem = Vp.v
        P.wait("gpsimd", Vp, v_mem)
        P.emit("gpsimd", lambda e: e.affine_select(
            out=idn[:], in_=ones_t[:], pattern=[[-1, 128]],
            compare_op=ALU.is_equal, fill=0.0, base=0,
            channel_multiplier=1), G)
        g_idn = G.v
        P.wait("tensor", Vp, v_mem)
        for half in range(2):
            msl = mem[0:1, half * 512:half * 512 + 512]
            P.emit("tensor", lambda e, msl=msl: e.matmul(
                paux[:], ones1f[0:1, :], msl, start=True, stop=True), Tm)
            m_bc = Tm.v
            P.wait("vector", Tm, m_bc)
            dst = membc[:, half * 512:half * 512 + 512]
            P.emit("vector", lambda e, dst=dst: e.tensor_copy(
                out=dst, in_=paux[:]), Vp)
            P.wait("tensor", Vp)
        v_mbc = Vp.v
        P.wait("tensor", G, g_idn)

        for i in range(NTT):
            P.wait("sync", Vp)
            P.wait("sync", Sp)
            src = x0[i * 128:(i + 1) * 128, :]
            P.emit("sync", lambda e, src=src: e.dma_start(out=xt[:],
                                                          in_=src), lxt, 16)
            lx_i = lxt.v
            P.wait("vector", lxt, lx_i)
            P.wait("vector", Vp, v_mbc)
            P.wait("vector", sx1)
            P.wait("vector", Tm)
            P.emit("vector", lambda e: e.tensor_tensor(
                out=acc[:], in0=xt[:], in1=membc[:], op=ALU.add), Vp)
            v_acc = Vp.v
            P.wait("sync", Vp, v_acc)
            dst = x1o[i * 128:(i + 1) * 128, :]
            P.emit("sync", lambda e, dst=dst: e.dma_start(out=dst,
                                                          in_=acc[:]),
                   sx1, 16)
            P.wait("scalar", Vp, v_acc)
            P.wait("scalar", Tm)
            ln_tile(P, Vp, Sp, acc, xnb, sq, cols, want_rstd2=False)
            for k in range(8):
                P.wait("tensor", Sp)
                P.wait("tensor", Vp)
                src = xnb[:, 128 * k:128 * (k + 1)]
                P.emit("tensor", lambda e, src=src: e.transpose(
                    ptr[:], src, idn[:]), Tm)
                P.wait("vector", Tm)
                P.wait("vector", sxn)
                P.emit("vector", lambda e: e.tensor_copy(out=xnfmt[:],
                                                         in_=ptr[:]), Vp)
                v_ev = Vp.v
                P.wait("sync", Vp, v_ev)
                dst = xnfm[128 * k:128 * (k + 1), i * 128:(i + 1) * 128]
                P.emit("sync", lambda e, dst=dst: e.dma_start(
                    out=dst, in_=xnfmt[:]), sxn, 16)
                P.wait("tensor", Vp, v_ev)
                rw_k = rwb[:, 14 * k:14 * k + 14]
                P.emit("tensor", lambda e, rw_k=rw_k, k=k: e.matmul(
                    plg[:], xnfmt[:], rw_k, start=(k == 0), stop=(k == 7)),
                    Tm)
            P.wait("scalar", Tm)
            P.wait("scalar", Sp)
            P.emit("scalar", lambda e: e.activation(
                out=lgt[:], in_=plg[:], func=AF.Identity), Sp)
            P.wait("vector", Sp)
            V(lambda e: e.reduce_max(out=w8[:, 0:1], in_=lgt[:], axis=AX.X,
                                     negate=True))
            P.wait("scalar", Vp)
            P.emit("scalar", lambda e: e.activation(
                out=w8[:, 16:30], in_=lgt[:], func=AF.Exp,
                bias=w8[:, 0:1]), Sp)
            P.wait("vector", Sp)
            V(lambda e: e.tensor_single_scalar(
                out=w8[:, 32:46], in_=w8[:, 16:30], scalar=1.0,
                op=ALU.is_ge))
            V(lambda e: e.reduce_sum(out=w8[:, 1:2], in_=w8[:, 16:30],
                                     axis=AX.X))
            P.wait("scalar", Vp)
            recip_lnexp(P, Sp, w8[:, 2:3], w8[:, 1:2])
            P.wait("vector", Sp)
            dst = cmb[:, 14 * i:14 * i + 14]
            P.emit("vector", lambda e, dst=dst: e.tensor_single_scalar(
                out=dst, in_=w8[:, 32:46], scalar=w8[:, 2:3], op=ALU.mult),
                Vp)
        v_cmb = Vp.v
        P.wait("sync", Vp, v_cmb)
        P.emit("sync", lambda e: e.dma_start(out=cmo, in_=cmb[:]), sfin, 16)
        P.wait("sync", sfin)
        P.wait("sync", sxn)
        P.wait("sync", sx1)
        P.play(blk)
    return nc



def build_A3():
    nc = bass.Bass("TRN2", target_bir_lowering=False, debug=False)
    xnf = nc.dram_tensor("xnf", [1024, T], BF16, kind="ExternalInput").ap()
    xnd = nc.dram_tensor("xnd", [128, T], BF16, kind="ExternalInput").ap()
    xpw = nc.dram_tensor("xpw", [128, 8 * 96], F32,
                         kind="ExternalInput").ap()
    dtw = nc.dram_tensor("dtw", [64, 128], F32, kind="ExternalInput").ap()
    dtb = nc.dram_tensor("dtb", [128, 1], F32, kind="ExternalInput").ap()
    alg = nc.dram_tensor("alg", [128, 1], F32, kind="ExternalInput").ap()
    dvc = nc.dram_tensor("dvc", [128, 1], F32, kind="ExternalInput").ap()
    selg = nc.dram_tensor("selg", [128, 16 * 128], F32,
                          kind="ExternalInput").ap()
    selbc = nc.dram_tensor("selbc", [96, 256], F32,
                           kind="ExternalInput").ap()
    selr = nc.dram_tensor("selr", [128, 16 * 128], F32,
                          kind="ExternalInput").ap()
    mws = nc.dram_tensor("mws", [1, 14 * T], F32, kind="ExternalInput").ap()
    mw14 = nc.dram_tensor("mw14", [14, T], F32, kind="ExternalInput").ap()
    w1c = nc.dram_tensor("w1c", [128, 28672], F32,
                         kind="ExternalInput").ap()
    w2c = nc.dram_tensor("w2c", [128, 28672], F32,
                         kind="ExternalInput").ap()
    b1c = nc.dram_tensor("b1c", [128, 28], F32, kind="ExternalInput").ap()
    b2f = nc.dram_tensor("b2f", [14, 1024], F32, kind="ExternalInput").ap()
    yo3 = nc.dram_tensor("yo3", [128, T], F32, kind="ExternalOutput").ap()
    mo3 = nc.dram_tensor("mo3", [1024, T], BF16, kind="ExternalOutput").ap()

    es = contextlib.ExitStack()
    with es:
        sb = lambda n, s, d: es.enter_context(nc.sbuf_tensor(n, s, d))
        psr = lambda n, s, d: es.enter_context(nc.psum_tensor(n, s, d))
        sem = lambda n: Sem(es.enter_context(nc.semaphore(n)))
        xch = [sb("xch0", [128, 8 * 512], BF16),
               sb("xch1", [128, 8 * 512], BF16)]
        ones1 = sb("ones1", [1, 128], BF16)
        ccol = sb("ccol", [128, 8], F32)
        paux = psr("paux", [128, 512], F32)
        paux2 = psr("paux2", [128, 512], F32)
        pb1 = psr("pb1", [128, 512], F32)
        pb2 = psr("pb2", [128, 512], F32)
        pyy = psr("pyy", [128, 512], F32)
        lda = sem("s_lda")
        lx = [sem("s_lx0"), sem("s_lx1")]
        lud = sem("s_lud")
        lmw = sem("s_lmw")
        lws = sem("s_lws")
        sy = sem("s_sy")
        smo = sem("s_smo")
        Vp = sem("s_v")
        Sp = sem("s_s")
        Mm = sem("s_m")
        ph = sem("s_ph")
        blk = es.enter_context(nc.Block())

        with contextlib.ExitStack() as es1:
            sb1 = lambda n, s, d: es1.enter_context(nc.sbuf_tensor(n, s, d))
            xpwf = sb1("xpwf", [128, 768], F32)
            xpwb = sb1("xpwb", [128, 768], BF16)
            dtwf = sb1("dtwf", [64, 128], F32)
            selgf = sb1("selgf", [128, 2048], F32)
            sbcf = sb1("sbcf", [96, 256], F32)
            selrf = sb1("selrf", [128, 2048], F32)
            xpc = sb1("xpc", [96, 512], F32)
            delc = sb1("delc", [128, 512], F32)
            spt = sb1("spt", [128, 2 * 512], F32)
            Bx = sb1("Bx", [128, 512], F32)
            Cx = sb1("Cx", [128, 512], F32)
            udc = sb1("udc", [128, 512], BF16)
            ucf = sb1("ucf", [128, 512], F32)
            duc = sb1("duc", [128, 512], F32)
            dAc = sb1("dAc", [128, 512], F32)
            u3 = sb1("u3", [128, 512], F32)
            s3 = sb1("s3", [128, 512], F32)
            sq3 = sb1("sq3", [128, 512], F32)
            yct = sb1("yct", [128, 512], F32)
            carr3 = sb1("carr3", [128, 16], F32)
            P = Prog()
            force_table(P, NLE_SET)
            P.emit("sync", lambda e: e.dma_start(out=xpwf[:], in_=xpw),
                   lda, 16)
            P.emit("sync", lambda e: e.dma_start(out=dtwf[:], in_=dtw),
                   lda, 16)
            P.emit("sync", lambda e: e.dma_start(out=ccol[:, 0:1],
                                                 in_=dtb), lda, 16)
            P.emit("sync", lambda e: e.dma_start(out=ccol[:, 4:5],
                                                 in_=alg), lda, 16)
            P.emit("sync", lambda e: e.dma_start(out=ccol[:, 2:3],
                                                 in_=dvc), lda, 16)
            P.emit("sync", lambda e: e.dma_start(out=selgf[:], in_=selg),
                   lda, 16)
            P.emit("sync", lambda e: e.dma_start(out=sbcf[:], in_=selbc),
                   lda, 16)
            P.emit("sync", lambda e: e.dma_start(out=selrf[:], in_=selr),
                   lda, 16)
            lda_all = lda.v
            P.emit("vector", lambda e: e.memset(ones1[:], 1.0))
            P.wait("vector", lda, lda_all)
            P.emit("vector", lambda e: e.tensor_copy(out=xpwb[:],
                                                     in_=xpwf[:]))
            P.emit("vector", lambda e: e.memset(ccol[:, 3:4], 1.0), Vp)
            v_cv = Vp.v
            P.wait("scalar", Vp, v_cv)
            P.emit("scalar", lambda e: e.activation(
                out=ccol[:, 1:2], in_=ccol[:, 4:5], func=AF.Exp), Sp)
            P.wait("vector", Sp)
            P.emit("vector", lambda e: e.tensor_single_scalar(
                out=ccol[:, 1:2], in_=ccol[:, 1:2], scalar=-1.0,
                op=ALU.mult), Vp)
            v_an = Vp.v
            P.wait("tensor", Vp, v_an)
            P.wait("scalar", Vp, v_an)

            def V(fn):
                P.emit("vector", fn, Vp)
                P.wait("vector", Vp)

            for b in range(2):
                for ch in range(16):
                    c0 = b * S + ch * 512
                    par = ch & 1
                    xb = xch[par]
                    P.wait("sync", Mm)
                    for k in range(8):
                        dst = xb[:, k * 512:(k + 1) * 512]
                        src = xnf[k * 128:(k + 1) * 128, c0:c0 + 512]
                        P.emit("sync", lambda e, dst=dst, src=src:
                               e.dma_start(out=dst, in_=src), lx[par], 16)
                    lx_t = lx[par].v
                    src = xnd[:, c0:c0 + 512]
                    P.wait("sync", Vp)
                    P.emit("sync", lambda e, src=src: e.dma_start(
                        out=udc[:], in_=src), lud, 16)
                    lu_t = lud.v
                    P.wait("tensor", lx[par], lx_t)
                    P.wait("tensor", Sp)
                    for k in range(8):
                        lhsT = xpwb[:, k * 96:(k + 1) * 96]
                        rhs = xb[:, k * 512:(k + 1) * 512]
                        P.emit("tensor", lambda e, lhsT=lhsT, rhs=rhs, k=k:
                               e.matmul(paux[0:96, :], lhsT, rhs,
                                        start=(k == 0), stop=(k == 7)),
                               Mm if k == 7 else None)
                    m_xp = Mm.v
                    P.wait("scalar", Mm, m_xp)
                    P.emit("scalar", lambda e: e.activation(
                        out=xpc[:], in_=paux[0:96, :], func=AF.Identity),
                        Sp)
                    s_xp = Sp.v
                    P.wait("tensor", Sp, s_xp)
                    P.emit("tensor", lambda e: e.matmul(
                        paux2[:], dtwf[:], xpc[0:64, :], start=True,
                        stop=True))
                    P.emit("tensor", lambda e: e.matmul(
                        pb1[:], sbcf[0:96, 0:128], xpc[:], start=True,
                        stop=True))
                    P.emit("tensor", lambda e: e.matmul(
                        pb2[:], sbcf[0:96, 128:256], xpc[:], start=True,
                        stop=True), Mm)
                    m_dbc = Mm.v
                    P.wait("scalar", Mm, m_dbc)
                    P.emit("scalar", lambda e: e.activation(
                        out=spt[:, 0:512], in_=paux2[:], func=AF.Abs,
                        bias=ccol[:, 0:1]), Sp)
                    P.wait("scalar", Sp)
                    P.emit("scalar", lambda e: e.activation(
                        out=spt[:, 0:512], in_=spt[:, 0:512], func=AF.Exp,
                        scale=-1.0), Sp)
                    P.wait("scalar", Sp)
                    P.emit("scalar", lambda e: e.activation(
                        out=spt[:, 0:512], in_=spt[:, 0:512], func=AF.Ln,
                        bias=ccol[:, 3:4]), Sp)
                    P.wait("scalar", Sp)
                    P.emit("scalar", lambda e: e.activation(
                        out=spt[:, 512:1024], in_=paux2[:], func=AF.Relu,
                        bias=ccol[:, 0:1]), Sp)
                    P.emit("scalar", lambda e: e.activation(
                        out=Bx[:], in_=pb1[:], func=AF.Identity), Sp)
                    P.emit("scalar", lambda e: e.activation(
                        out=Cx[:], in_=pb2[:], func=AF.Identity), Sp)
                    s_sp = Sp.v
                    P.wait("vector", Sp, s_sp)
                    P.wait("vector", lud, lu_t)
                    V(lambda e: e.tensor_tensor(
                        out=delc[:], in0=spt[:, 0:512], in1=spt[:, 512:1024],
                        op=ALU.add))
                    V(lambda e: e.tensor_copy(out=ucf[:], in_=udc[:]))
                    V(lambda e: e.tensor_tensor(
                        out=duc[:], in0=delc[:], in1=ucf[:], op=ALU.mult))
                    v_du = Vp.v
                    P.wait("tensor", Vp, v_du)
                    for g in range(16):
                        gsl = selgf[:, g * 128:(g + 1) * 128]
                        P.wait("tensor", Sp)
                        P.emit("tensor", lambda e, gsl=gsl: e.matmul(
                            paux2[:], gsl, delc[:], start=True, stop=True))
                        P.emit("tensor", lambda e, gsl=gsl: e.matmul(
                            pb1[:], gsl, duc[:], start=True, stop=True), Mm)
                        m_ex = Mm.v
                        P.wait("scalar", Mm, m_ex)
                        P.wait("scalar", Vp)
                        P.emit("scalar", lambda e: e.activation(
                            out=dAc[:], in_=paux2[:], func=AF.Exp,
                            scale=ccol[:, 1:2]), Sp)
                        s_da = Sp.v
                        P.wait("vector", Sp, s_da)
                        V(lambda e: e.tensor_tensor(
                            out=u3[:], in0=pb1[:], in1=Bx[:], op=ALU.mult))
                        ini = 0.0 if ch == 0 else carr3[:, g:g + 1]
                        V(lambda e, ini=ini: e.tensor_tensor_scan(
                            out=s3[:], data0=dAc[:], data1=u3[:],
                            initial=ini, op0=ALU.mult, op1=ALU.add))
                        V(lambda e, g=g: e.tensor_copy(
                            out=carr3[:, g:g + 1], in_=s3[:, 511:512]))
                        V(lambda e: e.tensor_tensor(
                            out=sq3[:], in0=s3[:], in1=Cx[:], op=ALU.mult))
                        v_sq = Vp.v
                        P.wait("tensor", Vp, v_sq)
                        rsl = selrf[:, g * 128:(g + 1) * 128]
                        P.emit("tensor", lambda e, rsl=rsl, g=g: e.matmul(
                            pyy[:], rsl, sq3[:], start=(g == 0),
                            stop=(g == 15)), Mm if g == 15 else None)
                    m_y = Mm.v
                    P.wait("vector", Mm, m_y)
                    V(lambda e: e.tensor_single_scalar(
                        out=yct[:], in_=ucf[:], scalar=ccol[:, 2:3],
                        op=ALU.mult))
                    P.wait("vector", sy)
                    V(lambda e: e.tensor_tensor(
                        out=yct[:], in0=yct[:], in1=pyy[:], op=ALU.add))
                    v_yc = Vp.v
                    P.wait("sync", Vp, v_yc)
                    dst = yo3[:, c0:c0 + 512]
                    P.emit("sync", lambda e, dst=dst: e.dma_start(
                        out=dst, in_=yct[:]), sy, 16)
                    P.wait("tensor", Vp, v_yc)
            P.wait("sync", sy)
            for eng in ENGS:
                P.wait(eng, sy)
            P.play(blk)
            PB = Prog()
            for eng in ENGS:
                PB.emit(eng, lambda e: e.nop(), ph)
            bar1 = ph.v
            for eng in ENGS:
                PB.wait(eng, ph, bar1)
            PB.play(blk)

        with contextlib.ExitStack() as es2:
            sb2 = lambda n, s, d: es2.enter_context(nc.sbuf_tensor(n, s, d))
            wst3 = sb2("wst3", [128, 4096], F32)
            w1b = sb2("w1b", [128, 28672], BF16)
            w2b = sb2("w2b", [128, 28672], BF16)
            b1t = sb2("b1t", [128, 28], F32)
            b2t = sb2("b2t", [14, 1024], F32)
            b2b = sb2("b2b", [14, 1024], BF16)
            mwr = sb2("mwr", [1, 512], F32)
            mwrb = sb2("mwrb", [1, 512], BF16)
            m14f = sb2("m14f", [14, 512], F32)
            m14b = sb2("m14b", [14, 512], BF16)
            mbc = sb2("mbc", [128, 512], BF16)
            hid = sb2("hid", [128, 28 * 512], BF16)
            mot = sb2("mot", [128, 512], BF16)
            P = Prog()
            force_table(P, 10)
            P.emit("sync", lambda e: e.dma_start(out=b1t[:], in_=b1c),
                   lda, 16)
            P.emit("sync", lambda e: e.dma_start(out=b2t[:], in_=b2f),
                   lda, 16)
            lda_m = lda.v
            P.wait("vector", lda, lda_m)
            P.emit("vector", lambda e: e.tensor_single_scalar(
                out=b2b[:], in_=b2t[:], scalar=0.125, op=ALU.mult), Vp)
            for wi, (warr, wdst) in enumerate(((w1c, w1b), (w2c, w2b))):
                for rd in range(7):
                    src = warr[:, rd * 4096:(rd + 1) * 4096]
                    P.wait("sync", Vp)
                    P.emit("sync", lambda e, src=src: e.dma_start(
                        out=wst3[:], in_=src), lws, 16)
                    lw_c = lws.v
                    P.wait("vector", lws, lw_c)
                    dst = wdst[:, rd * 4096:(rd + 1) * 4096]
                    P.emit("vector", lambda e, dst=dst: e.tensor_copy(
                        out=dst, in_=wst3[:]), Vp)
            v_w = Vp.v
            P.wait("tensor", Vp, v_w)

            for tc in range(T // 512):
                par = tc & 1
                xb = xch[par]
                P.wait("sync", Mm)
                for k in range(8):
                    dst = xb[:, k * 512:(k + 1) * 512]
                    src = xnf[k * 128:(k + 1) * 128,
                              tc * 512:(tc + 1) * 512]
                    P.emit("sync", lambda e, dst=dst, src=src: e.dma_start(
                        out=dst, in_=src), lx[par], 16)
                lx_t = lx[par].v
                P.wait("sync", Vp)
                src = mw14[0:14, tc * 512:(tc + 1) * 512]
                P.emit("sync", lambda e, src=src: e.dma_start(
                    out=m14f[:], in_=src), lmw, 16)
                lm_t = lmw.v
                P.wait("vector", lmw, lm_t)
                P.wait("vector", Mm)
                P.emit("vector", lambda e: e.tensor_copy(out=m14b[:],
                                                         in_=m14f[:]), Vp)
                v_mw = Vp.v
                P.wait("tensor", lx[par], lx_t)
                for eh in range(28):
                    P.wait("tensor", Sp)
                    for k in range(8):
                        lhsT = w1b[:, (eh * 8 + k) * 128:
                                   (eh * 8 + k) * 128 + 128]
                        rhs = xb[:, k * 512:(k + 1) * 512]
                        P.emit("tensor", lambda e, lhsT=lhsT, rhs=rhs, k=k:
                               e.matmul(paux[:], lhsT, rhs, start=(k == 0),
                                        stop=(k == 7)),
                               Mm if k == 7 else None)
                    m_h = Mm.v
                    P.wait("scalar", Mm, m_h)
                    P.wait("scalar", Vp)
                    hsl = hid[:, eh * 512:(eh + 1) * 512]
                    bcol = b1t[:, eh:eh + 1]
                    P.emit("scalar", lambda e, hsl=hsl, bcol=bcol:
                           e.activation(out=hsl, in_=paux[:], func=AF.Gelu,
                                        bias=bcol), Sp)
                s_hid = Sp.v
                P.wait("tensor", Vp, v_mw)
                for ee in range(14):
                    src = mws[0:1, ee * T + tc * 512:ee * T + tc * 512 + 512]
                    P.wait("sync", Vp)
                    P.emit("sync", lambda e, src=src: e.dma_start(
                        out=mwr[:], in_=src), lmw, 16)
                    lm_e = lmw.v
                    P.wait("vector", lmw, lm_e)
                    P.emit("vector", lambda e: e.tensor_copy(
                        out=mwrb[:], in_=mwr[:]), Vp)
                    v_me = Vp.v
                    P.wait("tensor", Vp, v_me)
                    P.wait("tensor", Sp)
                    P.emit("tensor", lambda e: e.matmul(
                        paux2[:], ones1[0:1, :], mwrb[0:1, :], start=True,
                        stop=True), Mm)
                    m_bc = Mm.v
                    P.wait("scalar", Mm, m_bc)
                    P.wait("scalar", Sp, s_hid)
                    P.wait("scalar", Vp)
                    P.emit("scalar", lambda e: e.activation(
                        out=mbc[:], in_=paux2[:], func=AF.Identity), Sp)
                    s_bc = Sp.v
                    P.wait("vector", Sp, s_bc)
                    for ht in range(2):
                        hsl = hid[:, (ee * 2 + ht) * 512:
                                  (ee * 2 + ht) * 512 + 512]
                        P.emit("vector", lambda e, hsl=hsl: e.tensor_tensor(
                            out=hsl, in0=hsl, in1=mbc[:], op=ALU.mult),
                            Vp if ht == 1 else None)
                v_hm = Vp.v
                P.wait("tensor", Vp, v_hm)
                for dt in range(8):
                    for eh in range(28):
                        lhsT = w2b[:, (eh * 8 + dt) * 128:
                                   (eh * 8 + dt) * 128 + 128]
                        rhs = hid[:, eh * 512:(eh + 1) * 512]
                        P.emit("tensor", lambda e, lhsT=lhsT, rhs=rhs, eh=eh:
                               e.matmul(pyy[:], lhsT, rhs, start=(eh == 0),
                                        stop=False))
                    b2sl = b2b[0:14, dt * 128:(dt + 1) * 128]
                    P.emit("tensor", lambda e, b2sl=b2sl: e.matmul(
                        pyy[:], b2sl, m14b[:], start=False, stop=True), Mm)
                    m_dn = Mm.v
                    P.wait("scalar", Mm, m_dn)
                    P.wait("scalar", smo)
                    P.emit("scalar", lambda e: e.activation(
                        out=mot[:], in_=pyy[:], func=AF.Identity), Sp)
                    s_mo = Sp.v
                    P.wait("sync", Sp, s_mo)
                    dst = mo3[dt * 128:(dt + 1) * 128,
                              tc * 512:(tc + 1) * 512]
                    P.emit("sync", lambda e, dst=dst: e.dma_start(
                        out=dst, in_=mot[:]), smo, 16)
                    P.wait("tensor", Sp, s_mo)
            P.wait("sync", smo)
            P.play(blk)
    return nc



_PROGS = {}


def _get(name, builder):
    if name not in _PROGS:
        _PROGS[name] = builder()
    return _PROGS[name]


def run_L0(inputs):
    x0 = np.ascontiguousarray(
        np.asarray(inputs["L0_attn_hidden_states"], np.float32)
        .reshape(T, 1024))
    nw = np.asarray(inputs["L0_ffn_norm_weight"], np.float32)
    rw = np.asarray(inputs["L0_ffn_router_weight"], np.float32)
    Wqkv = np.asarray(inputs["L0_attn_W_qkv"], np.float32)
    Wout = np.asarray(inputs["L0_attn_W_out"], np.float32)
    rel = np.asarray(inputs["L0_attn_rel_emb"], np.float32)
    gw = np.asarray(inputs["L0_ffn_gate_up_weight"], np.float32)
    dw = np.asarray(inputs["L0_ffn_down_weight"], np.float32)
    al = np.ascontiguousarray(
        np.asarray(inputs["L0_ffn_residual_alpha"],
                   np.float32).reshape(1, 1))

    nw8 = np.ascontiguousarray(nw.reshape(8, 128).T)
    rw64 = chunk128(np.ascontiguousarray(rw.T))

    pre0 = _get("pre0", build_pre0)
    im = [{"x": x0[c * TSL:(c + 1) * TSL], "nw8": nw8, "rw64": rw64}
          for c in range(NCORE)]
    r = run_bass_kernel_spmd(pre0, im, core_ids=list(range(NCORE)))
    xn0 = np.concatenate([r.results[c]["xnfm"] for c in range(NCORE)],
                         axis=1)
    r2 = np.concatenate([r.results[c]["r2o"].T.reshape(-1)
                         for c in range(NCORE)])
    comb = np.concatenate(
        [r.results[c]["cmo"].reshape(128, NTT, 8).transpose(1, 0, 2)
         .reshape(TSL, 8) for c in range(NCORE)], axis=0)

    relohT = make_rel_ohT()
    A0 = _get("A0", build_A0)

    def bperm(a, c, axis):
        if c // 4 == 0:
            return np.ascontiguousarray(a)
        return np.ascontiguousarray(np.concatenate(
            (np.take(a, range(S, T), axis=axis),
             np.take(a, range(0, S), axis=axis)), axis=axis))

    im = []
    for c in range(NCORE):
        hs = [4 * (c % 4) + j for j in range(4)]
        colsel = ([0 * 1024 + h * 64 + d for h in hs for d in range(64)]
                  + [1 * 1024 + h * 64 + d for h in hs for d in range(64)]
                  + [2 * 1024 + h * 64 + d for h in hs for d in range(64)])
        im.append({
            "xn": bperm(xn0, c, 1),
            "wq": chunk128(Wqkv[:, colsel]),
            "rel4": np.ascontiguousarray(rel[:, hs]),
            "relohT": relohT,
            "gup": chunk128(gw[c].T),
            "dwn": chunk128(np.ascontiguousarray(dw[c].T)),
            "combr": bperm(comb[:, c].reshape(1, T), c, 1),
            "alpha": al,
            "r2r": bperm(r2.reshape(1, T), c, 1),
            "nw8": nw8,
        })
    rA = run_bass_kernel_spmd(A0, im, core_ids=list(range(NCORE)))
    yfull = np.zeros((1024, T), BF)
    for c in range(NCORE):
        b = c // 4
        h0 = 4 * (c % 4)
        yfull[h0 * 64:(h0 + 4) * 64, b * S:(b + 1) * S] = \
            rA.results[c]["yo"]
    mps = [bperm(rA.results[c]["mpo"], c, 1) for c in range(NCORE)]

    B0 = _get("B0", build_B0)
    wout_arr = chunk128(Wout)
    im = []
    for c in range(NCORE):
        sl = slice(c * TSL, (c + 1) * TSL)
        mp_arr = np.concatenate(
            [np.ascontiguousarray(m.T[sl]) for m in mps], axis=0)
        im.append({
            "x0": x0[sl],
            "yfm": np.ascontiguousarray(yfull[:, sl]),
            "wout": wout_arr,
            "mp": mp_arr,
        })
    rB = run_bass_kernel_spmd(B0, im, core_ids=list(range(NCORE)))
    x1 = np.concatenate([rB.results[c]["x1o"] for c in range(NCORE)],
                        axis=0)
    xn1 = np.concatenate([rB.results[c]["xnfm"] for c in range(NCORE)],
                         axis=1)
    return x1, xn1, {"xn0": xn0, "r2": r2, "comb": comb, "y": yfull,
                     "mps": mps}


def run_L1(inputs, x1, xn1):
    W = np.asarray(inputs["L1_attn_W"], np.float32)
    bb = np.asarray(inputs["L1_attn_b"], np.float32)
    wg = np.asarray(inputs["L1_ffn_w_gate"])
    wu = np.asarray(inputs["L1_ffn_w_up"])
    wd = np.asarray(inputs["L1_ffn_w_down"])

    A1 = _get("A1", build_A1)

    def bperm(a, c, axis):
        if c // 4 == 0:
            return np.ascontiguousarray(a)
        return np.ascontiguousarray(np.concatenate(
            (np.take(a, range(S, T), axis=axis),
             np.take(a, range(0, S), axis=axis)), axis=axis))

    im = []
    for c in range(NCORE):
        hs = [4 * (c % 4) + j for j in range(4)]
        qcols = [h * 192 + j for h in hs for j in range(64)]
        kcols = [h * 192 + 64 + j for h in hs for j in range(64)]
        vcols = [h * 192 + 128 + j for h in hs for j in range(64)]
        colsel = qcols + kcols + vcols
        qkb = np.stack([bb[qcols[:128]], bb[qcols[128:]],
                        bb[kcols[:128]], bb[kcols[128:]]], axis=1)
        vbr = bb[vcols].reshape(1, 256)
        im.append({
            "xn": bperm(xn1, c, 1),
            "xnf": np.ascontiguousarray(xn1[:, c * TSL:(c + 1) * TSL]),
            "wq": chunk128(W[:, colsel]),
            "qkb": np.ascontiguousarray(qkb),
            "vbr": np.ascontiguousarray(vbr),
            "wg8": chunk128(wg.T).astype(np.int8),
            "wu8": chunk128(wu.T).astype(np.int8),
            "wd8": chunk128(np.ascontiguousarray(wd.T)).astype(np.int8),
        })
    rA = run_bass_kernel_spmd(A1, im, core_ids=list(range(NCORE)))
    yfull = np.zeros((1024, T), np.float32)
    for c in range(NCORE):
        b = c // 4
        h0 = 4 * (c % 4)
        yfull[h0 * 64:(h0 + 4) * 64, b * S:(b + 1) * S] = rA.results[c]["yo"]
    oos = [rA.results[c]["oo"] for c in range(NCORE)]

    B1 = _get("B1", lambda: build_B_add(2, 0))
    im = []
    for c in range(NCORE):
        sl = slice(c * TSL, (c + 1) * TSL)
        tf = np.concatenate([np.ascontiguousarray(yfull[:, sl].T),
                             np.ascontiguousarray(oos[c].T)], axis=0)
        im.append({"x0": x1[sl], "tf": tf})
    rB = run_bass_kernel_spmd(B1, im, core_ids=list(range(NCORE)))
    x2 = np.concatenate([rB.results[c]["x1o"] for c in range(NCORE)], axis=0)
    xn2 = np.concatenate([rB.results[c]["xnfm"] for c in range(NCORE)],
                         axis=1)
    return x2, xn2, {"y1": yfull, "oos": oos}


def _bperm(a, c, axis):
    if c // 4 == 0:
        return np.ascontiguousarray(a)
    return np.ascontiguousarray(np.concatenate(
        (np.take(a, range(S, T), axis=axis),
         np.take(a, range(0, S), axis=axis)), axis=axis))


def run_L2(inputs, x2, xn2):
    Wq = np.asarray(inputs["L2_attn_Wqkv_w"], np.float32)
    Wb = np.asarray(inputs["L2_attn_Wqkv_b"], np.float32)
    ow = np.asarray(inputs["L2_attn_out_w"], np.float32)
    ob = np.asarray(inputs["L2_attn_out_b"], np.float32)
    Wst = np.asarray(inputs["L2_attn_W_static"], np.float32)
    Lam = np.asarray(inputs["L2_attn_Lambda_raw"], np.float32)
    Gam = np.asarray(inputs["L2_attn_Gamma"], np.float32)
    gw1 = np.asarray(inputs["L2_attn_gate_w1"], np.float32)
    gb1 = np.asarray(inputs["L2_attn_gate_b1"], np.float32)
    gw2 = np.asarray(inputs["L2_attn_gate_w2"], np.float32)
    gb2 = np.asarray(inputs["L2_attn_gate_b2"], np.float32)
    ipat = np.asarray(inputs["L2_ffn_input_patterns"], np.float32)
    piw = np.asarray(inputs["L2_ffn_process_input_weights"], np.float32)
    pv = np.asarray(inputs["L2_ffn_process_values"], np.float32)
    oiw = np.asarray(inputs["L2_ffn_output_input_weights"], np.float32)
    opat = np.asarray(inputs["L2_ffn_output_patterns"], np.float32)
    rw3 = np.asarray(inputs["L3_ffn_router_w"], np.float32)

    sel2 = np.zeros((128, 32 * 64), np.float32)
    for t in range(32):
        for p in range(128):
            sel2[p, t * 64 + 2 * t + p // 64] = 1.0
    cmsk = np.zeros((128, 4 * 512), np.float32)
    for mi in range(4):
        for p in range(128):
            f0 = mi * 128 + p
            if f0 + 1 <= 512:
                cmsk[p, mi * 512:mi * 512 + f0] = 0.0
            cmsk[p, mi * 512 + min(f0 + 1, 512) - 1:(mi + 1) * 512] = 0.0
    cmsk[:] = 0.0
    for mi in range(4):
        for p in range(128):
            kpos = mi * 128 + p
            if kpos > 0:
                cmsk[p, mi * 512:mi * 512 + min(kpos, 512)] = -80000.0

    A2 = _get("A2", build_A2)
    im = []
    for c in range(NCORE):
        hs = [4 * (c % 4) + j for j in range(4)]
        qcols = [h * 64 + d for h in hs for d in range(64)]
        kcols = [1024 + h * 64 + d for h in hs for d in range(64)]
        vcols = [2048 + h * 64 + d for h in hs for d in range(64)]
        colsel = qcols + kcols + vcols
        qkb = np.stack([Wb[qcols[:128]], Wb[qcols[128:]],
                        Wb[kcols[:128]], Wb[kcols[128:]],
                        Wb[vcols[:128]], Wb[vcols[128:]]], axis=1)
        gsel = np.zeros((128, 4 * 4096), np.float32)
        for j, h in enumerate(hs):
            r0 = (j % 2) * 64
            for t in range(32):
                for dd in range(2):
                    gsel[r0 + 2 * t + dd,
                         j * 4096 + t * 128 + dd * 64:
                         j * 4096 + t * 128 + dd * 64 + 64] = Gam[h, 2 * t + dd]
        lamc = np.concatenate(
            [np.ascontiguousarray(Lam[h].reshape(32, 128).T) for h in hs],
            axis=1)
        wstac = np.concatenate([Wst[h].T for h in hs], axis=1)
        hselc = np.zeros((16, 4 * 64), np.float32)
        for j, h in enumerate(hs):
            hselc[h, j * 64:(j + 1) * 64] = 1.0
        im.append({
            "xn": _bperm(xn2, c, 1),
            "wq": chunk128(Wq[:, colsel]),
            "qkb": np.ascontiguousarray(qkb),
            "vbr": np.ascontiguousarray(Wb[vcols].reshape(1, 256)),
            "gw1c": chunk128(gw1),
            "gw2c": chunk128(gw2),
            "gb1": np.ascontiguousarray(gb1.reshape(4, 128).T),
            "gb2": np.ascontiguousarray(gb2.reshape(16, 1)),
            "lam": np.ascontiguousarray(lamc),
            "gsel": np.ascontiguousarray(gsel),
            "sel2": sel2,
            "wsta": np.ascontiguousarray(wstac),
            "hsel": hselc,
            "cmsk": cmsk,
            "ipc": chunk128(
                np.ascontiguousarray(ipat[c * 1024:(c + 1) * 1024].T)),
        })
    rA = run_bass_kernel_spmd(A2, im, core_ids=list(range(NCORE)))
    y2full = np.zeros((1024, T), BF)
    ia_sums = np.zeros((2, 8192), np.float32)
    for c in range(NCORE):
        b = c // 4
        h0 = 4 * (c % 4)
        y2full[h0 * 64:(h0 + 4) * 64, b * S:(b + 1) * S] = \
            rA.results[c]["yo"]
        iao = rA.results[c]["iao"]
        for pt in range(8):
            for half in range(2):
                bglob = b if half == 0 else 1 - b
                ia_sums[bglob, c * 1024 + pt * 128:
                        c * 1024 + pt * 128 + 128] = iao[:, pt * 2 + half]

    B2a = _get("B2a", build_B2a)
    wout_arr = chunk128(ow)
    bo_arr = np.ascontiguousarray(ob.reshape(1, 1024))
    im = []
    for c in range(NCORE):
        sl = slice(c * TSL, (c + 1) * TSL)
        pidx = (list(range(c * 1024, (c + 1) * 1024))
                + [j for j in range(8192) if not
                   (c * 1024 <= j < (c + 1) * 1024)])
        piw_sl = piw[:, c * 1024:(c + 1) * 1024]
        im.append({
            "x0": x2[sl],
            "yfm": np.ascontiguousarray(y2full[:, sl]),
            "wout": wout_arr,
            "bo": bo_arr,
            "iaf": np.ascontiguousarray(ia_sums[:, pidx]),
            "piw": chunk128(np.ascontiguousarray(piw_sl.T)),
        })
    rBa = run_bass_kernel_spmd(B2a, im, core_ids=list(range(NCORE)))
    x3p = [rBa.results[c]["x3p"] for c in range(NCORE)]
    pacs = []
    for c in range(NCORE):
        pp_ = rBa.results[c]["pactp"]
        pa = np.zeros((2, 4096), np.float32)
        for pt in range(32):
            for b in range(2):
                pa[b, pt * 128:(pt + 1) * 128] = pp_[:, pt * 2 + b]
        pacs.append(pa)

    B2b = _get("B2b", build_B2b)
    im = []
    for c in range(NCORE):
        perm2 = (list(range(c * 512, (c + 1) * 512))
                 + [j for j in range(4096) if not
                    (c * 512 <= j < (c + 1) * 512)])
        pac8 = np.concatenate([p[:, perm2] for p in pacs], axis=1)
        im.append({
            "pac8": np.ascontiguousarray(pac8),
            "pv": chunk128(np.ascontiguousarray(
                pv[c * 512:(c + 1) * 512])),
        })
    rBb = run_bass_kernel_spmd(B2b, im, core_ids=list(range(NCORE)))
    agg8 = np.concatenate([rBb.results[c]["aggp"] for c in range(NCORE)],
                          axis=1)

    B2c = _get("B2c", build_B2c)
    im = []
    for c in range(NCORE):
        oiw_sl = oiw[c * 1024:(c + 1) * 1024]
        im.append({
            "agg8": np.ascontiguousarray(agg8),
            "oiw": chunk128(np.ascontiguousarray(oiw_sl.T)),
        })
    rBc = run_bass_kernel_spmd(B2c, im, core_ids=list(range(NCORE)))
    oacts = np.zeros((2, 8192), np.float32)
    for c in range(NCORE):
        oa = rBc.results[c]["oactp"]
        for ot in range(8):
            for b in range(2):
                oacts[b, c * 1024 + ot * 128:
                      c * 1024 + ot * 128 + 128] = oa[:, ot * 2 + b]

    B2c2 = _get("B2c2", build_B2c2)
    im = []
    for c in range(NCORE):
        perm3 = (list(range(c * 1024, (c + 1) * 1024))
                 + [j for j in range(8192) if not
                    (c * 1024 <= j < (c + 1) * 1024)])
        im.append({
            "oact": np.ascontiguousarray(oacts[:, perm3]),
            "opat": chunk128(opat[c * 1024:(c + 1) * 1024]),
        })
    rBc2 = run_bass_kernel_spmd(B2c2, im, core_ids=list(range(NCORE)))
    outbs = []
    for c in range(NCORE):
        obs_ = rBc2.results[c]["outbp"]
        obv = np.zeros((2, 1024), np.float32)
        for dt in range(8):
            for b in range(2):
                obv[b, dt * 128:(dt + 1) * 128] = obs_[:, dt * 2 + b]
        outbs.append(obv)

    B2d = _get("B2d", build_B2d)
    rw_arr = chunk128(rw3)
    im = []
    for c in range(NCORE):
        b = c // 4
        obp = np.concatenate([outbs[j][b] for j in range(NCORE)])
        im.append({
            "x0": x3p[c],
            "obp": np.ascontiguousarray(obp.reshape(1, 8 * 1024)),
            "rw": rw_arr,
        })
    rBd = run_bass_kernel_spmd(B2d, im, core_ids=list(range(NCORE)))
    x3 = np.concatenate([rBd.results[c]["x1o"] for c in range(NCORE)],
                        axis=0)
    xn3 = np.concatenate([rBd.results[c]["xnfm"] for c in range(NCORE)],
                         axis=1)
    maskw = np.concatenate(
        [rBd.results[c]["cmo"].reshape(128, NTT, 14).transpose(1, 0, 2)
         .reshape(TSL, 14) for c in range(NCORE)], axis=0)
    return x3, xn3, maskw, {"y2": y2full, "ia": ia_sums, "pacs": pacs,
                            "agg8": agg8, "oacts": oacts, "outbs": outbs}


def run_L3(inputs, x3, xn3, maskw):
    A_log = np.asarray(inputs["L3_attn_A_log"], np.float32)
    xpw_w = np.asarray(inputs["L3_attn_x_proj_w"], np.float32)
    dtw_w = np.asarray(inputs["L3_attn_dt_proj_w"], np.float32)
    dtb_w = np.asarray(inputs["L3_attn_dt_proj_b"], np.float32)
    Dv = np.asarray(inputs["L3_attn_D"], np.float32)
    w1 = np.asarray(inputs["L3_ffn_w1"], np.float32)
    b1 = np.asarray(inputs["L3_ffn_b1"], np.float32)
    w2 = np.asarray(inputs["L3_ffn_w2"], np.float32)
    b2 = np.asarray(inputs["L3_ffn_b2"], np.float32)

    selg = np.zeros((128, 16 * 128), np.float32)
    selr = np.zeros((128, 16 * 128), np.float32)
    for g in range(16):
        for p in range(128):
            selg[8 * g + p // 16, g * 128 + p] = 1.0
            selr[p, g * 128 + 8 * g + p // 16] = 1.0
    selbc = np.zeros((96, 256), np.float32)
    for p in range(128):
        selbc[64 + p % 16, p] = 1.0
        selbc[80 + p % 16, 128 + p] = 1.0
    mwT = np.ascontiguousarray(maskw.T.astype(np.float32))
    mws = np.ascontiguousarray(mwT.reshape(1, 14 * T))
    xpw_arr = chunk128(np.ascontiguousarray(xpw_w.T))

    A3 = _get("A3", build_A3)
    im = []
    for c in range(NCORE):
        dsl = slice(c * 128, (c + 1) * 128)
        csl = slice(c * 256, (c + 1) * 256)
        w1sl = w1[:, :, csl]
        w1c = np.ascontiguousarray(
            w1sl.reshape(14, 8, 128, 2, 128).transpose(2, 0, 1, 3, 4)
            .reshape(128, 28672))
        w2sl = w2[:, csl, :]
        w2c = np.ascontiguousarray(
            w2sl.reshape(14, 2, 128, 8, 128).transpose(2, 0, 1, 3, 4)
            .reshape(128, 28672))
        b1sl = b1[:, csl]
        b1c = np.ascontiguousarray(
            b1sl.reshape(14, 2, 128).transpose(2, 0, 1).reshape(128, 28))
        im.append({
            "xnf": xn3,
            "xnd": np.ascontiguousarray(xn3[dsl]),
            "xpw": xpw_arr,
            "dtw": np.ascontiguousarray(dtw_w[dsl].T),
            "dtb": np.ascontiguousarray(dtb_w[dsl].reshape(128, 1)),
            "alg": np.ascontiguousarray(np.tile(A_log, 8).reshape(128, 1)),
            "dvc": np.ascontiguousarray(Dv[dsl].reshape(128, 1)),
            "selg": selg,
            "selbc": selbc,
            "selr": selr,
            "mws": mws,
            "mw14": mwT,
            "w1c": w1c,
            "w2c": w2c,
            "b1c": b1c,
            "b2f": np.ascontiguousarray(b2),
        })
    rA = run_bass_kernel_spmd(A3, im, core_ids=list(range(NCORE)))
    y3full = np.concatenate([rA.results[c]["yo3"] for c in range(NCORE)],
                            axis=0)
    mo3s = [rA.results[c]["mo3"] for c in range(NCORE)]

    B3 = _get("B3", lambda: build_B_add(1, 8, ln_out=False))
    im = []
    for c in range(NCORE):
        sl = slice(c * TSL, (c + 1) * TSL)
        tb = np.concatenate(
            [np.ascontiguousarray(m.T[sl]) for m in mo3s], axis=0)
        im.append({
            "x0": x3[sl],
            "tf": np.ascontiguousarray(y3full[:, sl].T),
            "tb": tb,
        })
    rB = run_bass_kernel_spmd(B3, im, core_ids=list(range(NCORE)))
    x4 = np.concatenate([rB.results[c]["x1o"] for c in range(NCORE)],
                        axis=0)
    return x4, {"y3": y3full, "mo3s": mo3s}


def kernel(**inputs):
    x1, xn1, _ = run_L0(inputs)
    x2, xn2, _ = run_L1(inputs, x1, xn1)
    x3, xn3, maskw, _ = run_L2(inputs, x2, xn2)
    x4, _ = run_L3(inputs, x3, xn3, maskw)
    return np.ascontiguousarray(x4.reshape(2, S, 1024).astype(np.float32))


# version 03 (final)
# speedup vs baseline: 1.0961x; 1.0961x over previous
import contextlib
import numpy as np
import ml_dtypes
import concourse.bass as bass
import concourse.mybir as mybir
from concourse.bass_utils import run_bass_kernel_spmd

F32 = mybir.dt.float32
BF16 = mybir.dt.bfloat16
I32 = mybir.dt.int32
I8 = mybir.dt.int8
AF = mybir.ActivationFunctionType
ALU = mybir.AluOpType
AX = mybir.AxisListType
BF = ml_dtypes.bfloat16

NCORE = 8
S = 8192
T = 16384
TSL = T // NCORE
NTT = TSL // 128

ENGS = ("sync", "vector", "scalar", "tensor", "gpsimd")


class Sem:
    def __init__(self, h):
        self.h = h
        self.v = 0


class Prog:

    def __init__(self):
        self.q = {k: [] for k in ENGS}

    def emit(self, eng, fn, inc=None, n=1):
        if inc is None:
            self.q[eng].append(fn)
        else:
            inc.v += n
            h = inc.h
            self.q[eng].append(lambda e, fn=fn, h=h, n=n:
                               fn(e).then_inc(h, n))

    def wait(self, eng, sem, val=None):
        v = sem.v if val is None else val
        self.q[eng].append(lambda e, s=sem.h, v=v: e.wait_ge(s, v))

    def play(self, blk):
        @blk.sync
        def _(e):
            for f in self.q["sync"]:
                f(e)

        @blk.vector
        def _(e):
            for f in self.q["vector"]:
                f(e)

        @blk.scalar
        def _(e):
            for f in self.q["scalar"]:
                f(e)

        @blk.tensor
        def _(e):
            for f in self.q["tensor"]:
                f(e)

        @blk.gpsimd
        def _(e):
            for f in self.q["gpsimd"]:
                f(e)



def _bucket(rel):
    rel = np.asarray(rel)
    nb, me = 16, 8
    ret = (rel > 0).astype(np.int64) * nb
    arel = np.abs(rel)
    small = arel < me
    rp = me + (np.log(np.maximum(arel, 1).astype(np.float64) / me)
               / np.log(128.0 / me) * (nb - me)).astype(np.int64)
    rp = np.minimum(rp, nb - 1)
    return ret + np.where(small, arel, rp)


def make_rel_ohT():
    t = np.arange(1280)
    bk = _bucket(639 - t)
    oh = np.zeros((32, 1280), np.float32)
    oh[bk, t] = 1.0
    return oh


L0_NEAR = (-128, 0, 128, 256, 384, 512)


def chunk128(w):
    w = np.asarray(w)
    k = w.shape[0] // 128
    return np.ascontiguousarray(
        w.reshape(k, 128, w.shape[1]).transpose(1, 0, 2)
        .reshape(128, k * w.shape[1]))



def sexp(P, out, in_, scale=1.0, bias=0.0, Sp=None):
    P.emit("scalar", lambda e: e.activation(out=out, in_=in_, func=AF.Exp,
                                            scale=scale, bias=bias), Sp)


NLE_SET = 6


def force_table(P, idx=NLE_SET):
    def em(e):
        return e.add_instruction(mybir.InstLoadActFuncSet(
            name=e.bass.get_next_instruction_name(), act_func_set_id=idx,
            ins=[], outs=[]))
    P.emit("scalar", em)


LN2_32 = float(np.float32(32 * np.log(2.0)))


def recip_lnexp(P, Sp, out, in_):
    P.emit("scalar", lambda e: e.activation(out=out, in_=in_, func=AF.Ln),
           Sp)
    P.wait("scalar", Sp)
    P.emit("scalar", lambda e: e.activation(out=out, in_=out, func=AF.Exp,
                                            scale=-1.0), Sp)


def recip_big(P, Sp, out, in_, nb):
    P.emit("scalar", lambda e: e.activation(out=out, in_=in_, func=AF.Ln,
                                            scale=float(2.0 ** -32)), Sp)
    P.wait("scalar", Sp)
    P.emit("scalar", lambda e: e.activation(out=out, in_=out, func=AF.Exp,
                                            scale=-1.0, bias=nb), Sp)


def ln_tile(P, Vp, Sp, xt, xn_out, sq, cols, want_rstd2=False):
    def V(fn):
        P.emit("vector", fn, Vp)
        P.wait("vector", Vp)

    P.emit("scalar", lambda e: e.activation(out=sq[:], in_=xt[:],
                                            func=AF.Square,
                                            accum_out=cols[:, 1:2]), Sp)
    P.wait("vector", Sp)
    V(lambda e: e.reduce_sum(out=cols[:, 0:1], in_=xt[:], axis=AX.X))
    V(lambda e: e.tensor_single_scalar(
        out=cols[:, 2:3], in_=cols[:, 0:1], scalar=1.0 / 1024, op=ALU.mult))
    V(lambda e: e.tensor_tensor(
        out=cols[:, 7:8], in0=cols[:, 2:3], in1=cols[:, 2:3], op=ALU.mult))
    V(lambda e: e.tensor_single_scalar(
        out=cols[:, 3:4], in_=cols[:, 1:2], scalar=1.0 / 1024, op=ALU.mult))
    V(lambda e: e.tensor_tensor(
        out=cols[:, 3:4], in0=cols[:, 3:4], in1=cols[:, 7:8],
        op=ALU.subtract))
    V(lambda e: e.tensor_single_scalar(
        out=cols[:, 7:8], in_=cols[:, 3:4], scalar=1e-5, op=ALU.add))
    P.wait("scalar", Vp)
    P.emit("scalar", lambda e: e.activation(out=cols[:, 4:5],
                                            in_=cols[:, 7:8], func=AF.Ln),
           Sp)
    P.wait("scalar", Sp)
    P.emit("scalar", lambda e: e.activation(out=cols[:, 4:5],
                                            in_=cols[:, 4:5], func=AF.Exp,
                                            scale=-0.5), Sp)
    P.wait("vector", Sp)
    V(lambda e: e.tensor_tensor(
        out=cols[:, 5:6], in0=cols[:, 2:3], in1=cols[:, 4:5], op=ALU.mult))
    V(lambda e: e.tensor_single_scalar(
        out=cols[:, 5:6], in_=cols[:, 5:6], scalar=-1.0, op=ALU.mult))
    P.wait("scalar", Vp)
    P.emit("scalar", lambda e: e.activation(out=xn_out[:], in_=xt[:],
                                            func=AF.Identity,
                                            scale=cols[:, 4:5],
                                            bias=cols[:, 5:6]), Sp)
    if want_rstd2:
        P.emit("scalar", lambda e: e.activation(out=cols[:, 6:7],
                                                in_=cols[:, 7:8],
                                                func=AF.Ln), Sp)
        P.wait("scalar", Sp)
        P.emit("scalar", lambda e: e.activation(out=cols[:, 6:7],
                                                in_=cols[:, 6:7],
                                                func=AF.Exp, scale=-1.0), Sp)
        P.wait("vector", Sp)
        V(lambda e: e.tensor_tensor(
            out=cols[:, 6:7], in0=cols[:, 3:4], in1=cols[:, 6:7],
            op=ALU.mult))
        V(lambda e: e.tensor_single_scalar(
            out=cols[:, 6:7], in_=cols[:, 6:7], scalar=1e-6, op=ALU.add))
        P.wait("scalar", Vp)
        P.emit("scalar", lambda e: e.activation(out=cols[:, 6:7],
                                                in_=cols[:, 6:7],
                                                func=AF.Ln), Sp)
        P.wait("scalar", Sp)
        P.emit("scalar", lambda e: e.activation(out=cols[:, 6:7],
                                                in_=cols[:, 6:7],
                                                func=AF.Exp, scale=-0.5), Sp)



def build_pre0():
    nc = bass.Bass("TRN2", target_bir_lowering=False, debug=False)
    x = nc.dram_tensor("x", [TSL, 1024], F32, kind="ExternalInput").ap()
    nw8 = nc.dram_tensor("nw8", [128, 8], F32, kind="ExternalInput").ap()
    rw64 = nc.dram_tensor("rw64", [128, 64], F32, kind="ExternalInput").ap()
    xnfm = nc.dram_tensor("xnfm", [1024, TSL], BF16,
                          kind="ExternalOutput").ap()
    r2o = nc.dram_tensor("r2o", [128, NTT], F32, kind="ExternalOutput").ap()
    cmo = nc.dram_tensor("cmo", [128, NTT * 8], F32,
                         kind="ExternalOutput").ap()

    es = contextlib.ExitStack()
    with es:
        sb = lambda n, s, d: es.enter_context(nc.sbuf_tensor(n, s, d))
        psr = lambda n, s, d: es.enter_context(nc.psum_tensor(n, s, d))
        sem = lambda n: Sem(es.enter_context(nc.semaphore(n)))
        xt = sb("xt", [128, 1024], F32)
        sq = sb("sq", [128, 1024], F32)
        xnb = sb("xnb", [128, 1024], BF16)
        xnfmt = sb("xnfmt", [128, 128], BF16)
        cols = sb("cols", [128, 8], F32)
        r2 = sb("r2", [128, NTT], F32)
        cmb = sb("cmb", [128, NTT * 8], F32)
        lgt = sb("lgt", [128, 8], F32)
        w8 = sb("w8", [128, 40], F32)
        rwf = sb("rwf", [128, 64], F32)
        rwb = sb("rwb", [128, 64], BF16)
        nwt = sb("nwt", [128, 8], F32)
        ones_t = sb("ones", [128, 128], BF16)
        idn = sb("idn", [128, 128], BF16)
        ptr = psr("ptr", [128, 128], BF16)
        plg = psr("plg", [128, 8], F32)
        lda = sem("s_lda")
        lxt = sem("s_lxt")
        sxn = sem("s_sxn")
        sfin = sem("s_sfin")
        Vp = sem("s_v")
        Sp = sem("s_s")
        Tm = sem("s_t")
        G = sem("s_g")
        blk = es.enter_context(nc.Block())
        P = Prog()
        force_table(P)

        P.emit("sync", lambda e: e.dma_start(out=nwt[:], in_=nw8), lda, 16)
        P.emit("sync", lambda e: e.dma_start(out=rwf[:], in_=rw64), lda, 16)
        lda_w = lda.v

        P.emit("vector", lambda e: e.memset(ones_t[:], 1.0))
        P.wait("vector", lda, lda_w)
        for k in range(8):
            a, b2, nwc = (rwb[:, 8 * k:8 * k + 8], rwf[:, 8 * k:8 * k + 8],
                          nwt[:, k:k + 1])
            P.emit("vector", lambda e, a=a, b2=b2, nwc=nwc:
                   e.tensor_single_scalar(out=a, in_=b2, scalar=nwc,
                                          op=ALU.mult),
                   Vp if k == 7 else None)
        v_rwb = Vp.v
        P.wait("gpsimd", Vp, v_rwb)
        P.emit("gpsimd", lambda e: e.affine_select(
            out=idn[:], in_=ones_t[:], pattern=[[-1, 128]],
            compare_op=ALU.is_equal, fill=0.0, base=0, channel_multiplier=1),
            G)
        g_idn = G.v
        P.wait("tensor", G, g_idn)

        for i in range(NTT):
            P.wait("sync", Sp)
            src = x[i * 128:(i + 1) * 128, :]
            P.emit("sync", lambda e, src=src: e.dma_start(out=xt[:],
                                                          in_=src), lxt, 16)
            lx = lxt.v
            P.wait("scalar", lxt, lx)
            P.wait("scalar", Tm)
            P.wait("vector", lxt, lx)
            ln_tile(P, Vp, Sp, xt, xnb, sq, cols, want_rstd2=True)
            P.wait("vector", Sp)
            dst = r2[:, i:i + 1]
            P.emit("vector", lambda e, dst=dst: e.tensor_copy(
                out=dst, in_=cols[:, 6:7]), Vp)

            for k in range(8):
                P.wait("tensor", Sp)
                P.wait("tensor", Vp)
                src = xnb[:, 128 * k:128 * (k + 1)]
                P.emit("tensor", lambda e, src=src: e.transpose(
                    ptr[:], src, idn[:]), Tm)
                P.wait("vector", Tm)
                P.wait("vector", sxn)
                P.emit("vector", lambda e: e.tensor_copy(out=xnfmt[:],
                                                         in_=ptr[:]), Vp)
                v_ev = Vp.v
                P.wait("sync", Vp, v_ev)
                dst = xnfm[128 * k:128 * (k + 1), i * 128:(i + 1) * 128]
                P.emit("sync", lambda e, dst=dst: e.dma_start(
                    out=dst, in_=xnfmt[:]), sxn, 16)
                P.wait("tensor", Vp, v_ev)
                rw_k = rwb[:, 8 * k:8 * k + 8]
                P.emit("tensor", lambda e, rw_k=rw_k, k=k: e.matmul(
                    plg[:], xnfmt[:], rw_k, start=(k == 0), stop=(k == 7)),
                    Tm)

            P.wait("scalar", Tm)
            P.wait("scalar", Sp)
            P.emit("scalar", lambda e: e.activation(
                out=lgt[:], in_=plg[:], func=AF.Identity,
                scale=cols[:, 6:7]), Sp)
            P.wait("vector", Sp)
            P.emit("vector", lambda e: e.reduce_max(
                out=w8[:, 0:1], in_=lgt[:], axis=AX.X, negate=True), Vp)
            P.wait("scalar", Vp)
            P.emit("scalar", lambda e: e.activation(
                out=w8[:, 8:16], in_=lgt[:], func=AF.Exp, bias=w8[:, 0:1]),
                Sp)
            P.wait("vector", Sp)

            def V(fn):
                P.emit("vector", fn, Vp)
                P.wait("vector", Vp)
            V(lambda e: e.tensor_single_scalar(
                out=w8[:, 16:24], in_=w8[:, 8:16], scalar=1.0, op=ALU.is_ge))
            V(lambda e: e.tensor_tensor(
                out=w8[:, 24:32], in0=w8[:, 8:16], in1=w8[:, 16:24],
                op=ALU.mult))
            V(lambda e: e.tensor_tensor(
                out=w8[:, 24:32], in0=w8[:, 8:16], in1=w8[:, 24:32],
                op=ALU.subtract))
            V(lambda e: e.reduce_max(
                out=w8[:, 1:2], in_=w8[:, 24:32], axis=AX.X))
            V(lambda e: e.tensor_single_scalar(
                out=w8[:, 32:40], in_=w8[:, 24:32], scalar=w8[:, 1:2],
                op=ALU.is_ge))
            V(lambda e: e.tensor_tensor(
                out=w8[:, 16:24], in0=w8[:, 16:24], in1=w8[:, 32:40],
                op=ALU.add))
            V(lambda e: e.tensor_tensor(
                out=w8[:, 16:24], in0=w8[:, 8:16], in1=w8[:, 16:24],
                op=ALU.mult))
            V(lambda e: e.reduce_sum(
                out=w8[:, 2:3], in_=w8[:, 16:24], axis=AX.X))
            P.wait("scalar", Vp)
            recip_lnexp(P, Sp, w8[:, 3:4], w8[:, 2:3])
            P.wait("vector", Sp)
            dst = cmb[:, 8 * i:8 * i + 8]
            P.emit("vector", lambda e, dst=dst: e.tensor_single_scalar(
                out=dst, in_=w8[:, 16:24], scalar=w8[:, 3:4], op=ALU.mult),
                Vp)

        P.wait("sync", Vp)
        P.emit("sync", lambda e: e.dma_start(out=r2o, in_=r2[:]), sfin, 16)
        P.emit("sync", lambda e: e.dma_start(out=cmo, in_=cmb[:]), sfin, 16)
        P.wait("sync", sfin)
        P.wait("sync", sxn)
        P.play(blk)
    return nc



def build_A0(stage=3, dbg=False):
    nc = bass.Bass("TRN2", target_bir_lowering=False, debug=False)
    xn = nc.dram_tensor("xn", [1024, T], BF16, kind="ExternalInput").ap()
    wq = nc.dram_tensor("wq", [128, 8 * 768], F32, kind="ExternalInput").ap()
    rel4 = nc.dram_tensor("rel4", [32, 4], F32, kind="ExternalInput").ap()
    relohT = nc.dram_tensor("relohT", [32, 1280], F32,
                            kind="ExternalInput").ap()
    gup = nc.dram_tensor("gup", [128, 8 * 5632], F32,
                         kind="ExternalInput").ap()
    dwn = nc.dram_tensor("dwn", [128, 22 * 1024], F32,
                         kind="ExternalInput").ap()
    combr = nc.dram_tensor("combr", [1, T], F32, kind="ExternalInput").ap()
    alpha = nc.dram_tensor("alpha", [1, 1], F32, kind="ExternalInput").ap()
    r2r = nc.dram_tensor("r2r", [1, T], F32, kind="ExternalInput").ap()
    nw8 = nc.dram_tensor("nw8", [128, 8], F32, kind="ExternalInput").ap()
    yo = nc.dram_tensor("yo", [256, S], BF16, kind="ExternalOutput").ap()
    mpo = nc.dram_tensor("mpo", [1024, T], BF16, kind="ExternalOutput").ap()
    if dbg:
        dqo = nc.dram_tensor("dqo", [128, S], BF16,
                             kind="ExternalOutput").ap()
        dko = nc.dram_tensor("dko", [128, S], BF16,
                             kind="ExternalOutput").ap()
        dvo = nc.dram_tensor("dvo", [128, 65 * 64], BF16,
                             kind="ExternalOutput").ap()

    NKT = S // 128
    NQC = S // 512
    es = contextlib.ExitStack()
    with es:
        sb = lambda n, s, d: es.enter_context(nc.sbuf_tensor(n, s, d))
        psr = lambda n, s, d: es.enter_context(nc.psum_tensor(n, s, d))
        sem = lambda n: Sem(es.enter_context(nc.semaphore(n)))
        xch = [sb("xch0", [128, 8 * 512], BF16),
               sb("xch1", [128, 8 * 512], BF16)]
        ones1 = sb("ones1", [1, 128], BF16)
        ones1f = sb("ones1f", [1, 128], F32)
        alf = sb("alf", [1, 128], BF16)
        alfs = sb("alfs", [1, 4], F32)
        stg = sb("stg", [1, 1024], F32)
        stgb = sb("stgb", [1, 512], BF16)
        farb = sb("farb", [128, 8], F32)
        nwt = sb("nwt", [128, 8], F32)
        psA = [psr("psA0", [128, 512], F32), psr("psA1", [128, 512], F32)]
        psB = [psr("psB0", [128, 512], F32), psr("psB1", [128, 512], F32)]
        outA = psr("outA", [128, 512], F32)
        outB = psr("outB", [128, 512], F32)
        paux = psr("paux", [128, 512], F32)
        paux2 = psr("paux2", [128, 512], F32)
        lda = sem("s_lda")
        ldg = sem("s_ldg")
        ldm = sem("s_ldm")
        lx = [sem("s_lx0"), sem("s_lx1")]
        lsg = sem("s_lsg")
        lws = sem("s_lws")
        sty = sem("s_sty")
        smo = sem("s_smo")
        sdbg = sem("s_sdbg")
        Vp = sem("s_v")
        Sp = sem("s_s")
        Mm = sem("s_m")
        Pv = sem("s_p")
        ph = sem("s_ph")

        blk = es.enter_context(nc.Block())

        with contextlib.ExitStack() as es1:
            sb1 = lambda n, s, d: es1.enter_context(nc.sbuf_tensor(n, s, d))
            qf = sb1("qf", [128, S], BF16)
            kf = sb1("kf", [128, S], BF16)
            vA = sb1("vA", [128, 65 * 64], BF16)
            vB = sb1("vB", [128, 65 * 64], BF16)
            wqb = sb1("wqb", [128, 8 * 768], BF16)
            megaA = sb1("megaA", [128, 1152], F32)
            megaB = sb1("megaB", [128, 1152], F32)
            relst = sb1("relst", [32, 1284], F32)
            diagR = sb1("diagR", [4, 1280], F32)
            dgr = sb1("dgr", [1, 4 * 1280], F32)
            eA = sb1("eA", [128, 2 * 512], BF16)
            eB = sb1("eB", [128, 2 * 512], BF16)
            sf32 = sb1("sf32", [128, 512], F32)
            rnrow = sb1("rnrow", [1, 1024], F32)
            rnb = sb1("rnb", [64, 512], F32)
            ych = sb1("ych", [128, 512], BF16)
            wst = sb1("wst", [128, 6144], F32)
            P = Prog()
            force_table(P)

            P.emit("sync", lambda e: e.dma_start(out=wst[:], in_=wq),
                   lda, 16)
            P.emit("sync", lambda e: e.dma_start(out=relst[:, 0:4],
                                                 in_=rel4), lda, 16)
            P.emit("sync", lambda e: e.dma_start(out=relst[:, 4:1284],
                                                 in_=relohT), lda, 16)
            P.emit("sync", lambda e: e.dma_start(out=alfs[0:1, 0:1],
                                                 in_=alpha), lda, 16)
            P.emit("sync", lambda e: e.dma_start(out=nwt[:], in_=nw8),
                   lda, 16)
            lda_all = lda.v

            P.emit("vector", lambda e: e.memset(ones1[:], 1.0))
            P.emit("vector", lambda e: e.memset(ones1f[:], 1.0))
            va_ones = vA[:].rearrange("p (t c) -> p t c", c=65)[:, :, 64:65]
            vb_ones = vB[:].rearrange("p (t c) -> p t c", c=65)[:, :, 64:65]
            P.emit("vector", lambda e: e.memset(va_ones, 1.0))
            P.emit("vector", lambda e: e.memset(vb_ones, 1.0))
            P.wait("vector", lda, lda_all)
            P.emit("vector", lambda e: e.tensor_copy(out=wqb[:],
                                                     in_=wst[:]), Vp)
            P.emit("vector", lambda e: e.tensor_single_scalar(
                out=alfs[0:1, 1:2], in_=alfs[0:1, 0:1], scalar=-0.5,
                op=ALU.max), Vp)
            P.wait("vector", Vp)
            P.emit("vector", lambda e: e.tensor_single_scalar(
                out=alfs[0:1, 1:2], in_=alfs[0:1, 1:2], scalar=2.0,
                op=ALU.min), Vp)
            v_setup = Vp.v
            P.wait("scalar", Vp, v_setup)
            P.emit("scalar", lambda e: e.activation(
                out=alf[:], in_=ones1[:], func=AF.Identity,
                scale=alfs[0:1, 1:2]), Sp)

            P.wait("tensor", lda, lda_all)
            for c0 in range(0, 1280, 320):
                rhs = relst[0:32, 4 + c0:4 + c0 + 320]
                P.emit("tensor", lambda e, rhs=rhs: e.matmul(
                    paux[0:4, 0:320], relst[0:32, 0:4], rhs,
                    start=True, stop=True), Mm)
                P.wait("vector", Mm)
                dst = diagR[0:4, c0:c0 + 320]
                P.emit("vector", lambda e, dst=dst: e.tensor_copy(
                    out=dst, in_=paux[0:4, 0:320]), Vp)
                P.wait("tensor", Vp)
            v_diag = Vp.v
            P.wait("sync", Vp, v_diag)
            for hh in range(4):
                dst = dgr[0:1, hh * 1280:(hh + 1) * 1280]
                srcr = diagR[hh:hh + 1, :]
                P.emit("sync", lambda e, dst=dst, srcr=srcr: e.dma_start(
                    out=dst, in_=srcr), ldg, 16)
            ldg_all = ldg.v

            P.wait("tensor", ldg, ldg_all)
            for hh in range(4):
                for side in range(2):
                    tpos = 1278 if side == 0 else 0
                    outp = paux2[:, hh * 2 + side:hh * 2 + side + 1]
                    rhs = dgr[0:1, hh * 1280 + tpos:hh * 1280 + tpos + 1]
                    P.emit("tensor", lambda e, outp=outp, rhs=rhs: e.matmul(
                        outp, ones1f[0:1, :], rhs, start=True, stop=True),
                        Mm if (hh, side) == (3, 1) else None)
            m_far = Mm.v
            P.wait("vector", Mm, m_far)
            P.emit("vector", lambda e: e.tensor_copy(out=farb[:, 0:8],
                                                     in_=paux2[:, 0:8]), Vp)
            v_far = Vp.v

            for pair in range(2):
                P.wait("tensor", Vp)
                P.wait("gpsimd", ldg, ldg_all)
                P.wait("gpsimd", Sp)
                for i in range(128):
                    for mg, hh in ((megaA, 2 * pair), (megaB, 2 * pair + 1)):
                        dst = mg[i:i + 1, :]
                        src = dgr[0:1, hh * 1280 + 127 - i:
                                  hh * 1280 + 127 - i + 1152]
                        P.emit("gpsimd", lambda e, dst=dst, src=src:
                               e.dma_start(out=dst, in_=src), ldm, 16)
                ldm_pair = ldm.v

                for tc in range(NQC):
                    par = tc & 1
                    xb = xch[par]
                    P.wait("sync", Mm)
                    for ch in range(8):
                        dst = xb[:, ch * 512:(ch + 1) * 512]
                        src = xn[ch * 128:(ch + 1) * 128,
                                 tc * 512:(tc + 1) * 512]
                        P.emit("sync", lambda e, dst=dst, src=src:
                               e.dma_start(out=dst, in_=src), lx[par], 16)
                    lx_t = lx[par].v
                    P.wait("tensor", lx[par], lx_t)
                    for which, base in (("q", 0), ("k", 256)):
                        P.wait("tensor", Sp)
                        for ch in range(8):
                            lhsT = wqb[:, ch * 768 + base + pair * 128:
                                       ch * 768 + base + pair * 128 + 128]
                            rhs = xb[:, ch * 512:(ch + 1) * 512]
                            P.emit("tensor",
                                   lambda e, lhsT=lhsT, rhs=rhs, ch=ch:
                                   e.matmul(paux[:], lhsT, rhs,
                                            start=(ch == 0), stop=(ch == 7)),
                                   Mm if ch == 7 else None)
                        m_qk = Mm.v
                        dstf = (qf if which == "q"
                                else kf)[:, tc * 512:(tc + 1) * 512]
                        P.wait("scalar", Mm, m_qk)
                        P.emit("scalar", lambda e, dstf=dstf: e.activation(
                            out=dstf, in_=paux[:], func=AF.Identity), Sp)
                    for sub in range(4):
                        kt = tc * 4 + sub
                        P.wait("tensor", Vp)
                        for ch in range(8):
                            lhsT = xb[:, ch * 512 + sub * 128:
                                      ch * 512 + sub * 128 + 128]
                            rhs = wqb[:, ch * 768 + 512 + pair * 128:
                                      ch * 768 + 512 + pair * 128 + 128]
                            P.emit("tensor",
                                   lambda e, lhsT=lhsT, rhs=rhs, ch=ch:
                                   e.matmul(paux2[:, 0:128], lhsT, rhs,
                                            start=(ch == 0), stop=(ch == 7)),
                                   Mm if ch == 7 else None)
                        m_v = Mm.v
                        dstA = vA[:, kt * 65:kt * 65 + 64]
                        dstB = vB[:, kt * 65:kt * 65 + 64]
                        P.wait("vector", Mm, m_v)
                        P.emit("vector", lambda e, dstA=dstA: e.tensor_copy(
                            out=dstA, in_=paux2[:, 0:64]))
                        P.emit("vector", lambda e, dstB=dstB: e.tensor_copy(
                            out=dstB, in_=paux2[:, 64:128]), Vp)

                if dbg and pair == 0:
                    P.wait("sync", Vp)
                    P.wait("sync", Sp)
                    P.emit("sync", lambda e: e.dma_start(out=dqo,
                                                         in_=qf[:]),
                           sdbg, 16)
                    P.emit("sync", lambda e: e.dma_start(out=dko,
                                                         in_=kf[:]),
                           sdbg, 16)
                    P.emit("sync", lambda e: e.dma_start(out=dvo,
                                                         in_=vA[:]),
                           sdbg, 16)
                if stage < 2:
                    break

                P.wait("tensor", Vp)
                P.wait("scalar", ldm, ldm_pair)
                P.wait("scalar", Vp, v_far)
                P.wait("vector", ldm, ldm_pair)
                for qc in range(NQC):
                    sp_qc0 = Sp.v
                    mm_qc0 = Mm.v
                    pv_qc0 = Pv.v
                    q0 = qc * 512
                    qslA = qf[0:64, q0:q0 + 512]
                    qslB = qf[64:128, q0:q0 + 512]
                    for kt in range(NKT):
                        par = kt & 1
                        P.wait("tensor", Sp,
                               sp_qc0 + (2 * (kt - 1) if kt >= 2 else 0))
                        kslA = kf[0:64, kt * 128:(kt + 1) * 128]
                        kslB = kf[64:128, kt * 128:(kt + 1) * 128]
                        pA, pB = psA[par], psB[par]
                        P.emit("tensor",
                               lambda e, kslA=kslA, qslA=qslA, pA=pA:
                               e.matmul(pA[:], kslA, qslA, start=True,
                                        stop=True, tile_position=(0, 0)))
                        P.emit("tensor",
                               lambda e, kslB=kslB, qslB=qslB, pB=pB:
                               e.matmul(pB[:], kslB, qslB, start=True,
                                        stop=True, tile_position=(64, 0)),
                               Mm)
                        if kt >= 1:
                            ktp = kt - 1
                            parp = ktp & 1
                            P.wait("tensor", Sp, sp_qc0 + 2 * (ktp + 1))
                            vslA = vA[:, ktp * 65:(ktp + 1) * 65]
                            vslB = vB[:, ktp * 65:(ktp + 1) * 65]
                            eslA = eA[:, parp * 512:parp * 512 + 512]
                            eslB = eB[:, parp * 512:parp * 512 + 512]
                            P.emit("tensor",
                                   lambda e, vslA=vslA, eslA=eslA, ktp=ktp:
                                   e.matmul(outA[0:65, :], vslA, eslA,
                                            start=(ktp == 0), stop=False))
                            P.emit("tensor",
                                   lambda e, vslB=vslB, eslB=eslB, ktp=ktp:
                                   e.matmul(outB[0:65, :], vslB, eslB,
                                            start=(ktp == 0), stop=False),
                                   Pv)
                        d0 = kt * 128 - q0
                        P.wait("scalar", Mm, mm_qc0 + (kt + 1))
                        P.wait("scalar", Pv,
                               pv_qc0 + (kt - 1 if kt >= 2 else 0))
                        for hh in range(2):
                            pX = pA if hh == 0 else pB
                            eX = eA if hh == 0 else eB
                            mg = megaA if hh == 0 else megaB
                            edst = eX[:, par * 512:par * 512 + 512]
                            if d0 in L0_NEAR:
                                s0 = 512 - d0
                                msl = mg[:, s0:s0 + 512]
                                P.wait("vector", Mm, mm_qc0 + (kt + 1))
                                P.wait("vector", Sp)
                                P.emit("vector",
                                       lambda e, pX=pX, msl=msl:
                                       e.tensor_tensor(out=sf32[:],
                                                       in0=pX[:], in1=msl,
                                                       op=ALU.add), Vp)
                                P.wait("scalar", Vp)
                                P.emit("scalar", lambda e, edst=edst:
                                       e.activation(out=edst, in_=sf32[:],
                                                    func=AF.Exp), Sp)
                            else:
                                side = 0 if d0 < 0 else 1
                                bcol = farb[:, (2 * pair + hh) * 2 + side:
                                            (2 * pair + hh) * 2 + side + 1]
                                P.emit("scalar",
                                       lambda e, edst=edst, pX=pX,
                                       bcol=bcol: e.activation(
                                           out=edst, in_=pX[:], func=AF.Exp,
                                           bias=bcol), Sp)
                    ktp = NKT - 1
                    parp = ktp & 1
                    P.wait("tensor", Sp, sp_qc0 + 2 * (ktp + 1))
                    vslA = vA[:, ktp * 65:(ktp + 1) * 65]
                    vslB = vB[:, ktp * 65:(ktp + 1) * 65]
                    eslA = eA[:, parp * 512:parp * 512 + 512]
                    eslB = eB[:, parp * 512:parp * 512 + 512]
                    P.emit("tensor", lambda e, vslA=vslA, eslA=eslA:
                           e.matmul(outA[0:65, :], vslA, eslA, start=False,
                                    stop=True))
                    P.emit("tensor", lambda e, vslB=vslB, eslB=eslB:
                           e.matmul(outB[0:65, :], vslB, eslB, start=False,
                                    stop=True), Pv)
                    pv_done = Pv.v

                    P.wait("vector", Pv, pv_done)
                    P.emit("vector", lambda e: e.reciprocal(
                        out=rnrow[0:1, 0:512], in_=outA[64:65, :]), Vp)
                    P.wait("vector", Vp)
                    P.emit("vector", lambda e: e.reciprocal(
                        out=rnrow[0:1, 512:1024], in_=outB[64:65, :]), Vp)
                    P.wait("vector", Vp)
                    s_rn = Vp.v
                    for hh in range(2):
                        P.wait("tensor", Vp, s_rn if hh == 0 else Vp.v)
                        rsl = rnrow[0:1, hh * 512:hh * 512 + 512]
                        P.emit("tensor", lambda e, rsl=rsl: e.matmul(
                            paux[0:64, :], ones1f[0:1, 0:64], rsl,
                            start=True, stop=True), Mm)
                        m_bc = Mm.v
                        P.wait("vector", Mm, m_bc)
                        P.emit("vector", lambda e: e.tensor_copy(
                            out=rnb[:], in_=paux[0:64, :]), Vp)
                        P.wait("vector", Vp)
                        P.wait("vector", sty)
                        oX = outA if hh == 0 else outB
                        ysl = ych[hh * 64:hh * 64 + 64, :]
                        P.emit("vector", lambda e, oX=oX, ysl=ysl:
                               e.tensor_tensor(out=ysl, in0=oX[0:64, :],
                                               in1=rnb[:], op=ALU.mult), Vp)
                    v_y = Vp.v
                    P.wait("scalar", Vp, v_y)
                    dst = yo[pair * 128:(pair + 1) * 128, q0:q0 + 512]
                    P.emit("scalar", lambda e, dst=dst: e.dma_start(
                        out=dst, in_=ych[:]), sty, 16)
                    P.wait("tensor", Vp, v_y)

            P.play(blk)
            PB = Prog()
            for eng in ENGS:
                PB.emit(eng, lambda e: e.nop(), ph)
            for eng in ENGS:
                PB.wait(eng, ph, 5 * (1 if True else 1))
            PB.play(blk)

        if stage < 3:
            PF = Prog()
            PF.wait("sync", sty)
            PF.wait("sync", sdbg)
            PF.wait("gpsimd", ldm)
            PF.play(blk)
            return nc

        with contextlib.ExitStack() as es2:
            sb2 = lambda n, s, d: es2.enter_context(nc.sbuf_tensor(n, s, d))
            gupb = sb2("gupb", [128, 8 * 5632], BF16)
            dwnb = sb2("dwnb", [128, 22 * 1024], BF16)
            P = Prog()
            with contextlib.ExitStack() as es2a:
                wst2 = es2a.enter_context(
                    nc.sbuf_tensor("wst2", [128, 2816], F32))
                force_table(P, 18)
                for ch in range(8):
                    for hlf in range(2):
                        src = gup[:, ch * 5632 + hlf * 2816:
                                  ch * 5632 + (hlf + 1) * 2816]
                        P.wait("sync", Vp)
                        P.emit("sync", lambda e, src=src: e.dma_start(
                            out=wst2[:], in_=src), lws, 16)
                        lw_c = lws.v
                        P.wait("vector", lws, lw_c)
                        dst = gupb[:, ch * 5632 + hlf * 2816:
                                   ch * 5632 + (hlf + 1) * 2816]
                        nwc = nwt[:, ch:ch + 1]
                        P.emit("vector", lambda e, dst=dst, nwc=nwc:
                               e.tensor_single_scalar(out=dst, in_=wst2[:],
                                                      scalar=nwc,
                                                      op=ALU.mult), Vp)
                for ch in range(11):
                    src = dwn[:, ch * 2048:(ch + 1) * 2048]
                    P.wait("sync", Vp)
                    P.emit("sync", lambda e, src=src: e.dma_start(
                        out=wst2[:, 0:2048], in_=src), lws, 16)
                    lw_c = lws.v
                    P.wait("vector", lws, lw_c)
                    dst = dwnb[:, ch * 2048:(ch + 1) * 2048]
                    P.emit("vector", lambda e, dst=dst: e.tensor_copy(
                        out=dst, in_=wst2[:, 0:2048]), Vp)
                P.play(blk)
                P = Prog()

            x2b = sb2("x2b", [128, 8 * 512], BF16)
            rbb = sb2("rbb", [128, 512], BF16)
            cbb = sb2("cbb", [128, 512], F32)
            gs = sb2("gs", [128, 512], BF16)
            ups = sb2("ups", [128, 512], BF16)
            hid = sb2("hid", [128, 22 * 512], BF16)
            mo = sb2("mo", [128, 512], BF16)

            v_w = Vp.v
            P.wait("tensor", Vp, v_w)
            NTC = T // 512
            for tc in range(NTC):
                par = tc & 1
                xb = xch[par]
                P.wait("sync", Mm)
                for ch in range(8):
                    dst = xb[:, ch * 512:(ch + 1) * 512]
                    src = xn[ch * 128:(ch + 1) * 128,
                             tc * 512:(tc + 1) * 512]
                    P.emit("sync", lambda e, dst=dst, src=src: e.dma_start(
                        out=dst, in_=src), lx[par], 16)
                lx_t = lx[par].v
                srcr = r2r[0:1, tc * 512:(tc + 1) * 512]
                P.emit("sync", lambda e, srcr=srcr: e.dma_start(
                    out=stg[0:1, 0:512], in_=srcr), lsg, 16)
                srcc = combr[0:1, tc * 512:(tc + 1) * 512]
                P.emit("sync", lambda e, srcc=srcc: e.dma_start(
                    out=stg[0:1, 512:1024], in_=srcc), lsg, 16)
                lsg_t = lsg.v
                P.wait("vector", lsg, lsg_t)
                P.emit("vector", lambda e: e.tensor_copy(
                    out=stgb[:], in_=stg[0:1, 512:1024]), Vp)
                v_st = Vp.v
                P.wait("tensor", Vp, v_st)
                P.emit("tensor", lambda e: e.matmul(
                    paux[:], ones1f[0:1, :], stg[0:1, 0:512], start=True,
                    stop=True), Mm)
                P.wait("tensor", Sp)
                P.emit("tensor", lambda e: e.matmul(
                    paux2[:], alf[0:1, :], stgb[0:1, :], start=True,
                    stop=True), Mm)
                m_bc = Mm.v
                P.wait("vector", Mm, m_bc)
                P.emit("vector", lambda e: e.tensor_copy(out=rbb[:],
                                                         in_=paux[:]))
                P.emit("vector", lambda e: e.tensor_copy(out=cbb[:],
                                                         in_=paux2[:]), Vp)
                v_bc = Vp.v
                P.wait("vector", lx[par], lx_t)
                for ch in range(8):
                    a = x2b[:, ch * 512:(ch + 1) * 512]
                    b2 = xb[:, ch * 512:(ch + 1) * 512]
                    P.emit("vector", lambda e, a=a, b2=b2: e.tensor_tensor(
                        out=a, in0=b2, in1=rbb[:], op=ALU.mult),
                        Vp if ch == 7 else None)
                v_x2 = Vp.v

                for hp in range(22):
                    P.wait("tensor", Vp, v_x2)
                    P.wait("tensor", Sp)
                    for ch in range(8):
                        lhsT = gupb[:, ch * 5632 + hp * 128:
                                    ch * 5632 + hp * 128 + 128]
                        rhs = x2b[:, ch * 512:(ch + 1) * 512]
                        P.emit("tensor", lambda e, lhsT=lhsT, rhs=rhs, ch=ch:
                               e.matmul(paux[:], lhsT, rhs, start=(ch == 0),
                                        stop=(ch == 7)))
                    for ch in range(8):
                        lhsT = gupb[:, ch * 5632 + (22 + hp) * 128:
                                    ch * 5632 + (22 + hp) * 128 + 128]
                        rhs = x2b[:, ch * 512:(ch + 1) * 512]
                        P.emit("tensor", lambda e, lhsT=lhsT, rhs=rhs, ch=ch:
                               e.matmul(paux2[:], lhsT, rhs,
                                        start=(ch == 0), stop=(ch == 7)),
                               Mm if ch == 7 else None)
                    m_gu = Mm.v
                    P.wait("scalar", Mm, m_gu)
                    P.wait("scalar", Vp)
                    P.emit("scalar", lambda e: e.activation(
                        out=gs[:], in_=paux[:], func=AF.Silu), Sp)
                    P.emit("scalar", lambda e: e.activation(
                        out=ups[:], in_=paux2[:], func=AF.Identity), Sp)
                    s_gu = Sp.v
                    P.wait("vector", Sp, s_gu)
                    hsl = hid[:, hp * 512:(hp + 1) * 512]
                    P.emit("vector", lambda e, hsl=hsl: e.tensor_tensor(
                        out=hsl, in0=gs[:], in1=ups[:], op=ALU.mult), Vp)
                    P.wait("tensor", Sp, s_gu)
                v_hid = Vp.v

                for dch in range(8):
                    P.wait("tensor", Vp, v_hid)
                    for hch in range(22):
                        lhsT = dwnb[:, hch * 1024 + dch * 128:
                                    hch * 1024 + dch * 128 + 128]
                        rhs = hid[:, hch * 512:(hch + 1) * 512]
                        P.emit("tensor",
                               lambda e, lhsT=lhsT, rhs=rhs, hch=hch:
                               e.matmul(paux[:], lhsT, rhs,
                                        start=(hch == 0), stop=(hch == 21)),
                               Mm if hch == 21 else None)
                    m_dn = Mm.v
                    P.wait("vector", Mm, m_dn)
                    P.wait("vector", Vp, v_bc)
                    P.wait("vector", smo)
                    P.emit("vector", lambda e: e.tensor_tensor(
                        out=mo[:], in0=paux[:], in1=cbb[:], op=ALU.mult),
                        Vp)
                    v_mo = Vp.v
                    P.wait("tensor", Vp, v_mo)
                    P.wait("sync", Vp, v_mo)
                    dst = mpo[dch * 128:(dch + 1) * 128,
                              tc * 512:(tc + 1) * 512]
                    P.emit("sync", lambda e, dst=dst: e.dma_start(
                        out=dst, in_=mo[:]), smo, 16)

            P.wait("sync", smo)
            P.wait("sync", sty)
            P.wait("gpsimd", ldm)
            if dbg:
                P.wait("sync", sdbg)
            P.play(blk)
    return nc



def build_B0():
    nc = bass.Bass("TRN2", target_bir_lowering=False, debug=False)
    x0 = nc.dram_tensor("x0", [TSL, 1024], F32, kind="ExternalInput").ap()
    yfm = nc.dram_tensor("yfm", [1024, TSL], BF16,
                         kind="ExternalInput").ap()
    wout = nc.dram_tensor("wout", [128, 8 * 1024], F32,
                          kind="ExternalInput").ap()
    mp = nc.dram_tensor("mp", [8 * TSL, 1024], BF16,
                        kind="ExternalInput").ap()
    x1o = nc.dram_tensor("x1o", [TSL, 1024], F32,
                         kind="ExternalOutput").ap()
    xnfm = nc.dram_tensor("xnfm", [1024, TSL], BF16,
                          kind="ExternalOutput").ap()

    es = contextlib.ExitStack()
    with es:
        sb = lambda n, s, d: es.enter_context(nc.sbuf_tensor(n, s, d))
        psr = lambda n, s, d: es.enter_context(nc.psum_tensor(n, s, d))
        sem = lambda n: Sem(es.enter_context(nc.semaphore(n)))
        xt = sb("xt", [128, 1024], F32)
        acc = sb("acc", [128, 1024], F32)
        sq = sb("sq", [128, 1024], F32)
        xn0 = sb("xn0", [128, 1024], F32)
        xnb = sb("xnb", [128, 1024], BF16)
        xnfmt = sb("xnfmt", [128, 128], BF16)
        cols = sb("cols", [128, 8], F32)
        mptb = sb("mptb", [128, 1024], BF16)
        mptf = sb("mptf", [128, 1024], F32)
        ysl = sb("ysl", [128, 8 * TSL], BF16)
        wof = sb("wof", [128, 8 * 1024], F32)
        wob = sb("wob", [128, 8 * 1024], BF16)
        ones_t = sb("ones", [128, 128], BF16)
        idn = sb("idn", [128, 128], BF16)
        ptr = psr("ptr", [128, 128], BF16)
        pj = [psr("pj0", [128, 512], F32), psr("pj1", [128, 512], F32)]
        lda = sem("s_lda")
        lxt = sem("s_lxt")
        lmp = sem("s_lmp")
        sx1 = sem("s_sx1")
        sxn = sem("s_sxn")
        Vp = sem("s_v")
        Sp = sem("s_s")
        Tm = sem("s_t")
        G = sem("s_g")
        blk = es.enter_context(nc.Block())
        P = Prog()
        force_table(P)

        P.emit("sync", lambda e: e.dma_start(out=wof[:], in_=wout), lda, 16)
        for ch in range(8):
            dsty = ysl[:, ch * TSL:(ch + 1) * TSL]
            srcy = yfm[ch * 128:(ch + 1) * 128, :]
            P.emit("sync", lambda e, dsty=dsty, srcy=srcy: e.dma_start(
                out=dsty, in_=srcy), lda, 16)
        lda_all = lda.v
        P.emit("vector", lambda e: e.memset(ones_t[:], 1.0))
        P.wait("vector", lda, lda_all)
        P.emit("vector", lambda e: e.tensor_copy(out=wob[:], in_=wof[:]),
               Vp)
        v_w = Vp.v
        P.wait("gpsimd", Vp, v_w)
        P.emit("gpsimd", lambda e: e.affine_select(
            out=idn[:], in_=ones_t[:], pattern=[[-1, 128]],
            compare_op=ALU.is_equal, fill=0.0, base=0, channel_multiplier=1),
            G)
        g_idn = G.v
        P.wait("tensor", G, g_idn)
        P.wait("tensor", Vp, v_w)

        for i in range(NTT):
            P.wait("sync", Vp)
            P.wait("sync", Sp)
            src = x0[i * 128:(i + 1) * 128, :]
            P.emit("sync", lambda e, src=src: e.dma_start(out=xt[:],
                                                          in_=src), lxt, 16)
            lx_i = lxt.v
            P.wait("scalar", lxt, lx_i)
            P.wait("scalar", Tm)
            P.wait("vector", lxt, lx_i)
            ln_tile(P, Vp, Sp, xt, xn0, sq, cols, want_rstd2=False)
            s_ln0 = Sp.v
            P.wait("tensor", Vp)
            for half in range(2):
                for ch in range(8):
                    lhsT = ysl[:, ch * TSL + i * 128:
                               ch * TSL + (i + 1) * 128]
                    rhs = wob[:, ch * 1024 + half * 512:
                              ch * 1024 + half * 512 + 512]
                    pjh = pj[half]
                    P.emit("tensor",
                           lambda e, lhsT=lhsT, rhs=rhs, ch=ch, pjh=pjh:
                           e.matmul(pjh[:], lhsT, rhs, start=(ch == 0),
                                    stop=(ch == 7)),
                           Tm if ch == 7 else None)
            m_pj = Tm.v
            P.wait("vector", Sp, s_ln0)
            P.wait("vector", sx1)
            P.emit("vector", lambda e: e.tensor_tensor(
                out=acc[:], in0=xt[:], in1=xn0[:], op=ALU.add), Vp)
            P.wait("vector", Tm, m_pj)
            P.emit("vector", lambda e: e.tensor_tensor(
                out=acc[:, 0:512], in0=acc[:, 0:512], in1=pj[0][:],
                op=ALU.add))
            P.emit("vector", lambda e: e.tensor_tensor(
                out=acc[:, 512:1024], in0=acc[:, 512:1024], in1=pj[1][:],
                op=ALU.add), Vp)
            for ee in range(8):
                src = mp[ee * TSL + i * 128:ee * TSL + (i + 1) * 128, :]
                P.wait("sync", Vp)
                P.emit("sync", lambda e, src=src: e.dma_start(out=mptb[:],
                                                              in_=src),
                       lmp, 16)
                lm_i = lmp.v
                P.wait("vector", lmp, lm_i)
                P.emit("vector", lambda e: e.tensor_copy(out=mptf[:],
                                                         in_=mptb[:]), Vp)
                P.wait("vector", Vp)
                P.emit("vector", lambda e: e.tensor_tensor(
                    out=acc[:], in0=acc[:], in1=mptf[:], op=ALU.add), Vp)
            v_acc = Vp.v
            P.wait("sync", Vp, v_acc)
            dst = x1o[i * 128:(i + 1) * 128, :]
            P.emit("sync", lambda e, dst=dst: e.dma_start(out=dst,
                                                          in_=acc[:]),
                   sx1, 16)
            P.wait("scalar", Vp, v_acc)
            P.wait("scalar", Tm)
            ln_tile(P, Vp, Sp, acc, xnb, sq, cols, want_rstd2=False)
            for k in range(8):
                P.wait("tensor", Sp)
                P.wait("tensor", Vp)
                src = xnb[:, 128 * k:128 * (k + 1)]
                P.emit("tensor", lambda e, src=src: e.transpose(
                    ptr[:], src, idn[:]), Tm)
                P.wait("vector", Tm)
                P.wait("vector", sxn)
                P.emit("vector", lambda e: e.tensor_copy(out=xnfmt[:],
                                                         in_=ptr[:]), Vp)
                v_ev = Vp.v
                P.wait("sync", Vp, v_ev)
                dst = xnfm[128 * k:128 * (k + 1), i * 128:(i + 1) * 128]
                P.emit("sync", lambda e, dst=dst: e.dma_start(
                    out=dst, in_=xnfmt[:]), sxn, 16)

        P.wait("sync", sxn)
        P.wait("sync", sx1)
        P.play(blk)
    return nc



def build_A1():
    nc = bass.Bass("TRN2", target_bir_lowering=False, debug=False)
    xn = nc.dram_tensor("xn", [1024, T], BF16, kind="ExternalInput").ap()
    xnf = nc.dram_tensor("xnf", [1024, TSL], BF16, kind="ExternalInput").ap()
    wq = nc.dram_tensor("wq", [128, 8 * 768], F32, kind="ExternalInput").ap()
    qkb = nc.dram_tensor("qkb", [128, 4], F32, kind="ExternalInput").ap()
    vbr = nc.dram_tensor("vbr", [1, 256], F32, kind="ExternalInput").ap()
    wg8 = nc.dram_tensor("wg8", [128, 8 * 3072], I8,
                         kind="ExternalInput").ap()
    wu8 = nc.dram_tensor("wu8", [128, 8 * 3072], I8,
                         kind="ExternalInput").ap()
    wd8 = nc.dram_tensor("wd8", [128, 24 * 1024], I8,
                         kind="ExternalInput").ap()
    yo = nc.dram_tensor("yo", [256, S], F32, kind="ExternalOutput").ap()
    oo = nc.dram_tensor("oo", [1024, TSL], F32, kind="ExternalOutput").ap()

    NKT = S // 128
    NQC = S // 512
    es = contextlib.ExitStack()
    with es:
        sb = lambda n, s, d: es.enter_context(nc.sbuf_tensor(n, s, d))
        psr = lambda n, s, d: es.enter_context(nc.psum_tensor(n, s, d))
        sem = lambda n: Sem(es.enter_context(nc.semaphore(n)))
        xch = [sb("xch0", [128, 8 * 512], BF16),
               sb("xch1", [128, 8 * 512], BF16)]
        ones1 = sb("ones1", [1, 128], BF16)
        ones1f = sb("ones1f", [1, 128], F32)
        ones_t = sb("onest", [128, 128], BF16)
        idn = sb("idn", [128, 128], BF16)
        qkbs = sb("qkbs", [128, 8], F32)
        vbb = sb("vbb", [1, 256], BF16)
        vbf = sb("vbf", [1, 256], F32)
        psA = [psr("psA0", [128, 512], F32), psr("psA1", [128, 512], F32)]
        psB = [psr("psB0", [128, 512], F32), psr("psB1", [128, 512], F32)]
        outA = psr("outA", [128, 512], F32)
        outB = psr("outB", [128, 512], F32)
        paux = psr("paux", [128, 512], F32)
        paux2 = psr("paux2", [128, 512], F32)
        lda = sem("s_lda")
        lx = [sem("s_lx0"), sem("s_lx1")]
        lxf = sem("s_lxf")
        lws = sem("s_lws")
        sty = sem("s_sty")
        soo = sem("s_soo")
        Vp = sem("s_v")
        Sp = sem("s_s")
        Mm = sem("s_m")
        Pv = sem("s_p")
        G = sem("s_g")
        ph = sem("s_ph")
        blk = es.enter_context(nc.Block())

        with contextlib.ExitStack() as es1:
            sb1 = lambda n, s, d: es1.enter_context(nc.sbuf_tensor(n, s, d))
            qha = sb1("qha", [128, S], BF16)
            qhb = sb1("qhb", [128, S], BF16)
            kha = sb1("kha", [128, S], BF16)
            khb = sb1("khb", [128, S], BF16)
            vA = sb1("vA", [128, 65 * 64], BF16)
            vB = sb1("vB", [128, 65 * 64], BF16)
            wqb = sb1("wqb", [128, 8 * 768], BF16)
            eA = sb1("eA", [128, 2 * 512], BF16)
            eB = sb1("eB", [128, 2 * 512], BF16)
            nmx = sb1("nmx", [128, 128], F32)
            nmb = sb1("nmb", [128, 1], BF16)
            rnrow = sb1("rnrow", [1, 1024], F32)
            rnb = sb1("rnb", [64, 512], F32)
            ych = sb1("ych", [128, 512], F32)
            wst = sb1("wst", [128, 6144], F32)
            P = Prog()
            force_table(P)

            P.emit("sync", lambda e: e.dma_start(out=wst[:], in_=wq),
                   lda, 16)
            P.emit("sync", lambda e: e.dma_start(out=qkbs[:, 0:4],
                                                 in_=qkb), lda, 16)
            P.emit("sync", lambda e: e.dma_start(out=vbf[:], in_=vbr),
                   lda, 16)
            lda_all = lda.v
            P.emit("vector", lambda e: e.memset(ones1[:], 1.0))
            P.emit("vector", lambda e: e.memset(ones1f[:], 1.0))
            P.emit("vector", lambda e: e.memset(ones_t[:], 1.0))
            va_ones = vA[:].rearrange("p (t c) -> p t c", c=65)[:, :, 64:65]
            vb_ones = vB[:].rearrange("p (t c) -> p t c", c=65)[:, :, 64:65]
            P.emit("vector", lambda e: e.memset(va_ones, 1.0))
            P.emit("vector", lambda e: e.memset(vb_ones, 1.0))
            P.emit("vector", lambda e: e.memset(kha[64:65, :], 1.0))
            P.emit("vector", lambda e: e.memset(khb[64:65, :], 1.0))
            P.wait("vector", lda, lda_all)
            P.emit("vector", lambda e: e.tensor_copy(out=wqb[:],
                                                     in_=wst[:]))
            P.emit("vector", lambda e: e.tensor_copy(out=vbb[:],
                                                     in_=vbf[:]))
            P.emit("vector", lambda e: e.tensor_single_scalar(
                out=qkbs[:, 4:6], in_=qkbs[:, 0:2], scalar=8.0,
                op=ALU.mult), Vp)
            v_setup = Vp.v
            P.wait("gpsimd", Vp, v_setup)
            P.emit("gpsimd", lambda e: e.affine_select(
                out=idn[:], in_=ones_t[:], pattern=[[-1, 128]],
                compare_op=ALU.is_equal, fill=0.0, base=0,
                channel_multiplier=1), G)
            g_idn = G.v

            for pair in range(2):
                P.wait("tensor", Vp)
                for tc in range(NQC):
                    par = tc & 1
                    xb = xch[par]
                    P.wait("sync", Mm)
                    for ch in range(8):
                        dst = xb[:, ch * 512:(ch + 1) * 512]
                        src = xn[ch * 128:(ch + 1) * 128,
                                 tc * 512:(tc + 1) * 512]
                        P.emit("sync", lambda e, dst=dst, src=src:
                               e.dma_start(out=dst, in_=src), lx[par], 16)
                    lx_t = lx[par].v
                    P.wait("tensor", lx[par], lx_t)
                    for which, base in (("q", 0), ("k", 256)):
                        P.wait("tensor", Sp)
                        for ch in range(8):
                            lhsT = wqb[:, ch * 768 + base + pair * 128:
                                       ch * 768 + base + pair * 128 + 128]
                            rhs = xb[:, ch * 512:(ch + 1) * 512]
                            P.emit("tensor",
                                   lambda e, lhsT=lhsT, rhs=rhs, ch=ch:
                                   e.matmul(paux[:], lhsT, rhs,
                                            start=(ch == 0),
                                            stop=(ch == 7)),
                                   Mm if ch == 7 else None)
                        m_qk = Mm.v
                        P.wait("scalar", Mm, m_qk)
                        if which == "q":
                            dA = qha[0:64, tc * 512:(tc + 1) * 512]
                            dB = qhb[0:64, tc * 512:(tc + 1) * 512]
                            bidx, scl = 4, 8.0
                        else:
                            dA = kha[0:64, tc * 512:(tc + 1) * 512]
                            dB = khb[0:64, tc * 512:(tc + 1) * 512]
                            bidx, scl = 2, 1.0
                        bA = qkbs[0:64, bidx + pair:bidx + pair + 1]
                        bB = qkbs[64:128, bidx + pair:bidx + pair + 1]
                        P.emit("scalar", lambda e, dA=dA, bA=bA, scl=scl:
                               e.activation(out=dA, in_=paux[0:64, :],
                                            func=AF.Identity, scale=scl,
                                            bias=bA), Sp)
                        P.emit("scalar", lambda e, dB=dB, bB=bB, scl=scl:
                               e.activation(out=dB, in_=paux[64:128, :],
                                            func=AF.Identity, scale=scl,
                                            bias=bB), Sp)
                    for sub in range(4):
                        kt = tc * 4 + sub
                        P.wait("tensor", Vp)
                        for ch in range(8):
                            lhsT = xb[:, ch * 512 + sub * 128:
                                      ch * 512 + sub * 128 + 128]
                            rhs = wqb[:, ch * 768 + 512 + pair * 128:
                                      ch * 768 + 512 + pair * 128 + 128]
                            P.emit("tensor",
                                   lambda e, lhsT=lhsT, rhs=rhs, ch=ch:
                                   e.matmul(paux2[:, 0:128], lhsT, rhs,
                                            start=(ch == 0), stop=False))
                        vbsl = vbb[0:1, pair * 128:pair * 128 + 128]
                        P.emit("tensor", lambda e, vbsl=vbsl: e.matmul(
                            paux2[:, 0:128], ones1[0:1, :], vbsl,
                            start=False, stop=True), Mm)
                        m_v = Mm.v
                        dstA = vA[:, kt * 65:kt * 65 + 64]
                        dstB = vB[:, kt * 65:kt * 65 + 64]
                        P.wait("vector", Mm, m_v)
                        P.emit("vector", lambda e, dstA=dstA: e.tensor_copy(
                            out=dstA, in_=paux2[:, 0:64]))
                        P.emit("vector", lambda e, dstB=dstB: e.tensor_copy(
                            out=dstB, in_=paux2[:, 64:128]), Vp)

                P.wait("tensor", Sp)
                for qt in range(NKT):
                    qslA = qha[0:64, qt * 128:(qt + 1) * 128]
                    qslB = qhb[0:64, qt * 128:(qt + 1) * 128]
                    ca = nmx[:, 0:1]
                    cb = nmx[:, 1:2]
                    for kt2 in range(NQC):
                        par = kt2 & 1
                        P.wait("tensor", Vp)
                        kslA = kha[0:64, kt2 * 512:(kt2 + 1) * 512]
                        kslB = khb[0:64, kt2 * 512:(kt2 + 1) * 512]
                        pA, pB = psA[par], psB[par]
                        P.emit("tensor",
                               lambda e, qslA=qslA, kslA=kslA, pA=pA:
                               e.matmul(pA[:], qslA, kslA, start=True,
                                        stop=True))
                        P.emit("tensor",
                               lambda e, qslB=qslB, kslB=kslB, pB=pB:
                               e.matmul(pB[:], qslB, kslB, start=True,
                                        stop=True), Mm)
                        m_s = Mm.v
                        P.wait("vector", Mm, m_s)
                        if kt2 == 0:
                            P.emit("vector", lambda e, ca=ca, pA=pA:
                                   e.reduce_max(out=ca, in_=pA[:],
                                                axis=AX.X, negate=True), Vp)
                            P.wait("vector", Vp)
                            P.emit("vector", lambda e, cb=cb, pB=pB:
                                   e.reduce_max(out=cb, in_=pB[:],
                                                axis=AX.X, negate=True), Vp)
                            P.wait("vector", Vp)
                        else:
                            P.emit("vector", lambda e, pA=pA:
                                   e.reduce_max(out=nmx[:, 2:3], in_=pA[:],
                                                axis=AX.X, negate=True), Vp)
                            P.wait("vector", Vp)
                            P.emit("vector", lambda e, ca=ca:
                                   e.tensor_tensor(out=ca, in0=ca,
                                                   in1=nmx[:, 2:3],
                                                   op=ALU.min), Vp)
                            P.wait("vector", Vp)
                            P.emit("vector", lambda e, pB=pB:
                                   e.reduce_max(out=nmx[:, 2:3], in_=pB[:],
                                                axis=AX.X, negate=True), Vp)
                            P.wait("vector", Vp)
                            P.emit("vector", lambda e, cb=cb:
                                   e.tensor_tensor(out=cb, in0=cb,
                                                   in1=nmx[:, 2:3],
                                                   op=ALU.min), Vp)
                            P.wait("vector", Vp)
                    for hh in range(2):
                        col = nmx[:, hh:hh + 1]
                        P.wait("vector", Mm)
                        P.emit("vector", lambda e, col=col: e.tensor_copy(
                            out=nmb[:], in_=col), Vp)
                        v_nm = Vp.v
                        P.wait("tensor", Vp, v_nm)
                        P.wait("tensor", G, g_idn)
                        P.wait("tensor", Sp)
                        P.emit("tensor", lambda e: e.matmul(
                            paux[0:1, 0:128], nmb[:], idn[:], start=True,
                            stop=True), Mm)
                        m_t = Mm.v
                        qh = qha if hh == 0 else qhb
                        dst = qh[64:65, qt * 128:(qt + 1) * 128]
                        P.wait("scalar", Mm, m_t)
                        P.emit("scalar", lambda e, dst=dst: e.activation(
                            out=dst, in_=paux[0:1, 0:128],
                            func=AF.Identity), Sp)

                P.wait("tensor", Sp)
                for qc in range(NQC):
                    sp_qc0 = Sp.v
                    mm_qc0 = Mm.v
                    pv_qc0 = Pv.v
                    q0 = qc * 512
                    for kt in range(NKT):
                        par = kt & 1
                        P.wait("tensor", Sp,
                               sp_qc0 + (2 * (kt - 1) if kt >= 2 else 0))
                        pA, pB = psA[par], psB[par]
                        kslA = kha[0:65, kt * 128:(kt + 1) * 128]
                        kslB = khb[0:65, kt * 128:(kt + 1) * 128]
                        qslA = qha[0:65, q0:q0 + 512]
                        qslB = qhb[0:65, q0:q0 + 512]
                        P.emit("tensor",
                               lambda e, kslA=kslA, qslA=qslA, pA=pA:
                               e.matmul(pA[:], kslA, qslA, start=True,
                                        stop=True))
                        P.emit("tensor",
                               lambda e, kslB=kslB, qslB=qslB, pB=pB:
                               e.matmul(pB[:], kslB, qslB, start=True,
                                        stop=True), Mm)
                        if kt >= 1:
                            ktp = kt - 1
                            parp = ktp & 1
                            P.wait("tensor", Sp, sp_qc0 + 2 * (ktp + 1))
                            vslA = vA[:, ktp * 65:(ktp + 1) * 65]
                            vslB = vB[:, ktp * 65:(ktp + 1) * 65]
                            eslA = eA[:, parp * 512:parp * 512 + 512]
                            eslB = eB[:, parp * 512:parp * 512 + 512]
                            P.emit("tensor",
                                   lambda e, vslA=vslA, eslA=eslA, ktp=ktp:
                                   e.matmul(outA[0:65, :], vslA, eslA,
                                            start=(ktp == 0), stop=False))
                            P.emit("tensor",
                                   lambda e, vslB=vslB, eslB=eslB, ktp=ktp:
                                   e.matmul(outB[0:65, :], vslB, eslB,
                                            start=(ktp == 0), stop=False),
                                   Pv)
                        P.wait("scalar", Mm, mm_qc0 + (kt + 1))
                        P.wait("scalar", Pv,
                               pv_qc0 + (kt - 1 if kt >= 2 else 0))
                        for hh in range(2):
                            pX = pA if hh == 0 else pB
                            eX = eA if hh == 0 else eB
                            edst = eX[:, par * 512:par * 512 + 512]
                            P.emit("scalar", lambda e, edst=edst, pX=pX:
                                   e.activation(out=edst, in_=pX[:],
                                                func=AF.Exp), Sp)
                    ktp = NKT - 1
                    parp = ktp & 1
                    P.wait("tensor", Sp, sp_qc0 + 2 * (ktp + 1))
                    vslA = vA[:, ktp * 65:(ktp + 1) * 65]
                    vslB = vB[:, ktp * 65:(ktp + 1) * 65]
                    eslA = eA[:, parp * 512:parp * 512 + 512]
                    eslB = eB[:, parp * 512:parp * 512 + 512]
                    P.emit("tensor", lambda e, vslA=vslA, eslA=eslA:
                           e.matmul(outA[0:65, :], vslA, eslA, start=False,
                                    stop=True))
                    P.emit("tensor", lambda e, vslB=vslB, eslB=eslB:
                           e.matmul(outB[0:65, :], vslB, eslB, start=False,
                                    stop=True), Pv)
                    pv_done = Pv.v
                    P.wait("vector", Pv, pv_done)
                    P.emit("vector", lambda e: e.reciprocal(
                        out=rnrow[0:1, 0:512], in_=outA[64:65, :]), Vp)
                    P.wait("vector", Vp)
                    P.emit("vector", lambda e: e.reciprocal(
                        out=rnrow[0:1, 512:1024], in_=outB[64:65, :]), Vp)
                    P.wait("vector", Vp)
                    for hh in range(2):
                        P.wait("tensor", Vp)
                        rsl = rnrow[0:1, hh * 512:hh * 512 + 512]
                        P.emit("tensor", lambda e, rsl=rsl: e.matmul(
                            paux[0:64, :], ones1f[0:1, 0:64], rsl,
                            start=True, stop=True), Mm)
                        m_bc = Mm.v
                        P.wait("vector", Mm, m_bc)
                        P.emit("vector", lambda e: e.tensor_copy(
                            out=rnb[:], in_=paux[0:64, :]), Vp)
                        P.wait("vector", Vp)
                        P.wait("vector", sty)
                        oX = outA if hh == 0 else outB
                        ysl = ych[hh * 64:hh * 64 + 64, :]
                        P.emit("vector", lambda e, oX=oX, ysl=ysl:
                               e.tensor_tensor(out=ysl, in0=oX[0:64, :],
                                               in1=rnb[:], op=ALU.mult), Vp)
                    v_y = Vp.v
                    P.wait("scalar", Vp, v_y)
                    dst = yo[pair * 128:(pair + 1) * 128, q0:q0 + 512]
                    P.emit("scalar", lambda e, dst=dst: e.dma_start(
                        out=dst, in_=ych[:]), sty, 16)
                    P.wait("tensor", Vp, v_y)

            P.play(blk)
            PB = Prog()
            for eng in ENGS:
                PB.emit(eng, lambda e: e.nop(), ph)
            for eng in ENGS:
                PB.wait(eng, ph, 5)
            PB.play(blk)

        with contextlib.ExitStack() as es2:
            sb2 = lambda n, s, d: es2.enter_context(nc.sbuf_tensor(n, s, d))
            wgb = sb2("wgb", [128, 8 * 3072], BF16)
            wub = sb2("wub", [128, 8 * 3072], BF16)
            wdb = sb2("wdb", [128, 24 * 1024], BF16)
            wsti = sb2("wsti", [128, 6144], I8)
            gs = sb2("gs", [128, 512], BF16)
            ups = sb2("ups", [128, 512], BF16)
            hid = sb2("hid", [128, 24 * 512], BF16)
            mo = sb2("mo", [128, 512], F32)
            P = Prog()
            force_table(P, 18)
            for (wsrc, wdst2) in ((wg8, wgb), (wu8, wub), (wd8, wdb)):
                for r0 in range(0, 24576, 6144):
                    srcw = wsrc[:, r0:r0 + 6144]
                    P.wait("sync", Vp)
                    P.emit("sync", lambda e, srcw=srcw: e.dma_start(
                        out=wsti[:], in_=srcw), lws, 16)
                    lw_c = lws.v
                    P.wait("vector", lws, lw_c)
                    dstw = wdst2[:, r0:r0 + 6144]
                    P.emit("vector", lambda e, dstw=dstw: e.tensor_copy(
                        out=dstw, in_=wsti[:]), Vp)
            v_w = Vp.v
            P.wait("tensor", Vp, v_w)

            for tcf in range(4):
                par = tcf & 1
                xb = xch[par]
                P.wait("sync", Mm)
                for ch in range(8):
                    dst = xb[:, ch * 512:(ch + 1) * 512]
                    src = xnf[ch * 128:(ch + 1) * 128,
                              tcf * 512:(tcf + 1) * 512]
                    P.emit("sync", lambda e, dst=dst, src=src: e.dma_start(
                        out=dst, in_=src), lx[par], 16)
                lx_t = lx[par].v
                P.wait("tensor", lx[par], lx_t)
                for hp in range(24):
                    P.wait("tensor", Sp)
                    for ch in range(8):
                        lhsT = wgb[:, ch * 3072 + hp * 128:
                                   ch * 3072 + hp * 128 + 128]
                        rhs = xb[:, ch * 512:(ch + 1) * 512]
                        P.emit("tensor", lambda e, lhsT=lhsT, rhs=rhs, ch=ch:
                               e.matmul(paux[:], lhsT, rhs, start=(ch == 0),
                                        stop=(ch == 7)))
                    for ch in range(8):
                        lhsT = wub[:, ch * 3072 + hp * 128:
                                   ch * 3072 + hp * 128 + 128]
                        rhs = xb[:, ch * 512:(ch + 1) * 512]
                        P.emit("tensor", lambda e, lhsT=lhsT, rhs=rhs, ch=ch:
                               e.matmul(paux2[:], lhsT, rhs,
                                        start=(ch == 0), stop=(ch == 7)),
                               Mm if ch == 7 else None)
                    m_gu = Mm.v
                    P.wait("scalar", Mm, m_gu)
                    P.wait("scalar", Vp)
                    P.emit("scalar", lambda e: e.activation(
                        out=gs[:], in_=paux[:], func=AF.Silu,
                        scale=1.0 / 127.0), Sp)
                    P.emit("scalar", lambda e: e.activation(
                        out=ups[:], in_=paux2[:], func=AF.Identity), Sp)
                    s_gu = Sp.v
                    P.wait("vector", Sp, s_gu)
                    hsl = hid[:, hp * 512:(hp + 1) * 512]
                    P.emit("vector", lambda e, hsl=hsl: e.tensor_tensor(
                        out=hsl, in0=gs[:], in1=ups[:], op=ALU.mult), Vp)
                    P.wait("tensor", Sp, s_gu)
                v_hid = Vp.v
                for dch in range(8):
                    P.wait("tensor", Vp, v_hid)
                    P.wait("tensor", Sp)
                    for hch in range(24):
                        lhsT = wdb[:, hch * 1024 + dch * 128:
                                   hch * 1024 + dch * 128 + 128]
                        rhs = hid[:, hch * 512:(hch + 1) * 512]
                        P.emit("tensor",
                               lambda e, lhsT=lhsT, rhs=rhs, hch=hch:
                               e.matmul(paux[:], lhsT, rhs,
                                        start=(hch == 0), stop=(hch == 23)),
                               Mm if hch == 23 else None)
                    m_dn = Mm.v
                    P.wait("scalar", Mm, m_dn)
                    P.wait("scalar", soo)
                    P.emit("scalar", lambda e: e.activation(
                        out=mo[:], in_=paux[:], func=AF.Identity,
                        scale=1.0 / 16129.0), Sp)
                    s_mo = Sp.v
                    P.wait("scalar", Sp, s_mo)
                    dst = oo[dch * 128:(dch + 1) * 128,
                             tcf * 512:(tcf + 1) * 512]
                    P.emit("scalar", lambda e, dst=dst: e.dma_start(
                        out=dst, in_=mo[:]), soo, 16)
            P.wait("sync", soo)
            P.wait("sync", sty)
            P.play(blk)
    return nc



def build_B_add(n_f32, n_bf16, ln_out=True):
    nc = bass.Bass("TRN2", target_bir_lowering=False, debug=False)
    x0 = nc.dram_tensor("x0", [TSL, 1024], F32, kind="ExternalInput").ap()
    tf = (nc.dram_tensor("tf", [n_f32 * TSL, 1024], F32,
                         kind="ExternalInput").ap() if n_f32 else None)
    tb = (nc.dram_tensor("tb", [n_bf16 * TSL, 1024], BF16,
                         kind="ExternalInput").ap() if n_bf16 else None)
    x1o = nc.dram_tensor("x1o", [TSL, 1024], F32, kind="ExternalOutput").ap()
    if ln_out:
        xnfm = nc.dram_tensor("xnfm", [1024, TSL], BF16,
                              kind="ExternalOutput").ap()

    es = contextlib.ExitStack()
    with es:
        sb = lambda n, s, d: es.enter_context(nc.sbuf_tensor(n, s, d))
        psr = lambda n, s, d: es.enter_context(nc.psum_tensor(n, s, d))
        sem = lambda n: Sem(es.enter_context(nc.semaphore(n)))
        xt = sb("xt", [128, 1024], F32)
        acc = sb("acc", [128, 1024], F32)
        sq = sb("sq", [128, 1024], F32)
        xnb = sb("xnb", [128, 1024], BF16)
        xnfmt = sb("xnfmt", [128, 128], BF16)
        cols = sb("cols", [128, 8], F32)
        tft = sb("tft", [128, 1024], F32)
        tbt = sb("tbt", [128, 1024], BF16)
        tbf = sb("tbf", [128, 1024], F32)
        ones_t = sb("ones", [128, 128], BF16)
        idn = sb("idn", [128, 128], BF16)
        ptr = psr("ptr", [128, 128], BF16)
        lxt = sem("s_lxt")
        ltf = sem("s_ltf")
        ltb = sem("s_ltb")
        sx1 = sem("s_sx1")
        sxn = sem("s_sxn")
        Vp = sem("s_v")
        Sp = sem("s_s")
        Tm = sem("s_t")
        G = sem("s_g")
        blk = es.enter_context(nc.Block())
        P = Prog()
        force_table(P)
        P.emit("vector", lambda e: e.memset(ones_t[:], 1.0), Vp)
        v0 = Vp.v
        P.wait("gpsimd", Vp, v0)
        P.emit("gpsimd", lambda e: e.affine_select(
            out=idn[:], in_=ones_t[:], pattern=[[-1, 128]],
            compare_op=ALU.is_equal, fill=0.0, base=0, channel_multiplier=1),
            G)
        g_idn = G.v
        P.wait("tensor", G, g_idn)

        for i in range(NTT):
            P.wait("sync", Vp)
            src = x0[i * 128:(i + 1) * 128, :]
            P.emit("sync", lambda e, src=src: e.dma_start(out=xt[:],
                                                          in_=src), lxt, 16)
            lx_i = lxt.v
            P.wait("vector", lxt, lx_i)
            P.wait("vector", sx1)
            P.emit("vector", lambda e: e.tensor_copy(out=acc[:],
                                                     in_=xt[:]), Vp)
            for ee in range(n_f32):
                src = tf[ee * TSL + i * 128:ee * TSL + (i + 1) * 128, :]
                P.wait("sync", Vp)
                P.emit("sync", lambda e, src=src: e.dma_start(out=tft[:],
                                                              in_=src),
                       ltf, 16)
                lt_i = ltf.v
                P.wait("vector", ltf, lt_i)
                P.emit("vector", lambda e: e.tensor_tensor(
                    out=acc[:], in0=acc[:], in1=tft[:], op=ALU.add), Vp)
            for ee in range(n_bf16):
                src = tb[ee * TSL + i * 128:ee * TSL + (i + 1) * 128, :]
                P.wait("sync", Vp)
                P.emit("sync", lambda e, src=src: e.dma_start(out=tbt[:],
                                                              in_=src),
                       ltb, 16)
                lt_i = ltb.v
                P.wait("vector", ltb, lt_i)
                P.emit("vector", lambda e: e.tensor_copy(out=tbf[:],
                                                         in_=tbt[:]), Vp)
                P.wait("vector", Vp)
                P.emit("vector", lambda e: e.tensor_tensor(
                    out=acc[:], in0=acc[:], in1=tbf[:], op=ALU.add), Vp)
            v_acc = Vp.v
            P.wait("sync", Vp, v_acc)
            dst = x1o[i * 128:(i + 1) * 128, :]
            P.emit("sync", lambda e, dst=dst: e.dma_start(out=dst,
                                                          in_=acc[:]),
                   sx1, 16)
            if ln_out:
                P.wait("scalar", Vp, v_acc)
                P.wait("scalar", Tm)
                ln_tile(P, Vp, Sp, acc, xnb, sq, cols, want_rstd2=False)
                for k in range(8):
                    P.wait("tensor", Sp)
                    P.wait("tensor", Vp)
                    src = xnb[:, 128 * k:128 * (k + 1)]
                    P.emit("tensor", lambda e, src=src: e.transpose(
                        ptr[:], src, idn[:]), Tm)
                    P.wait("vector", Tm)
                    P.wait("vector", sxn)
                    P.emit("vector", lambda e: e.tensor_copy(
                        out=xnfmt[:], in_=ptr[:]), Vp)
                    v_ev = Vp.v
                    P.wait("sync", Vp, v_ev)
                    dst = xnfm[128 * k:128 * (k + 1),
                               i * 128:(i + 1) * 128]
                    P.emit("sync", lambda e, dst=dst: e.dma_start(
                        out=dst, in_=xnfmt[:]), sxn, 16)
            else:
                pass

        P.wait("sync", sx1)
        if ln_out:
            P.wait("sync", sxn)
        P.play(blk)
    return nc



def bisect_mask(P, Vp, acts, scr, cms, k, iters=40):
    def V(fn):
        P.emit("vector", fn, Vp)
        P.wait("vector", Vp)

    V(lambda e: e.reduce_max(out=scr[:, 1:2], in_=acts, axis=AX.X))
    V(lambda e: e.tensor_single_scalar(
        out=scr[:, 1:2], in_=scr[:, 1:2], scalar=1.0, op=ALU.add))
    V(lambda e: e.reduce_max(out=scr[:, 0:1], in_=acts, axis=AX.X,
                             negate=True))
    V(lambda e: e.tensor_single_scalar(
        out=scr[:, 0:1], in_=scr[:, 0:1], scalar=-1.0, op=ALU.mult))
    V(lambda e: e.tensor_single_scalar(
        out=scr[:, 0:1], in_=scr[:, 0:1], scalar=-1.0, op=ALU.add))
    for _ in range(iters):
        V(lambda e: e.tensor_tensor(out=scr[:, 2:3], in0=scr[:, 0:1],
                                    in1=scr[:, 1:2], op=ALU.add))
        V(lambda e: e.tensor_single_scalar(
            out=scr[:, 2:3], in_=scr[:, 2:3], scalar=0.5, op=ALU.mult))
        V(lambda e: e.tensor_single_scalar(
            out=cms, in_=acts, scalar=scr[:, 2:3], op=ALU.is_ge))
        V(lambda e: e.reduce_sum(out=scr[:, 3:4], in_=cms, axis=AX.X))
        V(lambda e: e.tensor_single_scalar(
            out=scr[:, 4:5], in_=scr[:, 3:4], scalar=float(k) - 0.5,
            op=ALU.is_ge))
        V(lambda e: e.tensor_tensor(out=scr[:, 5:6], in0=scr[:, 2:3],
                                    in1=scr[:, 0:1], op=ALU.subtract))
        V(lambda e: e.tensor_tensor(out=scr[:, 5:6], in0=scr[:, 5:6],
                                    in1=scr[:, 4:5], op=ALU.mult))
        V(lambda e: e.tensor_tensor(out=scr[:, 0:1], in0=scr[:, 0:1],
                                    in1=scr[:, 5:6], op=ALU.add))
        V(lambda e: e.tensor_tensor(out=scr[:, 5:6], in0=scr[:, 2:3],
                                    in1=scr[:, 1:2], op=ALU.subtract))
        V(lambda e: e.tensor_tensor(out=scr[:, 6:7], in0=scr[:, 5:6],
                                    in1=scr[:, 4:5], op=ALU.mult))
        V(lambda e: e.tensor_tensor(out=scr[:, 1:2], in0=scr[:, 1:2],
                                    in1=scr[:, 5:6], op=ALU.add))
        V(lambda e: e.tensor_tensor(out=scr[:, 1:2], in0=scr[:, 1:2],
                                    in1=scr[:, 6:7], op=ALU.subtract))
    V(lambda e: e.tensor_single_scalar(
        out=cms, in_=acts, scalar=scr[:, 0:1], op=ALU.is_ge))



def build_A2():
    nc = bass.Bass("TRN2", target_bir_lowering=False, debug=False)
    xn = nc.dram_tensor("xn", [1024, T], BF16, kind="ExternalInput").ap()
    wq = nc.dram_tensor("wq", [128, 8 * 768], F32, kind="ExternalInput").ap()
    qkb = nc.dram_tensor("qkb", [128, 6], F32, kind="ExternalInput").ap()
    vbr = nc.dram_tensor("vbr", [1, 256], F32, kind="ExternalInput").ap()
    gw1c = nc.dram_tensor("gw1c", [128, 8 * 512], F32,
                          kind="ExternalInput").ap()
    gw2c = nc.dram_tensor("gw2c", [128, 4 * 16], F32,
                          kind="ExternalInput").ap()
    gb1 = nc.dram_tensor("gb1", [128, 4], F32, kind="ExternalInput").ap()
    gb2 = nc.dram_tensor("gb2", [16, 1], F32, kind="ExternalInput").ap()
    lam = nc.dram_tensor("lam", [128, 128], F32, kind="ExternalInput").ap()
    gsel = nc.dram_tensor("gsel", [128, 4 * 4096], F32,
                          kind="ExternalInput").ap()
    sel2 = nc.dram_tensor("sel2", [128, 32 * 64], F32,
                          kind="ExternalInput").ap()
    wsta = nc.dram_tensor("wsta", [64, 4 * 64], F32,
                          kind="ExternalInput").ap()
    hsel = nc.dram_tensor("hsel", [16, 4 * 64], F32,
                          kind="ExternalInput").ap()
    cmsk = nc.dram_tensor("cmsk", [128, 4 * 512], F32,
                          kind="ExternalInput").ap()
    ipc = nc.dram_tensor("ipc", [128, 8 * 1024], F32,
                         kind="ExternalInput").ap()
    yo = nc.dram_tensor("yo", [256, S], BF16, kind="ExternalOutput").ap()
    iao = nc.dram_tensor("iao", [128, 16], F32, kind="ExternalOutput").ap()

    NQC = S // 512
    es = contextlib.ExitStack()
    with es:
        sb = lambda n, s, d: es.enter_context(nc.sbuf_tensor(n, s, d))
        psr = lambda n, s, d: es.enter_context(nc.psum_tensor(n, s, d))
        sem = lambda n: Sem(es.enter_context(nc.semaphore(n)))
        xch = [sb("xch0", [128, 8 * 512], BF16),
               sb("xch1", [128, 8 * 512], BF16)]
        ones1 = sb("ones1", [1, 128], BF16)
        ones1f = sb("ones1f", [1, 128], F32)
        alf = sb("alf", [16, S], BF16)
        rcols = sb("rcols", [128, 128], F32)
        qkbs = sb("qkbs", [128, 6], F32)
        vbf = sb("vbf", [1, 256], F32)
        vbb = sb("vbb", [1, 256], BF16)
        gb2t = sb("gb2t", [16, 1], F32)
        psA = [psr("psA0", [128, 512], F32), psr("psA1", [128, 512], F32)]
        psB = [psr("psB0", [128, 512], F32), psr("psB1", [128, 512], F32)]
        outA = psr("outA", [128, 512], F32)
        outB = psr("outB", [128, 512], F32)
        paux = psr("paux", [128, 512], F32)
        paux2 = psr("paux2", [128, 512], F32)
        lda = sem("s_lda")
        lx = [sem("s_lx0"), sem("s_lx1")]
        lgs = sem("s_lgs")
        sty = sem("s_sty")
        sia = sem("s_sia")
        Vp = sem("s_v")
        Sp = sem("s_s")
        Mm = sem("s_m")
        Pv = sem("s_p")
        ph = sem("s_ph")
        blk = es.enter_context(nc.Block())

        with contextlib.ExitStack() as es0:
            sb0 = lambda n, s, d: es0.enter_context(nc.sbuf_tensor(n, s, d))
            wstg = sb0("wstg", [128, 8192], F32)
            gw1b = sb0("gw1b", [128, 8 * 512], BF16)
            gw2b = sb0("gw2b", [128, 4 * 16], BF16)
            ipb = sb0("ipb", [128, 8 * 1024], BF16)
            gch = sb0("gch", [128, 4 * 512], BF16)
            alog = sb0("alog", [16, S], F32)
            iacc = sb0("iacc", [128, 16], F32)
            iact = sb0("iact", [128, 16], F32)
            gb1t = sb0("gb1t", [128, 4], F32)
            gdump = sb0("gdump", [128, 512], F32)
            P = Prog()
            force_table(P, 10)
            P.emit("sync", lambda e: e.dma_start(out=wstg[:, 0:4096],
                                                 in_=gw1c), lda, 16)
            P.emit("sync", lambda e: e.dma_start(out=wstg[:, 4096:4160],
                                                 in_=gw2c), lda, 16)
            P.emit("sync", lambda e: e.dma_start(out=gb1t[:], in_=gb1),
                   lda, 16)
            P.emit("sync", lambda e: e.dma_start(out=gb2t[:], in_=gb2),
                   lda, 16)
            P.emit("sync", lambda e: e.dma_start(out=rcols[:], in_=lam),
                   lda, 16)
            P.emit("sync", lambda e: e.dma_start(out=qkbs[:], in_=qkb),
                   lda, 16)
            P.emit("sync", lambda e: e.dma_start(out=vbf[:], in_=vbr),
                   lda, 16)
            lda_1 = lda.v
            P.emit("vector", lambda e: e.memset(ones1[:], 1.0))
            P.emit("vector", lambda e: e.memset(ones1f[:], 1.0))
            P.emit("vector", lambda e: e.memset(iacc[:], 0.0))
            P.wait("vector", lda, lda_1)
            P.emit("vector", lambda e: e.tensor_copy(
                out=gw1b[:], in_=wstg[:, 0:4096]))
            P.emit("vector", lambda e: e.tensor_copy(
                out=gw2b[:], in_=wstg[:, 4096:4160]))
            P.emit("vector", lambda e: e.tensor_copy(
                out=vbb[:], in_=vbf[:]), Vp)
            v_g1 = Vp.v
            for half in range(2):
                src = ipc[:, half * 4096:(half + 1) * 4096]
                P.wait("sync", Vp, v_g1 if half == 0 else Vp.v)
                P.emit("sync", lambda e, src=src: e.dma_start(
                    out=wstg[:, 0:4096], in_=src), lda, 16)
                ld_c = lda.v
                P.wait("vector", lda, ld_c)
                dst = ipb[:, half * 4096:(half + 1) * 4096]
                P.emit("vector", lambda e, dst=dst: e.tensor_copy(
                    out=dst, in_=wstg[:, 0:4096]), Vp)
            v_ip = Vp.v

            for tc in range(T // 512):
                par = tc & 1
                xb = xch[par]
                P.wait("sync", Mm)
                for ch in range(8):
                    dst = xb[:, ch * 512:(ch + 1) * 512]
                    src = xn[ch * 128:(ch + 1) * 128,
                             tc * 512:(tc + 1) * 512]
                    P.emit("sync", lambda e, dst=dst, src=src: e.dma_start(
                        out=dst, in_=src), lx[par], 16)
                lx_t = lx[par].v
                P.wait("tensor", lx[par], lx_t)
                P.wait("tensor", Vp, v_ip)
                half = 0 if tc < 16 else 1
                for pt in range(8):
                    P.wait("tensor", Sp)
                    for ch in range(8):
                        lhsT = ipb[:, ch * 1024 + pt * 128:
                                   ch * 1024 + pt * 128 + 128]
                        rhs = xb[:, ch * 512:(ch + 1) * 512]
                        P.emit("tensor", lambda e, lhsT=lhsT, rhs=rhs, ch=ch:
                               e.matmul(paux[:], lhsT, rhs, start=(ch == 0),
                                        stop=(ch == 7)),
                               Mm if ch == 7 else None)
                    m_ia = Mm.v
                    P.wait("scalar", Mm, m_ia)
                    P.wait("scalar", Vp)
                    acol = iact[:, pt * 2 + half:pt * 2 + half + 1]
                    P.emit("scalar", lambda e, acol=acol: e.activation(
                        out=gdump[:], in_=paux[:], func=AF.Gelu,
                        accum_out=acol), Sp)
                    s_ia = Sp.v
                    col = iacc[:, pt * 2 + half:pt * 2 + half + 1]
                    P.wait("vector", Sp, s_ia)
                    P.emit("vector", lambda e, col=col, acol=acol:
                           e.tensor_tensor(out=col, in0=col, in1=acol,
                                           op=ALU.add), Vp)
                if tc < 16:
                    for gt in range(4):
                        P.wait("tensor", Sp)
                        for ch in range(8):
                            lhsT = gw1b[:, ch * 512 + gt * 128:
                                        ch * 512 + gt * 128 + 128]
                            rhs = xb[:, ch * 512:(ch + 1) * 512]
                            P.emit("tensor",
                                   lambda e, lhsT=lhsT, rhs=rhs, ch=ch:
                                   e.matmul(paux2[:], lhsT, rhs,
                                            start=(ch == 0), stop=(ch == 7)),
                                   Mm if ch == 7 else None)
                        m_g = Mm.v
                        P.wait("scalar", Mm, m_g)
                        dstg = gch[:, gt * 512:(gt + 1) * 512]
                        bcol = gb1t[:, gt:gt + 1]
                        P.emit("scalar", lambda e, dstg=dstg, bcol=bcol:
                               e.activation(out=dstg, in_=paux2[:],
                                            func=AF.Gelu, bias=bcol), Sp)
                    s_g = Sp.v
                    P.wait("tensor", Sp, s_g)
                    for gt in range(4):
                        lhsT = gw2b[:, gt * 16:gt * 16 + 16]
                        rhs = gch[:, gt * 512:(gt + 1) * 512]
                        P.emit("tensor", lambda e, lhsT=lhsT, rhs=rhs, gt=gt:
                               e.matmul(paux[0:16, :], lhsT, rhs,
                                        start=(gt == 0), stop=(gt == 3)),
                               Mm if gt == 3 else None)
                    m_al = Mm.v
                    P.wait("scalar", Mm, m_al)
                    dsta = alog[0:16, tc * 512:(tc + 1) * 512]
                    P.emit("scalar", lambda e, dsta=dsta: e.activation(
                        out=dsta, in_=paux[0:16, :], func=AF.Identity,
                        bias=gb2t[:, 0:1]), Sp)
            v_iac = Vp.v
            P.wait("sync", Vp, v_iac)
            P.emit("sync", lambda e: e.dma_start(out=iao, in_=iacc[:]),
                   sia, 16)
            P.wait("scalar", Sp)
            P.wait("scalar", Vp)
            force_table(P, 2)
            P.emit("scalar", lambda e: e.activation(
                out=alf[0:16, 0:4096], in_=alog[0:16, 0:4096],
                func=AF.Sigmoid), Sp)
            P.emit("scalar", lambda e: e.activation(
                out=alf[0:16, 4096:8192], in_=alog[0:16, 4096:8192],
                func=AF.Sigmoid), Sp)
            P.emit("scalar", lambda e: e.activation(
                out=rcols[:], in_=rcols[:], func=AF.Sigmoid, scale=-1.0),
                Sp)
            s_sig = Sp.v
            for eng in ENGS:
                P.wait(eng, Sp, s_sig)
            P.play(blk)
            PB = Prog()
            for eng in ENGS:
                PB.emit(eng, lambda e: e.nop(), ph)
            bar1 = ph.v
            for eng in ENGS:
                PB.wait(eng, ph, bar1)
            PB.play(blk)

        with contextlib.ExitStack() as es1:
            sb1 = lambda n, s, d: es1.enter_context(nc.sbuf_tensor(n, s, d))
            qf = sb1("qf", [128, S], BF16)
            kf = sb1("kf", [128, S], BF16)
            vfm = sb1("vfm", [128, S], BF16)
            vA = sb1("vA", [128, 65 * 64], BF16)
            vB = sb1("vB", [128, 65 * 64], BF16)
            wqb = sb1("wqb", [128, 8 * 768], BF16)
            ysoft = sb1("ysoft", [128, S], BF16)
            eA = sb1("eA", [128, 2 * 512], BF16)
            eB = sb1("eB", [128, 2 * 512], BF16)
            sf32 = sb1("sf32", [128, 512], F32)
            rnrow = sb1("rnrow", [1, 1024], F32)
            rnb = sb1("rnb", [64, 512], F32)
            cmt = sb1("cmt", [128, 4 * 512], F32)
            gselb = sb1("gselb", [128, 4096], BF16)
            sel2b = sb1("sel2b", [128, 2048], BF16)
            kdup = sb1("kdup", [128, 512], BF16)
            qdup = sb1("qdup", [128, 512], BF16)
            sqb2 = sb1("sqb2", [128, 2 * 512], BF16)
            wstf = sb1("wstf", [64, 256], F32)
            wstab = sb1("wstab", [64, 256], BF16)
            hself = sb1("hself", [16, 256], F32)
            hselb = sb1("hselb", [16, 256], BF16)
            wst = sb1("wst", [128, 6144], F32)
            ones5f = sb1("ones5f", [128, 512], F32)
            K2c = sb1("K2c", [128, 512], F32)
            Q2c = sb1("Q2c", [128, 512], F32)
            rbt = sb1("rbt", [128, 512], F32)
            uf = sb1("uf", [128, 512], F32)
            ut = sb1("ut", [128, 512], F32)
            sc = sb1("scn", [128, 512], F32)
            carr = sb1("carr", [128, 32], F32)
            alb = sb1("alb", [64, 512], F32)
            ychs = sb1("ychs", [64, 512], BF16)
            ysf0 = ut[0:64, :]
            dtt = uf[0:64, :]
            P = Prog()
            force_table(P, NLE_SET)

            P.emit("sync", lambda e: e.dma_start(out=wst[:], in_=wq),
                   lda, 16)
            P.emit("sync", lambda e: e.dma_start(out=cmt[:], in_=cmsk),
                   lda, 16)
            P.emit("sync", lambda e: e.dma_start(out=wstf[:], in_=wsta),
                   lda, 16)
            P.emit("sync", lambda e: e.dma_start(out=hself[:], in_=hsel),
                   lda, 16)
            lda_2 = lda.v
            P.emit("vector", lambda e: e.memset(ones5f[:], 1.0))
            va_ones = vA[:].rearrange("p (t c) -> p t c", c=65)[:, :, 64:65]
            vb_ones = vB[:].rearrange("p (t c) -> p t c", c=65)[:, :, 64:65]
            P.emit("vector", lambda e: e.memset(va_ones, 1.0))
            P.emit("vector", lambda e: e.memset(vb_ones, 1.0))
            P.wait("vector", lda, lda_2)
            P.emit("vector", lambda e: e.tensor_copy(out=wqb[:],
                                                     in_=wst[:]))
            P.emit("vector", lambda e: e.tensor_copy(out=wstab[:],
                                                     in_=wstf[:]))
            P.emit("vector", lambda e: e.tensor_copy(out=hselb[:],
                                                     in_=hself[:]), Vp)
            v_wqc = Vp.v
            P.wait("sync", Vp, v_wqc)
            P.emit("sync", lambda e: e.dma_start(out=wst[:, 0:2048],
                                                 in_=sel2), lda, 16)
            lda_s2 = lda.v
            P.wait("vector", lda, lda_s2)
            P.emit("vector", lambda e: e.tensor_copy(
                out=sel2b[:], in_=wst[:, 0:2048]), Vp)
            v_cv = Vp.v

            for pair in range(2):
                P.wait("tensor", Vp)
                P.wait("scalar", Vp)
                for tc in range(NQC):
                    par = tc & 1
                    xb = xch[par]
                    P.wait("sync", Mm)
                    for ch in range(8):
                        dst = xb[:, ch * 512:(ch + 1) * 512]
                        src = xn[ch * 128:(ch + 1) * 128,
                                 tc * 512:(tc + 1) * 512]
                        P.emit("sync", lambda e, dst=dst, src=src:
                               e.dma_start(out=dst, in_=src), lx[par], 16)
                    lx_t = lx[par].v
                    P.wait("tensor", lx[par], lx_t)
                    for wi, base in (("q", 0), ("k", 256), ("v", 512)):
                        P.wait("tensor", Sp)
                        for ch in range(8):
                            lhsT = wqb[:, ch * 768 + base + pair * 128:
                                       ch * 768 + base + pair * 128 + 128]
                            rhs = xb[:, ch * 512:(ch + 1) * 512]
                            P.emit("tensor",
                                   lambda e, lhsT=lhsT, rhs=rhs, ch=ch:
                                   e.matmul(paux[:], lhsT, rhs,
                                            start=(ch == 0), stop=(ch == 7)),
                                   Mm if ch == 7 else None)
                        m_qk = Mm.v
                        dstf = {"q": qf, "k": kf, "v": vfm}[wi][
                            :, tc * 512:(tc + 1) * 512]
                        bci = {"q": 0, "k": 2, "v": 4}[wi] + pair
                        bcol = qkbs[:, bci:bci + 1]
                        P.wait("scalar", Mm, m_qk)
                        P.emit("scalar", lambda e, dstf=dstf, bcol=bcol:
                               e.activation(out=dstf, in_=paux[:],
                                            func=AF.Identity, bias=bcol),
                               Sp)
                    for sub in range(4):
                        kt = tc * 4 + sub
                        P.wait("tensor", Vp)
                        for ch in range(8):
                            lhsT = xb[:, ch * 512 + sub * 128:
                                      ch * 512 + sub * 128 + 128]
                            rhs = wqb[:, ch * 768 + 512 + pair * 128:
                                      ch * 768 + 512 + pair * 128 + 128]
                            P.emit("tensor",
                                   lambda e, lhsT=lhsT, rhs=rhs, ch=ch:
                                   e.matmul(paux2[:, 0:128], lhsT, rhs,
                                            start=(ch == 0), stop=False))
                        vbsl = vbb[0:1, pair * 128:pair * 128 + 128]
                        P.emit("tensor", lambda e, vbsl=vbsl: e.matmul(
                            paux2[:, 0:128], ones1[0:1, :], vbsl,
                            start=False, stop=True), Mm)
                        m_v = Mm.v
                        dstA = vA[:, kt * 65:kt * 65 + 64]
                        dstB = vB[:, kt * 65:kt * 65 + 64]
                        P.wait("vector", Mm, m_v)
                        P.emit("vector", lambda e, dstA=dstA: e.tensor_copy(
                            out=dstA, in_=paux2[:, 0:64]))
                        P.emit("vector", lambda e, dstB=dstB: e.tensor_copy(
                            out=dstB, in_=paux2[:, 64:128]), Vp)

                P.wait("tensor", Vp)
                P.wait("vector", Sp)
                P.wait("scalar", Vp)
                for qc in range(NQC):
                    sp_qc0 = Sp.v
                    mm_qc0 = Mm.v
                    pv_qc0 = Pv.v
                    q0 = qc * 512
                    nktq = 4 * qc + 4
                    qslA = qf[0:64, q0:q0 + 512]
                    qslB = qf[64:128, q0:q0 + 512]
                    for kt in range(nktq):
                        par = kt & 1
                        P.wait("tensor", Sp,
                               sp_qc0 + (2 * (kt - 1) if kt >= 2 else 0))
                        kslA = kf[0:64, kt * 128:(kt + 1) * 128]
                        kslB = kf[64:128, kt * 128:(kt + 1) * 128]
                        pA, pB = psA[par], psB[par]
                        P.emit("tensor",
                               lambda e, kslA=kslA, qslA=qslA, pA=pA:
                               e.matmul(pA[:], kslA, qslA, start=True,
                                        stop=True, tile_position=(0, 0)))
                        P.emit("tensor",
                               lambda e, kslB=kslB, qslB=qslB, pB=pB:
                               e.matmul(pB[:], kslB, qslB, start=True,
                                        stop=True, tile_position=(64, 0)),
                               Mm)
                        if kt >= 1:
                            ktp = kt - 1
                            parp = ktp & 1
                            P.wait("tensor", Sp, sp_qc0 + 2 * (ktp + 1))
                            vslA = vA[:, ktp * 65:(ktp + 1) * 65]
                            vslB = vB[:, ktp * 65:(ktp + 1) * 65]
                            eslA = eA[:, parp * 512:parp * 512 + 512]
                            eslB = eB[:, parp * 512:parp * 512 + 512]
                            P.emit("tensor",
                                   lambda e, vslA=vslA, eslA=eslA, ktp=ktp:
                                   e.matmul(outA[0:65, :], vslA, eslA,
                                            start=(ktp == 0), stop=False))
                            P.emit("tensor",
                                   lambda e, vslB=vslB, eslB=eslB, ktp=ktp:
                                   e.matmul(outB[0:65, :], vslB, eslB,
                                            start=(ktp == 0), stop=False),
                                   Pv)
                        mi = kt - 4 * qc
                        P.wait("scalar", Mm, mm_qc0 + (kt + 1))
                        P.wait("scalar", Pv,
                               pv_qc0 + (kt - 1 if kt >= 2 else 0))
                        for hh in range(2):
                            pX = pA if hh == 0 else pB
                            eX = eA if hh == 0 else eB
                            edst = eX[:, par * 512:par * 512 + 512]
                            if mi >= 0:
                                msl = cmt[:, mi * 512:(mi + 1) * 512]
                                P.wait("vector", Mm, mm_qc0 + (kt + 1))
                                P.wait("vector", Sp)
                                P.emit("vector",
                                       lambda e, pX=pX, msl=msl:
                                       e.tensor_tensor(out=sf32[:],
                                                       in0=pX[:], in1=msl,
                                                       op=ALU.add), Vp)
                                P.wait("scalar", Vp)
                                P.emit("scalar", lambda e, edst=edst:
                                       e.activation(out=edst, in_=sf32[:],
                                                    func=AF.Exp,
                                                    scale=0.125), Sp)
                            else:
                                P.emit("scalar",
                                       lambda e, edst=edst, pX=pX:
                                       e.activation(out=edst, in_=pX[:],
                                                    func=AF.Exp,
                                                    scale=0.125), Sp)
                    ktp = nktq - 1
                    parp = ktp & 1
                    P.wait("tensor", Sp, sp_qc0 + 2 * (ktp + 1))
                    vslA = vA[:, ktp * 65:(ktp + 1) * 65]
                    vslB = vB[:, ktp * 65:(ktp + 1) * 65]
                    eslA = eA[:, parp * 512:parp * 512 + 512]
                    eslB = eB[:, parp * 512:parp * 512 + 512]
                    P.emit("tensor", lambda e, vslA=vslA, eslA=eslA, ktp=ktp:
                           e.matmul(outA[0:65, :], vslA, eslA,
                                    start=(ktp == 0), stop=True))
                    P.emit("tensor", lambda e, vslB=vslB, eslB=eslB, ktp=ktp:
                           e.matmul(outB[0:65, :], vslB, eslB,
                                    start=(ktp == 0), stop=True), Pv)
                    pv_done = Pv.v

                    P.wait("vector", Pv, pv_done)
                    P.emit("vector", lambda e: e.reciprocal(
                        out=rnrow[0:1, 0:512], in_=outA[64:65, :]), Vp)
                    P.wait("vector", Vp)
                    P.emit("vector", lambda e: e.reciprocal(
                        out=rnrow[0:1, 512:1024], in_=outB[64:65, :]), Vp)
                    P.wait("vector", Vp)
                    s_rn = Vp.v
                    for hh in range(2):
                        P.wait("tensor", Vp, s_rn if hh == 0 else Vp.v)
                        rsl = rnrow[0:1, hh * 512:hh * 512 + 512]
                        P.emit("tensor", lambda e, rsl=rsl: e.matmul(
                            paux[0:64, :], ones1f[0:1, 0:64], rsl,
                            start=True, stop=True), Mm)
                        m_bc = Mm.v
                        P.wait("vector", Mm, m_bc)
                        P.emit("vector", lambda e: e.tensor_copy(
                            out=rnb[:], in_=paux[0:64, :]), Vp)
                        P.wait("vector", Vp)
                        oX = outA if hh == 0 else outB
                        ysl = ysoft[hh * 64:hh * 64 + 64, q0:q0 + 512]
                        P.emit("vector", lambda e, oX=oX, ysl=ysl:
                               e.tensor_tensor(out=ysl, in0=oX[0:64, :],
                                               in1=rnb[:], op=ALU.mult), Vp)
                    v_y = Vp.v
                    P.wait("tensor", Vp, v_y)

                for hh in range(2):
                    hsl_ = pair * 2 + hh
                    P.wait("sync", Vp)
                    src = gsel[:, hsl_ * 4096:(hsl_ + 1) * 4096]
                    P.emit("sync", lambda e, src=src: e.dma_start(
                        out=wst[:, 0:4096], in_=src), lgs, 16)
                    lg_h = lgs.v
                    P.wait("vector", lgs, lg_h)
                    P.emit("vector", lambda e: e.tensor_copy(
                        out=gselb[:], in_=wst[:, 0:4096]), Vp)
                    v_gs = Vp.v
                    P.wait("tensor", Vp, v_gs)
                    P.wait("scalar", Vp, v_gs)

                    def V(fn):
                        P.emit("vector", fn, Vp)
                        P.wait("vector", Vp)

                    for ch in range(16):
                        c0 = ch * 512
                        ksl = kf[hh * 64:hh * 64 + 64, c0:c0 + 512]
                        qsl = qf[hh * 64:hh * 64 + 64, c0:c0 + 512]
                        P.wait("gpsimd", Vp)
                        P.wait("gpsimd", Sp)
                        P.wait("gpsimd", Mm)
                        P.emit("gpsimd", lambda e, ksl=ksl: e.dma_start(
                            out=kdup[0:64, :], in_=ksl), lgs, 16)
                        P.emit("gpsimd", lambda e, ksl=ksl: e.dma_start(
                            out=kdup[64:128, :], in_=ksl), lgs, 16)
                        P.emit("gpsimd", lambda e, qsl=qsl: e.dma_start(
                            out=qdup[0:64, :], in_=qsl), lgs, 16)
                        P.emit("gpsimd", lambda e, qsl=qsl: e.dma_start(
                            out=qdup[64:128, :], in_=qsl), lgs, 16)
                        lk_c = lgs.v
                        P.wait("scalar", lgs, lk_c)
                        P.wait("scalar", Vp)
                        P.emit("scalar", lambda e: e.activation(
                            out=K2c[:], in_=kdup[:], func=AF.Identity,
                            scale=0.125))
                        P.emit("scalar", lambda e: e.activation(
                            out=Q2c[:], in_=qdup[:], func=AF.Identity),
                            Sp)
                        s_kq = Sp.v
                        P.wait("tensor", Sp, s_kq)
                        hsl_b = hselb[:, hsl_ * 64:(hsl_ + 1) * 64]
                        asl = alf[0:16, c0:c0 + 512]
                        P.emit("tensor", lambda e, hsl_b=hsl_b, asl=asl:
                               e.matmul(paux2[0:64, :], hsl_b, asl,
                                        start=True, stop=True), Mm)
                        m_alb = Mm.v
                        P.wait("scalar", Mm, m_alb)
                        P.emit("scalar", lambda e: e.activation(
                            out=alb[:], in_=paux2[0:64, :],
                            func=AF.Identity), Sp)
                        s_alb = Sp.v
                        P.wait("vector", Sp, s_alb)

                        m_ve = {}
                        s_uf = {}
                        v_ut = {}
                        v_sq = {}
                        vsl = vfm[hh * 64:hh * 64 + 64, c0:c0 + 512]
                        for t in range(32):
                            pX = [paux, outB][t & 1]
                            gsl = gselb[hh * 64:hh * 64 + 64,
                                        t * 128:(t + 1) * 128]
                            if t >= 2:
                                P.wait("tensor", Sp, s_uf[t - 2])
                            elif ch == 0 and t == 0:
                                P.wait("tensor", Sp)
                            P.emit("tensor",
                                   lambda e, gsl=gsl, vsl=vsl, pX=pX:
                                   e.matmul(pX[:], gsl, vsl, start=True,
                                            stop=True), Mm)
                            m_ve[t] = Mm.v
                            if t >= 1:
                                ssl = sel2b[:, (t - 1) * 64:t * 64]
                                sqs = sqb2[:, (1 - (t & 1)) * 512:
                                           (1 - (t & 1)) * 512 + 512]
                                P.wait("tensor", Vp, v_sq[t - 1])
                                P.emit("tensor",
                                       lambda e, ssl=ssl, sqs=sqs, t=t:
                                       e.matmul(outA[0:64, :], ssl, sqs,
                                                start=(t == 1),
                                                stop=False))
                            P.wait("scalar", Mm, m_ve[t])
                            if t >= 1:
                                P.wait("scalar", Vp, v_ut[t - 1])
                            P.emit("scalar", lambda e, pX=pX: e.activation(
                                out=uf[:], in_=pX[:], func=AF.Identity),
                                Sp)
                            s_uf[t] = Sp.v
                            rcol = rcols[:, hsl_ * 32 + t:hsl_ * 32 + t + 1]
                            P.emit("vector", lambda e, rcol=rcol:
                                   e.tensor_single_scalar(
                                       out=rbt[:], in_=ones5f[:],
                                       scalar=rcol, op=ALU.mult))
                            P.wait("vector", Sp, s_uf[t])
                            P.emit("vector", lambda e: e.tensor_tensor(
                                out=ut[:], in0=uf[:], in1=K2c[:],
                                op=ALU.mult), Vp)
                            v_ut[t] = Vp.v
                            P.wait("vector", Vp, v_ut[t])
                            ini = 0.0 if ch == 0 else carr[:, t:t + 1]
                            P.emit("vector",
                                   lambda e, ini=ini: e.tensor_tensor_scan(
                                       out=sc[:], data0=rbt[:], data1=ut[:],
                                       initial=ini, op0=ALU.mult,
                                       op1=ALU.add))
                            P.emit("vector", lambda e, t=t: e.tensor_copy(
                                out=carr[:, t:t + 1], in_=sc[:, 511:512]))
                            sqd = sqb2[:, (t & 1) * 512:(t & 1) * 512 + 512]
                            P.emit("vector", lambda e, sqd=sqd:
                                   e.tensor_tensor(out=sqd, in0=sc[:],
                                                   in1=Q2c[:],
                                                   op=ALU.mult), Vp)
                            v_sq[t] = Vp.v
                        ssl = sel2b[:, 31 * 64:32 * 64]
                        sqs = sqb2[:, 512:1024]
                        P.wait("tensor", Vp, v_sq[31])
                        P.emit("tensor", lambda e, ssl=ssl, sqs=sqs:
                               e.matmul(outA[0:64, :], ssl, sqs,
                                        start=False, stop=False))
                        wsl = wstab[:, hsl_ * 64:(hsl_ + 1) * 64]
                        P.emit("tensor", lambda e, wsl=wsl:
                               e.matmul(outA[0:64, :], wsl, qdup[0:64, :],
                                        start=False, stop=True), Mm)
                        m_yst = Mm.v
                        yssl = ysoft[hh * 64:hh * 64 + 64, c0:c0 + 512]
                        P.wait("gpsimd", Sp)
                        P.emit("gpsimd", lambda e, yssl=yssl: e.dma_start(
                            out=kdup[0:64, :], in_=yssl), lgs, 16)
                        lk_y = lgs.v
                        P.wait("scalar", lgs, lk_y)
                        P.wait("scalar", Vp)
                        P.emit("scalar", lambda e: e.activation(
                            out=ysf0, in_=kdup[0:64, :],
                            func=AF.Identity), Sp)
                        s_ys = Sp.v
                        P.wait("vector", Sp, s_ys)
                        P.wait("vector", Mm, m_yst)
                        V(lambda e: e.tensor_tensor(
                            out=dtt, in0=ysf0, in1=outA[0:64, :],
                            op=ALU.subtract))
                        V(lambda e: e.tensor_tensor(
                            out=dtt, in0=dtt, in1=alb[:],
                            op=ALU.mult))
                        P.wait("vector", sty)
                        V(lambda e: e.tensor_tensor(
                            out=ychs[:], in0=dtt, in1=outA[0:64, :],
                            op=ALU.add))
                        v_yc = Vp.v
                        P.wait("scalar", Vp, v_yc)
                        dst = yo[hsl_ * 64:hsl_ * 64 + 64, c0:c0 + 512]
                        P.emit("scalar", lambda e, dst=dst: e.dma_start(
                            out=dst, in_=ychs[:]), sty, 16)
                        P.wait("tensor", Vp, v_yc)

            P.wait("sync", sty)
            P.wait("sync", sia)
            P.play(blk)
    return nc



def build_B2a():
    nc = bass.Bass("TRN2", target_bir_lowering=False, debug=False)
    x0 = nc.dram_tensor("x0", [TSL, 1024], F32, kind="ExternalInput").ap()
    yfm = nc.dram_tensor("yfm", [1024, TSL], BF16,
                         kind="ExternalInput").ap()
    wout = nc.dram_tensor("wout", [128, 8 * 1024], F32,
                          kind="ExternalInput").ap()
    bo = nc.dram_tensor("bo", [1, 1024], F32, kind="ExternalInput").ap()
    iaf = nc.dram_tensor("iaf", [2, 8192], F32, kind="ExternalInput").ap()
    piw = nc.dram_tensor("piw", [128, 8 * 4096], F32,
                         kind="ExternalInput").ap()
    x3p = nc.dram_tensor("x3p", [TSL, 1024], F32,
                         kind="ExternalOutput").ap()
    pactp = nc.dram_tensor("pactp", [128, 64], F32,
                           kind="ExternalOutput").ap()

    es = contextlib.ExitStack()
    with es:
        sb = lambda n, s, d: es.enter_context(nc.sbuf_tensor(n, s, d))
        psr = lambda n, s, d: es.enter_context(nc.psum_tensor(n, s, d))
        sem = lambda n: Sem(es.enter_context(nc.semaphore(n)))
        ones1 = sb("ones1", [1, 128], BF16)
        ones_t = sb("ones", [128, 128], BF16)
        idn = sb("idn", [128, 128], BF16)
        pj = [psr("pj0", [128, 512], F32), psr("pj1", [128, 512], F32)]
        ptr2 = psr("ptr2", [128, 2], F32)
        pp = psr("pp", [128, 2], F32)
        lda = sem("s_lda")
        lxt = sem("s_lxt")
        lws = sem("s_lws")
        sx1 = sem("s_sx1")
        sfin = sem("s_sfin")
        Vp = sem("s_v")
        Sp = sem("s_s")
        Tm = sem("s_t")
        G = sem("s_g")
        ph = sem("s_ph")
        blk = es.enter_context(nc.Block())

        with contextlib.ExitStack() as es1:
            sb1 = lambda n, s, d: es1.enter_context(nc.sbuf_tensor(n, s, d))
            xt = sb1("xt", [128, 1024], F32)
            acc = sb1("acc", [128, 1024], F32)
            ysl = sb1("ysl", [128, 8 * TSL], BF16)
            wof = sb1("wof", [128, 8 * 1024], F32)
            wob = sb1("wob", [128, 8 * 1024], BF16)
            bot = sb1("bot", [1, 1024], F32)
            bob = sb1("bob", [1, 1024], BF16)
            P = Prog()
            force_table(P, NLE_SET)
            P.emit("sync", lambda e: e.dma_start(out=wof[:], in_=wout),
                   lda, 16)
            P.emit("sync", lambda e: e.dma_start(out=bot[:], in_=bo),
                   lda, 16)
            for ch in range(8):
                dsty = ysl[:, ch * TSL:(ch + 1) * TSL]
                srcy = yfm[ch * 128:(ch + 1) * 128, :]
                P.emit("sync", lambda e, dsty=dsty, srcy=srcy: e.dma_start(
                    out=dsty, in_=srcy), lda, 16)
            lda_all = lda.v
            P.emit("vector", lambda e: e.memset(ones_t[:], 1.0))
            P.emit("vector", lambda e: e.memset(ones1[:], 1.0))
            P.wait("vector", lda, lda_all)
            P.emit("vector", lambda e: e.tensor_copy(out=wob[:],
                                                     in_=wof[:]))
            P.emit("vector", lambda e: e.tensor_copy(out=bob[:],
                                                     in_=bot[:]), Vp)
            v_w = Vp.v
            P.wait("gpsimd", Vp, v_w)
            P.emit("gpsimd", lambda e: e.affine_select(
                out=idn[:], in_=ones_t[:], pattern=[[-1, 128]],
                compare_op=ALU.is_equal, fill=0.0, base=0,
                channel_multiplier=1), G)
            g_idn = G.v
            P.wait("tensor", G, g_idn)
            P.wait("tensor", Vp, v_w)

            for i in range(NTT):
                P.wait("sync", Vp)
                src = x0[i * 128:(i + 1) * 128, :]
                P.emit("sync", lambda e, src=src: e.dma_start(
                    out=xt[:], in_=src), lxt, 16)
                lx_i = lxt.v
                P.wait("tensor", Vp)
                for half in range(2):
                    for ch in range(8):
                        lhsT = ysl[:, ch * TSL + i * 128:
                                   ch * TSL + (i + 1) * 128]
                        rhs = wob[:, ch * 1024 + half * 512:
                                  ch * 1024 + half * 512 + 512]
                        pjh = pj[half]
                        P.emit("tensor",
                               lambda e, lhsT=lhsT, rhs=rhs, ch=ch, pjh=pjh:
                               e.matmul(pjh[:], lhsT, rhs, start=(ch == 0),
                                        stop=False))
                    bsl = bob[0:1, half * 512:half * 512 + 512]
                    pjh = pj[half]
                    P.emit("tensor", lambda e, bsl=bsl, pjh=pjh: e.matmul(
                        pjh[:], ones1[0:1, :], bsl, start=False, stop=True),
                        Tm if half == 1 else None)
                m_pj = Tm.v
                P.wait("vector", lxt, lx_i)
                P.wait("vector", Tm, m_pj)
                P.wait("vector", sx1)
                P.emit("vector", lambda e: e.tensor_tensor(
                    out=acc[:, 0:512], in0=xt[:, 0:512], in1=pj[0][:],
                    op=ALU.add))
                P.emit("vector", lambda e: e.tensor_tensor(
                    out=acc[:, 512:1024], in0=xt[:, 512:1024], in1=pj[1][:],
                    op=ALU.add), Vp)
                v_acc = Vp.v
                P.wait("sync", Vp, v_acc)
                dst = x3p[i * 128:(i + 1) * 128, :]
                P.emit("sync", lambda e, dst=dst: e.dma_start(
                    out=dst, in_=acc[:]), sx1, 16)
            P.wait("sync", sx1)
            for eng in ENGS:
                P.wait(eng, sx1)
            P.play(blk)
            PB = Prog()
            for eng in ENGS:
                PB.emit(eng, lambda e: e.nop(), ph)
            bar1 = ph.v
            for eng in ENGS:
                PB.wait(eng, ph, bar1)
            PB.play(blk)

        with contextlib.ExitStack() as es2:
            sb2 = lambda n, s, d: es2.enter_context(nc.sbuf_tensor(n, s, d))
            iat = sb2("iat", [2, 8192], F32)
            cms = sb2("cms", [2, 8192], F32)
            irb = sb2("irb", [2, 8192], BF16)
            scr = sb2("scr", [2, 8], F32)
            irT = sb2("irT", [128, 16], BF16)
            pacts = sb2("pacts", [128, 64], F32)
            wst = sb2("wst", [128, 8192], F32)
            piwb = sb2("piwb", [128, 8 * 4096], BF16)
            P = Prog()
            P.emit("sync", lambda e: e.dma_start(out=iat[:], in_=iaf),
                   lda, 16)
            lda_ia = lda.v
            for rd in range(4):
                src = piw[:, rd * 8192:(rd + 1) * 8192]
                P.wait("sync", Vp)
                P.emit("sync", lambda e, src=src: e.dma_start(
                    out=wst[:], in_=src), lws, 16)
                lw_c = lws.v
                P.wait("vector", lws, lw_c)
                dst = piwb[:, rd * 8192:(rd + 1) * 8192]
                P.emit("vector", lambda e, dst=dst: e.tensor_copy(
                    out=dst, in_=wst[:]), Vp)
            v_piw = Vp.v

            def V(fn):
                P.emit("vector", fn, Vp)
                P.wait("vector", Vp)

            P.wait("vector", lda, lda_ia)
            V(lambda e: e.tensor_single_scalar(
                out=iat[:], in_=iat[:], scalar=1.0 / 8192, op=ALU.mult))
            bisect_mask(P, Vp, iat[:], scr, cms[:], 1024)
            V(lambda e: e.tensor_tensor(out=iat[:], in0=iat[:], in1=cms[:],
                                        op=ALU.mult))
            V(lambda e: e.tensor_copy(out=irb[:], in_=iat[:]))
            v_ir = Vp.v
            P.wait("tensor", Vp, v_ir)
            for chv in range(8):
                lhsT = irb[0:2, chv * 128:(chv + 1) * 128]
                P.emit("tensor", lambda e, lhsT=lhsT: e.matmul(
                    ptr2[:], lhsT, idn[0:2, 0:2], start=True, stop=True),
                    Tm)
                m_tr = Tm.v
                P.wait("vector", Tm, m_tr)
                dst = irT[:, chv * 2:(chv + 1) * 2]
                P.emit("vector", lambda e, dst=dst: e.tensor_copy(
                    out=dst, in_=ptr2[:]), Vp)
                P.wait("tensor", Vp)
            for pt in range(32):
                P.wait("tensor", Vp)
                for chv in range(8):
                    lhsT = piwb[:, chv * 4096 + pt * 128:
                                chv * 4096 + pt * 128 + 128]
                    rhs = irT[:, chv * 2:(chv + 1) * 2]
                    P.emit("tensor", lambda e, lhsT=lhsT, rhs=rhs, chv=chv:
                           e.matmul(pp[:], lhsT, rhs, start=(chv == 0),
                                    stop=(chv == 7)),
                           Tm if chv == 7 else None)
                m_pp = Tm.v
                P.wait("vector", Tm, m_pp)
                dst = pacts[:, pt * 2:(pt + 1) * 2]
                P.emit("vector", lambda e, dst=dst: e.tensor_copy(
                    out=dst, in_=pp[:]), Vp)
            v_pa = Vp.v
            P.wait("sync", Vp, v_pa)
            P.emit("sync", lambda e: e.dma_start(out=pactp, in_=pacts[:]),
                   sfin, 16)
            P.wait("sync", sfin)
            P.play(blk)
    return nc



def build_B2b():
    nc = bass.Bass("TRN2", target_bir_lowering=False, debug=False)
    pac8 = nc.dram_tensor("pac8", [2, 8 * 4096], F32,
                          kind="ExternalInput").ap()
    pv = nc.dram_tensor("pv", [128, 4 * 512], F32,
                        kind="ExternalInput").ap()
    aggp = nc.dram_tensor("aggp", [128, 8], F32, kind="ExternalOutput").ap()

    es = contextlib.ExitStack()
    with es:
        sb = lambda n, s, d: es.enter_context(nc.sbuf_tensor(n, s, d))
        psr = lambda n, s, d: es.enter_context(nc.psum_tensor(n, s, d))
        sem = lambda n: Sem(es.enter_context(nc.semaphore(n)))
        stg2 = sb("stg2", [2, 4096], F32)
        pacc = sb("pacc", [2, 4096], F32)
        pe = sb("pe", [2, 4096], F32)
        cms2 = sb("cms2", [2, 4096], F32)
        web = sb("web", [2, 4096], BF16)
        scr = sb("scr", [2, 8], F32)
        nmc = sb("nmc", [2, 4], F32)
        wtT = sb("wtT", [128, 8], BF16)
        aggs = sb("aggs", [128, 8], F32)
        pvf = sb("pvf", [128, 2048], F32)
        pvb = sb("pvb", [128, 2048], BF16)
        ones_t = sb("ones", [128, 128], BF16)
        idn = sb("idn", [128, 128], BF16)
        ptr2 = psr("ptr2", [128, 2], F32)
        pp = psr("pp", [128, 2], F32)
        lda = sem("s_lda")
        sfin = sem("s_sfin")
        Vp = sem("s_v")
        Sp = sem("s_s")
        Tm = sem("s_t")
        G = sem("s_g")
        blk = es.enter_context(nc.Block())
        P = Prog()
        force_table(P, 10)

        P.emit("sync", lambda e: e.dma_start(out=pvf[:], in_=pv), lda, 16)
        for j in range(8):
            src = pac8[0:2, j * 4096:(j + 1) * 4096]
            P.wait("sync", Vp)
            P.emit("sync", lambda e, src=src: e.dma_start(
                out=stg2[:], in_=src), lda, 16)
            ld_j = lda.v
            P.wait("vector", lda, ld_j)
            if j == 0:
                P.emit("vector", lambda e: e.tensor_copy(
                    out=pacc[:], in_=stg2[:]), Vp)
            else:
                P.emit("vector", lambda e: e.tensor_tensor(
                    out=pacc[:], in0=pacc[:], in1=stg2[:], op=ALU.add), Vp)
        v_sum = Vp.v
        P.emit("vector", lambda e: e.memset(ones_t[:], 1.0), Vp)
        v_one = Vp.v
        P.wait("gpsimd", Vp, v_one)
        P.emit("gpsimd", lambda e: e.affine_select(
            out=idn[:], in_=ones_t[:], pattern=[[-1, 128]],
            compare_op=ALU.is_equal, fill=0.0, base=0,
            channel_multiplier=1), G)
        g_idn = G.v
        P.emit("vector", lambda e: e.tensor_copy(out=pvb[:], in_=pvf[:]),
               Vp)
        P.wait("scalar", Vp, v_sum)
        P.emit("scalar", lambda e: e.activation(
            out=pacc[:], in_=pacc[:], func=AF.Gelu), Sp)
        s_gl = Sp.v
        P.wait("scalar", Sp, s_gl)
        force_table(P, NLE_SET)

        def V(fn):
            P.emit("vector", fn, Vp)
            P.wait("vector", Vp)

        P.wait("vector", Sp, s_gl)
        bisect_mask(P, Vp, pacc[:], scr, cms2[:], 512)
        V(lambda e: e.reduce_max(out=nmc[:, 0:1], in_=pacc[:], axis=AX.X,
                                 negate=True))
        P.wait("scalar", Vp)
        P.emit("scalar", lambda e: e.activation(
            out=pe[:], in_=pacc[:], func=AF.Exp, bias=nmc[:, 0:1]), Sp)
        s_e = Sp.v
        P.wait("vector", Sp, s_e)
        V(lambda e: e.tensor_tensor(out=pe[:], in0=pe[:], in1=cms2[:],
                                    op=ALU.mult))
        V(lambda e: e.reduce_sum(out=nmc[:, 1:2], in_=pe[:], axis=AX.X))
        V(lambda e: e.reciprocal(out=nmc[:, 2:3], in_=nmc[:, 1:2]))
        V(lambda e: e.tensor_single_scalar(
            out=pe[:], in_=pe[:], scalar=nmc[:, 2:3], op=ALU.mult))
        V(lambda e: e.tensor_copy(out=web[:], in_=pe[:]))
        v_wt = Vp.v
        P.wait("tensor", Vp, v_wt)
        P.wait("tensor", G, g_idn)
        for chv in range(4):
            lhsT = web[0:2, chv * 128:(chv + 1) * 128]
            P.emit("tensor", lambda e, lhsT=lhsT: e.matmul(
                ptr2[:], lhsT, idn[0:2, 0:2], start=True, stop=True), Tm)
            m_tr = Tm.v
            P.wait("vector", Tm, m_tr)
            dst = wtT[:, chv * 2:(chv + 1) * 2]
            P.emit("vector", lambda e, dst=dst: e.tensor_copy(
                out=dst, in_=ptr2[:]), Vp)
            P.wait("tensor", Vp)
        for cg in range(4):
            P.wait("tensor", Vp)
            for chv in range(4):
                lhsT = pvb[:, chv * 512 + cg * 128:
                           chv * 512 + cg * 128 + 128]
                rhs = wtT[:, chv * 2:(chv + 1) * 2]
                P.emit("tensor", lambda e, lhsT=lhsT, rhs=rhs, chv=chv:
                       e.matmul(pp[:], lhsT, rhs, start=(chv == 0),
                                stop=(chv == 3)), Tm if chv == 3 else None)
            m_pp = Tm.v
            P.wait("vector", Tm, m_pp)
            dst = aggs[:, cg * 2:(cg + 1) * 2]
            P.emit("vector", lambda e, dst=dst: e.tensor_copy(
                out=dst, in_=pp[:]), Vp)
        v_ag = Vp.v
        P.wait("sync", Vp, v_ag)
        P.emit("sync", lambda e: e.dma_start(out=aggp, in_=aggs[:]),
               sfin, 16)
        P.wait("sync", sfin)
        P.play(blk)
    return nc



def build_B2c():
    nc = bass.Bass("TRN2", target_bir_lowering=False, debug=False)
    agg8 = nc.dram_tensor("agg8", [128, 64], F32, kind="ExternalInput").ap()
    oiw = nc.dram_tensor("oiw", [128, 4 * 1024], F32,
                         kind="ExternalInput").ap()
    oactp = nc.dram_tensor("oactp", [128, 16], F32,
                           kind="ExternalOutput").ap()

    es = contextlib.ExitStack()
    with es:
        sb = lambda n, s, d: es.enter_context(nc.sbuf_tensor(n, s, d))
        psr = lambda n, s, d: es.enter_context(nc.psum_tensor(n, s, d))
        sem = lambda n: Sem(es.enter_context(nc.semaphore(n)))
        at8 = sb("at8", [128, 64], F32)
        aggT = sb("aggT", [128, 8], F32)
        aggb = sb("aggb", [128, 8], BF16)
        oiwf = sb("oiwf", [128, 4096], F32)
        oiwb = sb("oiwb", [128, 4096], BF16)
        oacts = sb("oacts", [128, 16], F32)
        pp = psr("pp", [128, 2], F32)
        lda = sem("s_lda")
        sfin = sem("s_sfin")
        Vp = sem("s_v")
        Sp = sem("s_s")
        Tm = sem("s_t")
        blk = es.enter_context(nc.Block())
        P = Prog()
        force_table(P, 10)
        P.emit("sync", lambda e: e.dma_start(out=at8[:], in_=agg8), lda, 16)
        P.emit("sync", lambda e: e.dma_start(out=oiwf[:], in_=oiw), lda, 16)
        lda_all = lda.v

        def V(fn):
            P.emit("vector", fn, Vp)
            P.wait("vector", Vp)

        P.wait("vector", lda, lda_all)
        V(lambda e: e.tensor_copy(out=aggT[:], in_=at8[:, 0:8]))
        for j in range(1, 8):
            sl = at8[:, j * 8:(j + 1) * 8]
            V(lambda e, sl=sl: e.tensor_tensor(out=aggT[:], in0=aggT[:],
                                               in1=sl, op=ALU.add))
        V(lambda e: e.tensor_copy(out=aggb[:], in_=aggT[:]))
        V(lambda e: e.tensor_copy(out=oiwb[:], in_=oiwf[:]))
        v_cv = Vp.v
        P.wait("tensor", Vp, v_cv)
        for ot in range(8):
            P.wait("tensor", Sp)
            for chv in range(4):
                lhsT = oiwb[:, chv * 1024 + ot * 128:
                            chv * 1024 + ot * 128 + 128]
                rhs = aggb[:, chv * 2:(chv + 1) * 2]
                P.emit("tensor", lambda e, lhsT=lhsT, rhs=rhs, chv=chv:
                       e.matmul(pp[:], lhsT, rhs, start=(chv == 0),
                                stop=(chv == 3)), Tm if chv == 3 else None)
            m_pp = Tm.v
            P.wait("scalar", Tm, m_pp)
            dst = oacts[:, ot * 2:(ot + 1) * 2]
            P.emit("scalar", lambda e, dst=dst: e.activation(
                out=dst, in_=pp[:], func=AF.Gelu), Sp)
        s_all = Sp.v
        P.wait("sync", Sp, s_all)
        P.emit("sync", lambda e: e.dma_start(out=oactp, in_=oacts[:]),
               sfin, 16)
        P.wait("sync", sfin)
        P.play(blk)
    return nc



def build_B2c2():
    nc = bass.Bass("TRN2", target_bir_lowering=False, debug=False)
    oact = nc.dram_tensor("oact", [2, 8192], F32, kind="ExternalInput").ap()
    opat = nc.dram_tensor("opat", [128, 8 * 1024], F32,
                          kind="ExternalInput").ap()
    outbp = nc.dram_tensor("outbp", [128, 16], F32,
                           kind="ExternalOutput").ap()

    es = contextlib.ExitStack()
    with es:
        sb = lambda n, s, d: es.enter_context(nc.sbuf_tensor(n, s, d))
        psr = lambda n, s, d: es.enter_context(nc.psum_tensor(n, s, d))
        sem = lambda n: Sem(es.enter_context(nc.semaphore(n)))
        oat = sb("oat", [2, 8192], F32)
        cms = sb("cms", [2, 8192], F32)
        scr = sb("scr", [2, 8], F32)
        mb = sb("mb", [2, 1024], BF16)
        mT = sb("mT", [128, 16], BF16)
        wst = sb("wst", [128, 8192], F32)
        opb = sb("opb", [128, 8192], BF16)
        obs = sb("obs", [128, 16], F32)
        ones_t = sb("ones", [128, 128], BF16)
        idn = sb("idn", [128, 128], BF16)
        ptr2 = psr("ptr2", [128, 2], F32)
        pp = psr("pp", [128, 2], F32)
        lda = sem("s_lda")
        sfin = sem("s_sfin")
        Vp = sem("s_v")
        Tm = sem("s_t")
        G = sem("s_g")
        blk = es.enter_context(nc.Block())
        P = Prog()
        force_table(P, NLE_SET)
        P.emit("sync", lambda e: e.dma_start(out=oat[:], in_=oact), lda, 16)
        P.emit("sync", lambda e: e.dma_start(out=wst[:], in_=opat), lda, 16)
        lda_all = lda.v
        P.emit("vector", lambda e: e.memset(ones_t[:], 1.0), Vp)
        v_one = Vp.v
        P.wait("gpsimd", Vp, v_one)
        P.emit("gpsimd", lambda e: e.affine_select(
            out=idn[:], in_=ones_t[:], pattern=[[-1, 128]],
            compare_op=ALU.is_equal, fill=0.0, base=0,
            channel_multiplier=1), G)
        g_idn = G.v

        def V(fn):
            P.emit("vector", fn, Vp)
            P.wait("vector", Vp)

        P.wait("vector", lda, lda_all)
        V(lambda e: e.tensor_copy(out=opb[:], in_=wst[:]))
        bisect_mask(P, Vp, oat[:], scr, cms[:], 1024)
        V(lambda e: e.tensor_tensor(out=oat[:], in0=oat[:], in1=cms[:],
                                    op=ALU.mult))
        V(lambda e: e.tensor_copy(out=mb[:], in_=oat[0:2, 0:1024]))
        v_mb = Vp.v
        P.wait("tensor", Vp, v_mb)
        P.wait("tensor", G, g_idn)
        for chv in range(8):
            lhsT = mb[0:2, chv * 128:(chv + 1) * 128]
            P.emit("tensor", lambda e, lhsT=lhsT: e.matmul(
                ptr2[:], lhsT, idn[0:2, 0:2], start=True, stop=True), Tm)
            m_tr = Tm.v
            P.wait("vector", Tm, m_tr)
            dst = mT[:, chv * 2:(chv + 1) * 2]
            P.emit("vector", lambda e, dst=dst: e.tensor_copy(
                out=dst, in_=ptr2[:]), Vp)
            P.wait("tensor", Vp)
        for dt in range(8):
            P.wait("tensor", Vp)
            for chv in range(8):
                lhsT = opb[:, chv * 1024 + dt * 128:
                           chv * 1024 + dt * 128 + 128]
                rhs = mT[:, chv * 2:(chv + 1) * 2]
                P.emit("tensor", lambda e, lhsT=lhsT, rhs=rhs, chv=chv:
                       e.matmul(pp[:], lhsT, rhs, start=(chv == 0),
                                stop=(chv == 7)), Tm if chv == 7 else None)
            m_pp = Tm.v
            P.wait("vector", Tm, m_pp)
            dst = obs[:, dt * 2:(dt + 1) * 2]
            P.emit("vector", lambda e, dst=dst: e.tensor_copy(
                out=dst, in_=pp[:]), Vp)
        v_ob = Vp.v
        P.wait("sync", Vp, v_ob)
        P.emit("sync", lambda e: e.dma_start(out=outbp, in_=obs[:]),
               sfin, 16)
        P.wait("sync", sfin)
        P.play(blk)
    return nc



def build_B2d():
    nc = bass.Bass("TRN2", target_bir_lowering=False, debug=False)
    x0 = nc.dram_tensor("x0", [TSL, 1024], F32, kind="ExternalInput").ap()
    obp = nc.dram_tensor("obp", [1, 8 * 1024], F32,
                         kind="ExternalInput").ap()
    rw = nc.dram_tensor("rw", [128, 8 * 14], F32, kind="ExternalInput").ap()
    x1o = nc.dram_tensor("x1o", [TSL, 1024], F32, kind="ExternalOutput").ap()
    xnfm = nc.dram_tensor("xnfm", [1024, TSL], BF16,
                          kind="ExternalOutput").ap()
    cmo = nc.dram_tensor("cmo", [128, NTT * 14], F32,
                         kind="ExternalOutput").ap()

    es = contextlib.ExitStack()
    with es:
        sb = lambda n, s, d: es.enter_context(nc.sbuf_tensor(n, s, d))
        psr = lambda n, s, d: es.enter_context(nc.psum_tensor(n, s, d))
        sem = lambda n: Sem(es.enter_context(nc.semaphore(n)))
        xt = sb("xt", [128, 1024], F32)
        acc = sb("acc", [128, 1024], F32)
        sq = sb("sq", [128, 1024], F32)
        xnb = sb("xnb", [128, 1024], BF16)
        xnfmt = sb("xnfmt", [128, 128], BF16)
        cols = sb("cols", [128, 8], F32)
        obt = sb("obt", [1, 8192], F32)
        mem = sb("mem", [1, 1024], F32)
        membc = sb("membc", [128, 1024], F32)
        rwf = sb("rwf", [128, 112], F32)
        rwb = sb("rwb", [128, 112], BF16)
        lgt = sb("lgt", [128, 14], F32)
        w8 = sb("w8", [128, 48], F32)
        cmb = sb("cmb", [128, NTT * 14], F32)
        ones_t = sb("ones", [128, 128], BF16)
        ones1f = sb("ones1f", [1, 128], F32)
        idn = sb("idn", [128, 128], BF16)
        ptr = psr("ptr", [128, 128], BF16)
        plg = psr("plg", [128, 14], F32)
        paux = psr("paux", [128, 512], F32)
        lda = sem("s_lda")
        lxt = sem("s_lxt")
        sx1 = sem("s_sx1")
        sxn = sem("s_sxn")
        sfin = sem("s_sfin")
        Vp = sem("s_v")
        Sp = sem("s_s")
        Tm = sem("s_t")
        G = sem("s_g")
        blk = es.enter_context(nc.Block())
        P = Prog()
        force_table(P, NLE_SET)
        P.emit("sync", lambda e: e.dma_start(out=obt[:], in_=obp), lda, 16)
        P.emit("sync", lambda e: e.dma_start(out=rwf[:], in_=rw), lda, 16)
        lda_all = lda.v
        P.emit("vector", lambda e: e.memset(ones_t[:], 1.0))
        P.emit("vector", lambda e: e.memset(ones1f[:], 1.0))
        P.wait("vector", lda, lda_all)
        P.emit("vector", lambda e: e.tensor_copy(out=rwb[:], in_=rwf[:]))

        def V(fn):
            P.emit("vector", fn, Vp)
            P.wait("vector", Vp)

        V(lambda e: e.tensor_copy(out=mem[:], in_=obt[0:1, 0:1024]))
        for j in range(1, 8):
            sl = obt[0:1, j * 1024:(j + 1) * 1024]
            V(lambda e, sl=sl: e.tensor_tensor(out=mem[:], in0=mem[:],
                                               in1=sl, op=ALU.add))
        v_mem = Vp.v
        P.wait("gpsimd", Vp, v_mem)
        P.emit("gpsimd", lambda e: e.affine_select(
            out=idn[:], in_=ones_t[:], pattern=[[-1, 128]],
            compare_op=ALU.is_equal, fill=0.0, base=0,
            channel_multiplier=1), G)
        g_idn = G.v
        P.wait("tensor", Vp, v_mem)
        for half in range(2):
            msl = mem[0:1, half * 512:half * 512 + 512]
            P.emit("tensor", lambda e, msl=msl: e.matmul(
                paux[:], ones1f[0:1, :], msl, start=True, stop=True), Tm)
            m_bc = Tm.v
            P.wait("vector", Tm, m_bc)
            dst = membc[:, half * 512:half * 512 + 512]
            P.emit("vector", lambda e, dst=dst: e.tensor_copy(
                out=dst, in_=paux[:]), Vp)
            P.wait("tensor", Vp)
        v_mbc = Vp.v
        P.wait("tensor", G, g_idn)

        for i in range(NTT):
            P.wait("sync", Vp)
            P.wait("sync", Sp)
            src = x0[i * 128:(i + 1) * 128, :]
            P.emit("sync", lambda e, src=src: e.dma_start(out=xt[:],
                                                          in_=src), lxt, 16)
            lx_i = lxt.v
            P.wait("vector", lxt, lx_i)
            P.wait("vector", Vp, v_mbc)
            P.wait("vector", sx1)
            P.wait("vector", Tm)
            P.emit("vector", lambda e: e.tensor_tensor(
                out=acc[:], in0=xt[:], in1=membc[:], op=ALU.add), Vp)
            v_acc = Vp.v
            P.wait("sync", Vp, v_acc)
            dst = x1o[i * 128:(i + 1) * 128, :]
            P.emit("sync", lambda e, dst=dst: e.dma_start(out=dst,
                                                          in_=acc[:]),
                   sx1, 16)
            P.wait("scalar", Vp, v_acc)
            P.wait("scalar", Tm)
            ln_tile(P, Vp, Sp, acc, xnb, sq, cols, want_rstd2=False)
            for k in range(8):
                P.wait("tensor", Sp)
                P.wait("tensor", Vp)
                src = xnb[:, 128 * k:128 * (k + 1)]
                P.emit("tensor", lambda e, src=src: e.transpose(
                    ptr[:], src, idn[:]), Tm)
                P.wait("vector", Tm)
                P.wait("vector", sxn)
                P.emit("vector", lambda e: e.tensor_copy(out=xnfmt[:],
                                                         in_=ptr[:]), Vp)
                v_ev = Vp.v
                P.wait("sync", Vp, v_ev)
                dst = xnfm[128 * k:128 * (k + 1), i * 128:(i + 1) * 128]
                P.emit("sync", lambda e, dst=dst: e.dma_start(
                    out=dst, in_=xnfmt[:]), sxn, 16)
                P.wait("tensor", Vp, v_ev)
                rw_k = rwb[:, 14 * k:14 * k + 14]
                P.emit("tensor", lambda e, rw_k=rw_k, k=k: e.matmul(
                    plg[:], xnfmt[:], rw_k, start=(k == 0), stop=(k == 7)),
                    Tm)
            P.wait("scalar", Tm)
            P.wait("scalar", Sp)
            P.emit("scalar", lambda e: e.activation(
                out=lgt[:], in_=plg[:], func=AF.Identity), Sp)
            P.wait("vector", Sp)
            V(lambda e: e.reduce_max(out=w8[:, 0:1], in_=lgt[:], axis=AX.X,
                                     negate=True))
            P.wait("scalar", Vp)
            P.emit("scalar", lambda e: e.activation(
                out=w8[:, 16:30], in_=lgt[:], func=AF.Exp,
                bias=w8[:, 0:1]), Sp)
            P.wait("vector", Sp)
            V(lambda e: e.tensor_single_scalar(
                out=w8[:, 32:46], in_=w8[:, 16:30], scalar=1.0,
                op=ALU.is_ge))
            V(lambda e: e.reduce_sum(out=w8[:, 1:2], in_=w8[:, 16:30],
                                     axis=AX.X))
            P.wait("scalar", Vp)
            recip_lnexp(P, Sp, w8[:, 2:3], w8[:, 1:2])
            P.wait("vector", Sp)
            dst = cmb[:, 14 * i:14 * i + 14]
            P.emit("vector", lambda e, dst=dst: e.tensor_single_scalar(
                out=dst, in_=w8[:, 32:46], scalar=w8[:, 2:3], op=ALU.mult),
                Vp)
        v_cmb = Vp.v
        P.wait("sync", Vp, v_cmb)
        P.emit("sync", lambda e: e.dma_start(out=cmo, in_=cmb[:]), sfin, 16)
        P.wait("sync", sfin)
        P.wait("sync", sxn)
        P.wait("sync", sx1)
        P.play(blk)
    return nc



def build_A3():
    nc = bass.Bass("TRN2", target_bir_lowering=False, debug=False)
    xnf = nc.dram_tensor("xnf", [1024, T], BF16, kind="ExternalInput").ap()
    xnd = nc.dram_tensor("xnd", [128, T], BF16, kind="ExternalInput").ap()
    xpw = nc.dram_tensor("xpw", [128, 8 * 96], F32,
                         kind="ExternalInput").ap()
    dtw = nc.dram_tensor("dtw", [64, 128], F32, kind="ExternalInput").ap()
    dtb = nc.dram_tensor("dtb", [128, 1], F32, kind="ExternalInput").ap()
    alg = nc.dram_tensor("alg", [128, 1], F32, kind="ExternalInput").ap()
    dvc = nc.dram_tensor("dvc", [128, 1], F32, kind="ExternalInput").ap()
    selg = nc.dram_tensor("selg", [128, 16 * 128], F32,
                          kind="ExternalInput").ap()
    selbc = nc.dram_tensor("selbc", [96, 256], F32,
                           kind="ExternalInput").ap()
    selr = nc.dram_tensor("selr", [128, 16 * 128], F32,
                          kind="ExternalInput").ap()
    mws = nc.dram_tensor("mws", [1, 14 * T], F32, kind="ExternalInput").ap()
    mw14 = nc.dram_tensor("mw14", [14, T], F32, kind="ExternalInput").ap()
    w1c = nc.dram_tensor("w1c", [128, 28672], F32,
                         kind="ExternalInput").ap()
    w2c = nc.dram_tensor("w2c", [128, 28672], F32,
                         kind="ExternalInput").ap()
    b1c = nc.dram_tensor("b1c", [128, 28], F32, kind="ExternalInput").ap()
    b2f = nc.dram_tensor("b2f", [14, 1024], F32, kind="ExternalInput").ap()
    yo3 = nc.dram_tensor("yo3", [128, T], F32, kind="ExternalOutput").ap()
    mo3 = nc.dram_tensor("mo3", [1024, T], BF16, kind="ExternalOutput").ap()

    es = contextlib.ExitStack()
    with es:
        sb = lambda n, s, d: es.enter_context(nc.sbuf_tensor(n, s, d))
        psr = lambda n, s, d: es.enter_context(nc.psum_tensor(n, s, d))
        sem = lambda n: Sem(es.enter_context(nc.semaphore(n)))
        xch = [sb("xch0", [128, 8 * 512], BF16),
               sb("xch1", [128, 8 * 512], BF16)]
        ones1 = sb("ones1", [1, 128], BF16)
        ccol = sb("ccol", [128, 8], F32)
        paux = psr("paux", [128, 512], F32)
        paux2 = psr("paux2", [128, 512], F32)
        pb1 = psr("pb1", [128, 512], F32)
        pb2 = psr("pb2", [128, 512], F32)
        pyy = psr("pyy", [128, 512], F32)
        lda = sem("s_lda")
        lx = [sem("s_lx0"), sem("s_lx1")]
        lud = sem("s_lud")
        lmw = sem("s_lmw")
        lws = sem("s_lws")
        sy = sem("s_sy")
        smo = sem("s_smo")
        Vp = sem("s_v")
        Sp = sem("s_s")
        Mm = sem("s_m")
        ph = sem("s_ph")
        blk = es.enter_context(nc.Block())

        with contextlib.ExitStack() as es1:
            sb1 = lambda n, s, d: es1.enter_context(nc.sbuf_tensor(n, s, d))
            xpwf = sb1("xpwf", [128, 768], F32)
            xpwb = sb1("xpwb", [128, 768], BF16)
            dtwf = sb1("dtwf", [64, 128], F32)
            selgf = sb1("selgf", [128, 2048], F32)
            sbcf = sb1("sbcf", [96, 256], F32)
            selrf = sb1("selrf", [128, 2048], F32)
            xpc = sb1("xpc", [96, 512], F32)
            delc = sb1("delc", [128, 512], F32)
            spt = sb1("spt", [128, 2 * 512], F32)
            Bx = sb1("Bx", [128, 512], F32)
            Cx = sb1("Cx", [128, 512], F32)
            udc = sb1("udc", [128, 512], BF16)
            ucf = sb1("ucf", [128, 512], F32)
            duc = sb1("duc", [128, 512], F32)
            dAc2 = sb1("dAc2", [128, 2 * 512], F32)
            u3 = sb1("u3", [128, 512], F32)
            s3 = sb1("s3", [128, 512], F32)
            sq3 = sb1("sq3", [128, 2 * 512], F32)
            yct = sb1("yct", [128, 512], F32)
            carr3 = sb1("carr3", [128, 16], F32)
            P = Prog()
            force_table(P, NLE_SET)
            P.emit("sync", lambda e: e.dma_start(out=xpwf[:], in_=xpw),
                   lda, 16)
            P.emit("sync", lambda e: e.dma_start(out=dtwf[:], in_=dtw),
                   lda, 16)
            P.emit("sync", lambda e: e.dma_start(out=ccol[:, 0:1],
                                                 in_=dtb), lda, 16)
            P.emit("sync", lambda e: e.dma_start(out=ccol[:, 4:5],
                                                 in_=alg), lda, 16)
            P.emit("sync", lambda e: e.dma_start(out=ccol[:, 2:3],
                                                 in_=dvc), lda, 16)
            P.emit("sync", lambda e: e.dma_start(out=selgf[:], in_=selg),
                   lda, 16)
            P.emit("sync", lambda e: e.dma_start(out=sbcf[:], in_=selbc),
                   lda, 16)
            P.emit("sync", lambda e: e.dma_start(out=selrf[:], in_=selr),
                   lda, 16)
            lda_all = lda.v
            P.emit("vector", lambda e: e.memset(ones1[:], 1.0))
            P.wait("vector", lda, lda_all)
            P.emit("vector", lambda e: e.tensor_copy(out=xpwb[:],
                                                     in_=xpwf[:]))
            P.emit("vector", lambda e: e.memset(ccol[:, 3:4], 1.0), Vp)
            v_cv = Vp.v
            P.wait("scalar", Vp, v_cv)
            P.emit("scalar", lambda e: e.activation(
                out=ccol[:, 1:2], in_=ccol[:, 4:5], func=AF.Exp), Sp)
            P.wait("vector", Sp)
            P.emit("vector", lambda e: e.tensor_single_scalar(
                out=ccol[:, 1:2], in_=ccol[:, 1:2], scalar=-1.0,
                op=ALU.mult), Vp)
            v_an = Vp.v
            P.wait("tensor", Vp, v_an)
            P.wait("scalar", Vp, v_an)

            def V(fn):
                P.emit("vector", fn, Vp)
                P.wait("vector", Vp)

            for b in range(2):
                for ch in range(16):
                    c0 = b * S + ch * 512
                    par = ch & 1
                    xb = xch[par]
                    P.wait("sync", Mm)
                    for k in range(8):
                        dst = xb[:, k * 512:(k + 1) * 512]
                        src = xnf[k * 128:(k + 1) * 128, c0:c0 + 512]
                        P.emit("sync", lambda e, dst=dst, src=src:
                               e.dma_start(out=dst, in_=src), lx[par], 16)
                    lx_t = lx[par].v
                    src = xnd[:, c0:c0 + 512]
                    P.wait("sync", Vp)
                    P.emit("sync", lambda e, src=src: e.dma_start(
                        out=udc[:], in_=src), lud, 16)
                    lu_t = lud.v
                    P.wait("tensor", lx[par], lx_t)
                    P.wait("tensor", Sp)
                    P.wait("tensor", Vp)
                    for k in range(8):
                        lhsT = xpwb[:, k * 96:(k + 1) * 96]
                        rhs = xb[:, k * 512:(k + 1) * 512]
                        P.emit("tensor", lambda e, lhsT=lhsT, rhs=rhs, k=k:
                               e.matmul(paux[0:96, :], lhsT, rhs,
                                        start=(k == 0), stop=(k == 7)),
                               Mm if k == 7 else None)
                    m_xp = Mm.v
                    P.wait("scalar", Mm, m_xp)
                    P.emit("scalar", lambda e: e.activation(
                        out=xpc[:], in_=paux[0:96, :], func=AF.Identity),
                        Sp)
                    s_xp = Sp.v
                    P.wait("tensor", Sp, s_xp)
                    P.emit("tensor", lambda e: e.matmul(
                        paux2[:], dtwf[:], xpc[0:64, :], start=True,
                        stop=True))
                    P.emit("tensor", lambda e: e.matmul(
                        pb1[:], sbcf[0:96, 0:128], xpc[:], start=True,
                        stop=True))
                    P.emit("tensor", lambda e: e.matmul(
                        pb2[:], sbcf[0:96, 128:256], xpc[:], start=True,
                        stop=True), Mm)
                    m_dbc = Mm.v
                    P.wait("scalar", Mm, m_dbc)
                    P.emit("scalar", lambda e: e.activation(
                        out=spt[:, 0:512], in_=paux2[:], func=AF.Abs,
                        bias=ccol[:, 0:1]), Sp)
                    P.wait("scalar", Sp)
                    P.emit("scalar", lambda e: e.activation(
                        out=spt[:, 0:512], in_=spt[:, 0:512], func=AF.Exp,
                        scale=-1.0), Sp)
                    P.wait("scalar", Sp)
                    P.emit("scalar", lambda e: e.activation(
                        out=spt[:, 0:512], in_=spt[:, 0:512], func=AF.Ln,
                        bias=ccol[:, 3:4]), Sp)
                    P.wait("scalar", Sp)
                    P.emit("scalar", lambda e: e.activation(
                        out=spt[:, 512:1024], in_=paux2[:], func=AF.Relu,
                        bias=ccol[:, 0:1]), Sp)
                    P.emit("scalar", lambda e: e.activation(
                        out=Bx[:], in_=pb1[:], func=AF.Identity), Sp)
                    P.emit("scalar", lambda e: e.activation(
                        out=Cx[:], in_=pb2[:], func=AF.Identity), Sp)
                    s_sp = Sp.v
                    P.wait("vector", Sp, s_sp)
                    P.wait("vector", lud, lu_t)
                    V(lambda e: e.tensor_tensor(
                        out=delc[:], in0=spt[:, 0:512], in1=spt[:, 512:1024],
                        op=ALU.add))
                    V(lambda e: e.tensor_copy(out=ucf[:], in_=udc[:]))
                    V(lambda e: e.tensor_tensor(
                        out=duc[:], in0=delc[:], in1=ucf[:], op=ALU.mult))
                    v_du = Vp.v
                    P.wait("tensor", Vp, v_du)
                    for g in range(16):
                        gsl = selgf[:, g * 128:(g + 1) * 128]
                        P.wait("tensor", Sp)
                        P.emit("tensor", lambda e, gsl=gsl: e.matmul(
                            paux2[:], gsl, delc[:], start=True, stop=True))
                        P.emit("tensor", lambda e, gsl=gsl: e.matmul(
                            pb1[:], gsl, duc[:], start=True, stop=True), Mm)
                        m_ex = Mm.v
                        P.wait("scalar", Mm, m_ex)
                        P.wait("scalar", Vp)
                        P.emit("scalar", lambda e: e.activation(
                            out=dAc2[:, 0:512], in_=paux2[:], func=AF.Exp,
                            scale=ccol[:, 1:2]), Sp)
                        s_da = Sp.v
                        P.wait("vector", Sp, s_da)
                        V(lambda e: e.tensor_tensor(
                            out=u3[:], in0=pb1[:], in1=Bx[:], op=ALU.mult))
                        ini = 0.0 if ch == 0 else carr3[:, g:g + 1]
                        V(lambda e, ini=ini: e.tensor_tensor_scan(
                            out=s3[:], data0=dAc2[:, 0:512], data1=u3[:],
                            initial=ini, op0=ALU.mult, op1=ALU.add))
                        V(lambda e, g=g: e.tensor_copy(
                            out=carr3[:, g:g + 1], in_=s3[:, 511:512]))
                        V(lambda e: e.tensor_tensor(
                            out=sq3[:, 0:512], in0=s3[:], in1=Cx[:],
                            op=ALU.mult))
                        v_sq = Vp.v
                        P.wait("tensor", Vp, v_sq)
                        rsl = selrf[:, g * 128:(g + 1) * 128]
                        P.emit("tensor", lambda e, rsl=rsl, g=g: e.matmul(
                            pyy[:], rsl, sq3[:, 0:512], start=(g == 0),
                            stop=(g == 15)), Mm if g == 15 else None)
                    m_y = Mm.v
                    P.wait("vector", Mm, m_y)
                    V(lambda e: e.tensor_single_scalar(
                        out=yct[:], in_=ucf[:], scalar=ccol[:, 2:3],
                        op=ALU.mult))
                    P.wait("vector", sy)
                    V(lambda e: e.tensor_tensor(
                        out=yct[:], in0=yct[:], in1=pyy[:], op=ALU.add))
                    v_yc = Vp.v
                    P.wait("sync", Vp, v_yc)
                    dst = yo3[:, c0:c0 + 512]
                    P.emit("sync", lambda e, dst=dst: e.dma_start(
                        out=dst, in_=yct[:]), sy, 16)
                    P.wait("tensor", Vp, v_yc)
            P.wait("sync", sy)
            for eng in ENGS:
                P.wait(eng, sy)
            P.play(blk)
            PB = Prog()
            for eng in ENGS:
                PB.emit(eng, lambda e: e.nop(), ph)
            bar1 = ph.v
            for eng in ENGS:
                PB.wait(eng, ph, bar1)
            PB.play(blk)

        with contextlib.ExitStack() as es2:
            sb2 = lambda n, s, d: es2.enter_context(nc.sbuf_tensor(n, s, d))
            wst3 = sb2("wst3", [128, 4096], F32)
            w1b = sb2("w1b", [128, 28672], BF16)
            w2b = sb2("w2b", [128, 28672], BF16)
            b1t = sb2("b1t", [128, 28], F32)
            b2t = sb2("b2t", [14, 1024], F32)
            b2b = sb2("b2b", [14, 1024], BF16)
            mwr = sb2("mwr", [1, 512], F32)
            mwrb = sb2("mwrb", [1, 512], BF16)
            m14f = sb2("m14f", [14, 512], F32)
            m14b = sb2("m14b", [14, 512], BF16)
            mbc = sb2("mbc", [128, 512], BF16)
            hid = sb2("hid", [128, 28 * 512], BF16)
            mot = sb2("mot", [128, 512], BF16)
            P = Prog()
            force_table(P, 10)
            P.emit("sync", lambda e: e.dma_start(out=b1t[:], in_=b1c),
                   lda, 16)
            P.emit("sync", lambda e: e.dma_start(out=b2t[:], in_=b2f),
                   lda, 16)
            lda_m = lda.v
            P.wait("vector", lda, lda_m)
            P.emit("vector", lambda e: e.tensor_single_scalar(
                out=b2b[:], in_=b2t[:], scalar=0.125, op=ALU.mult), Vp)
            for wi, (warr, wdst) in enumerate(((w1c, w1b), (w2c, w2b))):
                for rd in range(7):
                    src = warr[:, rd * 4096:(rd + 1) * 4096]
                    P.wait("sync", Vp)
                    P.emit("sync", lambda e, src=src: e.dma_start(
                        out=wst3[:], in_=src), lws, 16)
                    lw_c = lws.v
                    P.wait("vector", lws, lw_c)
                    dst = wdst[:, rd * 4096:(rd + 1) * 4096]
                    P.emit("vector", lambda e, dst=dst: e.tensor_copy(
                        out=dst, in_=wst3[:]), Vp)
            v_w = Vp.v
            P.wait("tensor", Vp, v_w)

            for tc in range(T // 512):
                par = tc & 1
                xb = xch[par]
                P.wait("sync", Mm)
                for k in range(8):
                    dst = xb[:, k * 512:(k + 1) * 512]
                    src = xnf[k * 128:(k + 1) * 128,
                              tc * 512:(tc + 1) * 512]
                    P.emit("sync", lambda e, dst=dst, src=src: e.dma_start(
                        out=dst, in_=src), lx[par], 16)
                lx_t = lx[par].v
                P.wait("sync", Vp)
                src = mw14[0:14, tc * 512:(tc + 1) * 512]
                P.emit("sync", lambda e, src=src: e.dma_start(
                    out=m14f[:], in_=src), lmw, 16)
                lm_t = lmw.v
                P.wait("vector", lmw, lm_t)
                P.wait("vector", Mm)
                P.emit("vector", lambda e: e.tensor_copy(out=m14b[:],
                                                         in_=m14f[:]), Vp)
                v_mw = Vp.v
                P.wait("tensor", lx[par], lx_t)
                for eh in range(28):
                    P.wait("tensor", Sp)
                    for k in range(8):
                        lhsT = w1b[:, (eh * 8 + k) * 128:
                                   (eh * 8 + k) * 128 + 128]
                        rhs = xb[:, k * 512:(k + 1) * 512]
                        P.emit("tensor", lambda e, lhsT=lhsT, rhs=rhs, k=k:
                               e.matmul(paux[:], lhsT, rhs, start=(k == 0),
                                        stop=(k == 7)),
                               Mm if k == 7 else None)
                    m_h = Mm.v
                    P.wait("scalar", Mm, m_h)
                    P.wait("scalar", Vp)
                    hsl = hid[:, eh * 512:(eh + 1) * 512]
                    bcol = b1t[:, eh:eh + 1]
                    P.emit("scalar", lambda e, hsl=hsl, bcol=bcol:
                           e.activation(out=hsl, in_=paux[:], func=AF.Gelu,
                                        bias=bcol), Sp)
                s_hid = Sp.v
                P.wait("tensor", Vp, v_mw)
                for ee in range(14):
                    src = mws[0:1, ee * T + tc * 512:ee * T + tc * 512 + 512]
                    P.wait("sync", Vp)
                    P.emit("sync", lambda e, src=src: e.dma_start(
                        out=mwr[:], in_=src), lmw, 16)
                    lm_e = lmw.v
                    P.wait("vector", lmw, lm_e)
                    P.emit("vector", lambda e: e.tensor_copy(
                        out=mwrb[:], in_=mwr[:]), Vp)
                    v_me = Vp.v
                    P.wait("tensor", Vp, v_me)
                    P.wait("tensor", Sp)
                    P.emit("tensor", lambda e: e.matmul(
                        paux2[:], ones1[0:1, :], mwrb[0:1, :], start=True,
                        stop=True), Mm)
                    m_bc = Mm.v
                    P.wait("scalar", Mm, m_bc)
                    P.wait("scalar", Sp, s_hid)
                    P.wait("scalar", Vp)
                    P.emit("scalar", lambda e: e.activation(
                        out=mbc[:], in_=paux2[:], func=AF.Identity), Sp)
                    s_bc = Sp.v
                    P.wait("vector", Sp, s_bc)
                    for ht in range(2):
                        hsl = hid[:, (ee * 2 + ht) * 512:
                                  (ee * 2 + ht) * 512 + 512]
                        P.emit("vector", lambda e, hsl=hsl: e.tensor_tensor(
                            out=hsl, in0=hsl, in1=mbc[:], op=ALU.mult),
                            Vp if ht == 1 else None)
                v_hm = Vp.v
                P.wait("tensor", Vp, v_hm)
                for dt in range(8):
                    for eh in range(28):
                        lhsT = w2b[:, (eh * 8 + dt) * 128:
                                   (eh * 8 + dt) * 128 + 128]
                        rhs = hid[:, eh * 512:(eh + 1) * 512]
                        P.emit("tensor", lambda e, lhsT=lhsT, rhs=rhs, eh=eh:
                               e.matmul(pyy[:], lhsT, rhs, start=(eh == 0),
                                        stop=False))
                    b2sl = b2b[0:14, dt * 128:(dt + 1) * 128]
                    P.emit("tensor", lambda e, b2sl=b2sl: e.matmul(
                        pyy[:], b2sl, m14b[:], start=False, stop=True), Mm)
                    m_dn = Mm.v
                    P.wait("scalar", Mm, m_dn)
                    P.wait("scalar", smo)
                    P.emit("scalar", lambda e: e.activation(
                        out=mot[:], in_=pyy[:], func=AF.Identity), Sp)
                    s_mo = Sp.v
                    P.wait("sync", Sp, s_mo)
                    dst = mo3[dt * 128:(dt + 1) * 128,
                              tc * 512:(tc + 1) * 512]
                    P.emit("sync", lambda e, dst=dst: e.dma_start(
                        out=dst, in_=mot[:]), smo, 16)
                    P.wait("tensor", Sp, s_mo)
            P.wait("sync", smo)
            P.play(blk)
    return nc



_PROGS = {}


def _get(name, builder):
    if name not in _PROGS:
        _PROGS[name] = builder()
    return _PROGS[name]


def run_L0(inputs):
    x0 = np.ascontiguousarray(
        np.asarray(inputs["L0_attn_hidden_states"], np.float32)
        .reshape(T, 1024))
    nw = np.asarray(inputs["L0_ffn_norm_weight"], np.float32)
    rw = np.asarray(inputs["L0_ffn_router_weight"], np.float32)
    Wqkv = np.asarray(inputs["L0_attn_W_qkv"], np.float32)
    Wout = np.asarray(inputs["L0_attn_W_out"], np.float32)
    rel = np.asarray(inputs["L0_attn_rel_emb"], np.float32)
    gw = np.asarray(inputs["L0_ffn_gate_up_weight"], np.float32)
    dw = np.asarray(inputs["L0_ffn_down_weight"], np.float32)
    al = np.ascontiguousarray(
        np.asarray(inputs["L0_ffn_residual_alpha"],
                   np.float32).reshape(1, 1))

    nw8 = np.ascontiguousarray(nw.reshape(8, 128).T)
    rw64 = chunk128(np.ascontiguousarray(rw.T))

    pre0 = _get("pre0", build_pre0)
    im = [{"x": x0[c * TSL:(c + 1) * TSL], "nw8": nw8, "rw64": rw64}
          for c in range(NCORE)]
    r = run_bass_kernel_spmd(pre0, im, core_ids=list(range(NCORE)))
    xn0 = np.concatenate([r.results[c]["xnfm"] for c in range(NCORE)],
                         axis=1)
    r2 = np.concatenate([r.results[c]["r2o"].T.reshape(-1)
                         for c in range(NCORE)])
    comb = np.concatenate(
        [r.results[c]["cmo"].reshape(128, NTT, 8).transpose(1, 0, 2)
         .reshape(TSL, 8) for c in range(NCORE)], axis=0)

    relohT = make_rel_ohT()
    A0 = _get("A0", build_A0)

    def bperm(a, c, axis):
        if c // 4 == 0:
            return np.ascontiguousarray(a)
        return np.ascontiguousarray(np.concatenate(
            (np.take(a, range(S, T), axis=axis),
             np.take(a, range(0, S), axis=axis)), axis=axis))

    im = []
    for c in range(NCORE):
        hs = [4 * (c % 4) + j for j in range(4)]
        colsel = ([0 * 1024 + h * 64 + d for h in hs for d in range(64)]
                  + [1 * 1024 + h * 64 + d for h in hs for d in range(64)]
                  + [2 * 1024 + h * 64 + d for h in hs for d in range(64)])
        im.append({
            "xn": bperm(xn0, c, 1),
            "wq": chunk128(Wqkv[:, colsel]),
            "rel4": np.ascontiguousarray(rel[:, hs]),
            "relohT": relohT,
            "gup": chunk128(gw[c].T),
            "dwn": chunk128(np.ascontiguousarray(dw[c].T)),
            "combr": bperm(comb[:, c].reshape(1, T), c, 1),
            "alpha": al,
            "r2r": bperm(r2.reshape(1, T), c, 1),
            "nw8": nw8,
        })
    rA = run_bass_kernel_spmd(A0, im, core_ids=list(range(NCORE)))
    yfull = np.zeros((1024, T), BF)
    for c in range(NCORE):
        b = c // 4
        h0 = 4 * (c % 4)
        yfull[h0 * 64:(h0 + 4) * 64, b * S:(b + 1) * S] = \
            rA.results[c]["yo"]
    mps = [bperm(rA.results[c]["mpo"], c, 1) for c in range(NCORE)]

    B0 = _get("B0", build_B0)
    wout_arr = chunk128(Wout)
    im = []
    for c in range(NCORE):
        sl = slice(c * TSL, (c + 1) * TSL)
        mp_arr = np.concatenate(
            [np.ascontiguousarray(m.T[sl]) for m in mps], axis=0)
        im.append({
            "x0": x0[sl],
            "yfm": np.ascontiguousarray(yfull[:, sl]),
            "wout": wout_arr,
            "mp": mp_arr,
        })
    rB = run_bass_kernel_spmd(B0, im, core_ids=list(range(NCORE)))
    x1 = np.concatenate([rB.results[c]["x1o"] for c in range(NCORE)],
                        axis=0)
    xn1 = np.concatenate([rB.results[c]["xnfm"] for c in range(NCORE)],
                         axis=1)
    return x1, xn1, {"xn0": xn0, "r2": r2, "comb": comb, "y": yfull,
                     "mps": mps}


def run_L1(inputs, x1, xn1):
    W = np.asarray(inputs["L1_attn_W"], np.float32)
    bb = np.asarray(inputs["L1_attn_b"], np.float32)
    wg = np.asarray(inputs["L1_ffn_w_gate"])
    wu = np.asarray(inputs["L1_ffn_w_up"])
    wd = np.asarray(inputs["L1_ffn_w_down"])

    A1 = _get("A1", build_A1)

    def bperm(a, c, axis):
        if c // 4 == 0:
            return np.ascontiguousarray(a)
        return np.ascontiguousarray(np.concatenate(
            (np.take(a, range(S, T), axis=axis),
             np.take(a, range(0, S), axis=axis)), axis=axis))

    im = []
    for c in range(NCORE):
        hs = [4 * (c % 4) + j for j in range(4)]
        qcols = [h * 192 + j for h in hs for j in range(64)]
        kcols = [h * 192 + 64 + j for h in hs for j in range(64)]
        vcols = [h * 192 + 128 + j for h in hs for j in range(64)]
        colsel = qcols + kcols + vcols
        qkb = np.stack([bb[qcols[:128]], bb[qcols[128:]],
                        bb[kcols[:128]], bb[kcols[128:]]], axis=1)
        vbr = bb[vcols].reshape(1, 256)
        im.append({
            "xn": bperm(xn1, c, 1),
            "xnf": np.ascontiguousarray(xn1[:, c * TSL:(c + 1) * TSL]),
            "wq": chunk128(W[:, colsel]),
            "qkb": np.ascontiguousarray(qkb),
            "vbr": np.ascontiguousarray(vbr),
            "wg8": chunk128(wg.T).astype(np.int8),
            "wu8": chunk128(wu.T).astype(np.int8),
            "wd8": chunk128(np.ascontiguousarray(wd.T)).astype(np.int8),
        })
    rA = run_bass_kernel_spmd(A1, im, core_ids=list(range(NCORE)))
    yfull = np.zeros((1024, T), np.float32)
    for c in range(NCORE):
        b = c // 4
        h0 = 4 * (c % 4)
        yfull[h0 * 64:(h0 + 4) * 64, b * S:(b + 1) * S] = rA.results[c]["yo"]
    oos = [rA.results[c]["oo"] for c in range(NCORE)]

    B1 = _get("B1", lambda: build_B_add(2, 0))
    im = []
    for c in range(NCORE):
        sl = slice(c * TSL, (c + 1) * TSL)
        tf = np.concatenate([np.ascontiguousarray(yfull[:, sl].T),
                             np.ascontiguousarray(oos[c].T)], axis=0)
        im.append({"x0": x1[sl], "tf": tf})
    rB = run_bass_kernel_spmd(B1, im, core_ids=list(range(NCORE)))
    x2 = np.concatenate([rB.results[c]["x1o"] for c in range(NCORE)], axis=0)
    xn2 = np.concatenate([rB.results[c]["xnfm"] for c in range(NCORE)],
                         axis=1)
    return x2, xn2, {"y1": yfull, "oos": oos}


def _bperm(a, c, axis):
    if c // 4 == 0:
        return np.ascontiguousarray(a)
    return np.ascontiguousarray(np.concatenate(
        (np.take(a, range(S, T), axis=axis),
         np.take(a, range(0, S), axis=axis)), axis=axis))


def run_L2(inputs, x2, xn2):
    Wq = np.asarray(inputs["L2_attn_Wqkv_w"], np.float32)
    Wb = np.asarray(inputs["L2_attn_Wqkv_b"], np.float32)
    ow = np.asarray(inputs["L2_attn_out_w"], np.float32)
    ob = np.asarray(inputs["L2_attn_out_b"], np.float32)
    Wst = np.asarray(inputs["L2_attn_W_static"], np.float32)
    Lam = np.asarray(inputs["L2_attn_Lambda_raw"], np.float32)
    Gam = np.asarray(inputs["L2_attn_Gamma"], np.float32)
    gw1 = np.asarray(inputs["L2_attn_gate_w1"], np.float32)
    gb1 = np.asarray(inputs["L2_attn_gate_b1"], np.float32)
    gw2 = np.asarray(inputs["L2_attn_gate_w2"], np.float32)
    gb2 = np.asarray(inputs["L2_attn_gate_b2"], np.float32)
    ipat = np.asarray(inputs["L2_ffn_input_patterns"], np.float32)
    piw = np.asarray(inputs["L2_ffn_process_input_weights"], np.float32)
    pv = np.asarray(inputs["L2_ffn_process_values"], np.float32)
    oiw = np.asarray(inputs["L2_ffn_output_input_weights"], np.float32)
    opat = np.asarray(inputs["L2_ffn_output_patterns"], np.float32)
    rw3 = np.asarray(inputs["L3_ffn_router_w"], np.float32)

    sel2 = np.zeros((128, 32 * 64), np.float32)
    for t in range(32):
        for p in range(128):
            sel2[p, t * 64 + 2 * t + p // 64] = 1.0
    cmsk = np.zeros((128, 4 * 512), np.float32)
    for mi in range(4):
        for p in range(128):
            f0 = mi * 128 + p
            if f0 + 1 <= 512:
                cmsk[p, mi * 512:mi * 512 + f0] = 0.0
            cmsk[p, mi * 512 + min(f0 + 1, 512) - 1:(mi + 1) * 512] = 0.0
    cmsk[:] = 0.0
    for mi in range(4):
        for p in range(128):
            kpos = mi * 128 + p
            if kpos > 0:
                cmsk[p, mi * 512:mi * 512 + min(kpos, 512)] = -80000.0

    A2 = _get("A2", build_A2)
    im = []
    for c in range(NCORE):
        hs = [4 * (c % 4) + j for j in range(4)]
        qcols = [h * 64 + d for h in hs for d in range(64)]
        kcols = [1024 + h * 64 + d for h in hs for d in range(64)]
        vcols = [2048 + h * 64 + d for h in hs for d in range(64)]
        colsel = qcols + kcols + vcols
        qkb = np.stack([Wb[qcols[:128]], Wb[qcols[128:]],
                        Wb[kcols[:128]], Wb[kcols[128:]],
                        Wb[vcols[:128]], Wb[vcols[128:]]], axis=1)
        gsel = np.zeros((128, 4 * 4096), np.float32)
        for j, h in enumerate(hs):
            r0 = (j % 2) * 64
            for t in range(32):
                for dd in range(2):
                    gsel[r0 + 2 * t + dd,
                         j * 4096 + t * 128 + dd * 64:
                         j * 4096 + t * 128 + dd * 64 + 64] = Gam[h, 2 * t + dd]
        lamc = np.concatenate(
            [np.ascontiguousarray(Lam[h].reshape(32, 128).T) for h in hs],
            axis=1)
        wstac = np.concatenate([Wst[h].T for h in hs], axis=1)
        hselc = np.zeros((16, 4 * 64), np.float32)
        for j, h in enumerate(hs):
            hselc[h, j * 64:(j + 1) * 64] = 1.0
        im.append({
            "xn": _bperm(xn2, c, 1),
            "wq": chunk128(Wq[:, colsel]),
            "qkb": np.ascontiguousarray(qkb),
            "vbr": np.ascontiguousarray(Wb[vcols].reshape(1, 256)),
            "gw1c": chunk128(gw1),
            "gw2c": chunk128(gw2),
            "gb1": np.ascontiguousarray(gb1.reshape(4, 128).T),
            "gb2": np.ascontiguousarray(gb2.reshape(16, 1)),
            "lam": np.ascontiguousarray(lamc),
            "gsel": np.ascontiguousarray(gsel),
            "sel2": sel2,
            "wsta": np.ascontiguousarray(wstac),
            "hsel": hselc,
            "cmsk": cmsk,
            "ipc": chunk128(
                np.ascontiguousarray(ipat[c * 1024:(c + 1) * 1024].T)),
        })
    rA = run_bass_kernel_spmd(A2, im, core_ids=list(range(NCORE)))
    y2full = np.zeros((1024, T), BF)
    ia_sums = np.zeros((2, 8192), np.float32)
    for c in range(NCORE):
        b = c // 4
        h0 = 4 * (c % 4)
        y2full[h0 * 64:(h0 + 4) * 64, b * S:(b + 1) * S] = \
            rA.results[c]["yo"]
        iao = rA.results[c]["iao"]
        for pt in range(8):
            for half in range(2):
                bglob = b if half == 0 else 1 - b
                ia_sums[bglob, c * 1024 + pt * 128:
                        c * 1024 + pt * 128 + 128] = iao[:, pt * 2 + half]

    B2a = _get("B2a", build_B2a)
    wout_arr = chunk128(ow)
    bo_arr = np.ascontiguousarray(ob.reshape(1, 1024))
    im = []
    for c in range(NCORE):
        sl = slice(c * TSL, (c + 1) * TSL)
        pidx = (list(range(c * 1024, (c + 1) * 1024))
                + [j for j in range(8192) if not
                   (c * 1024 <= j < (c + 1) * 1024)])
        piw_sl = piw[:, c * 1024:(c + 1) * 1024]
        im.append({
            "x0": x2[sl],
            "yfm": np.ascontiguousarray(y2full[:, sl]),
            "wout": wout_arr,
            "bo": bo_arr,
            "iaf": np.ascontiguousarray(ia_sums[:, pidx]),
            "piw": chunk128(np.ascontiguousarray(piw_sl.T)),
        })
    rBa = run_bass_kernel_spmd(B2a, im, core_ids=list(range(NCORE)))
    x3p = [rBa.results[c]["x3p"] for c in range(NCORE)]
    pacs = []
    for c in range(NCORE):
        pp_ = rBa.results[c]["pactp"]
        pa = np.zeros((2, 4096), np.float32)
        for pt in range(32):
            for b in range(2):
                pa[b, pt * 128:(pt + 1) * 128] = pp_[:, pt * 2 + b]
        pacs.append(pa)

    B2b = _get("B2b", build_B2b)
    im = []
    for c in range(NCORE):
        perm2 = (list(range(c * 512, (c + 1) * 512))
                 + [j for j in range(4096) if not
                    (c * 512 <= j < (c + 1) * 512)])
        pac8 = np.concatenate([p[:, perm2] for p in pacs], axis=1)
        im.append({
            "pac8": np.ascontiguousarray(pac8),
            "pv": chunk128(np.ascontiguousarray(
                pv[c * 512:(c + 1) * 512])),
        })
    rBb = run_bass_kernel_spmd(B2b, im, core_ids=list(range(NCORE)))
    agg8 = np.concatenate([rBb.results[c]["aggp"] for c in range(NCORE)],
                          axis=1)

    B2c = _get("B2c", build_B2c)
    im = []
    for c in range(NCORE):
        oiw_sl = oiw[c * 1024:(c + 1) * 1024]
        im.append({
            "agg8": np.ascontiguousarray(agg8),
            "oiw": chunk128(np.ascontiguousarray(oiw_sl.T)),
        })
    rBc = run_bass_kernel_spmd(B2c, im, core_ids=list(range(NCORE)))
    oacts = np.zeros((2, 8192), np.float32)
    for c in range(NCORE):
        oa = rBc.results[c]["oactp"]
        for ot in range(8):
            for b in range(2):
                oacts[b, c * 1024 + ot * 128:
                      c * 1024 + ot * 128 + 128] = oa[:, ot * 2 + b]

    B2c2 = _get("B2c2", build_B2c2)
    im = []
    for c in range(NCORE):
        perm3 = (list(range(c * 1024, (c + 1) * 1024))
                 + [j for j in range(8192) if not
                    (c * 1024 <= j < (c + 1) * 1024)])
        im.append({
            "oact": np.ascontiguousarray(oacts[:, perm3]),
            "opat": chunk128(opat[c * 1024:(c + 1) * 1024]),
        })
    rBc2 = run_bass_kernel_spmd(B2c2, im, core_ids=list(range(NCORE)))
    outbs = []
    for c in range(NCORE):
        obs_ = rBc2.results[c]["outbp"]
        obv = np.zeros((2, 1024), np.float32)
        for dt in range(8):
            for b in range(2):
                obv[b, dt * 128:(dt + 1) * 128] = obs_[:, dt * 2 + b]
        outbs.append(obv)

    B2d = _get("B2d", build_B2d)
    rw_arr = chunk128(rw3)
    im = []
    for c in range(NCORE):
        b = c // 4
        obp = np.concatenate([outbs[j][b] for j in range(NCORE)])
        im.append({
            "x0": x3p[c],
            "obp": np.ascontiguousarray(obp.reshape(1, 8 * 1024)),
            "rw": rw_arr,
        })
    rBd = run_bass_kernel_spmd(B2d, im, core_ids=list(range(NCORE)))
    x3 = np.concatenate([rBd.results[c]["x1o"] for c in range(NCORE)],
                        axis=0)
    xn3 = np.concatenate([rBd.results[c]["xnfm"] for c in range(NCORE)],
                         axis=1)
    maskw = np.concatenate(
        [rBd.results[c]["cmo"].reshape(128, NTT, 14).transpose(1, 0, 2)
         .reshape(TSL, 14) for c in range(NCORE)], axis=0)
    return x3, xn3, maskw, {"y2": y2full, "ia": ia_sums, "pacs": pacs,
                            "agg8": agg8, "oacts": oacts, "outbs": outbs}


def run_L3(inputs, x3, xn3, maskw):
    A_log = np.asarray(inputs["L3_attn_A_log"], np.float32)
    xpw_w = np.asarray(inputs["L3_attn_x_proj_w"], np.float32)
    dtw_w = np.asarray(inputs["L3_attn_dt_proj_w"], np.float32)
    dtb_w = np.asarray(inputs["L3_attn_dt_proj_b"], np.float32)
    Dv = np.asarray(inputs["L3_attn_D"], np.float32)
    w1 = np.asarray(inputs["L3_ffn_w1"], np.float32)
    b1 = np.asarray(inputs["L3_ffn_b1"], np.float32)
    w2 = np.asarray(inputs["L3_ffn_w2"], np.float32)
    b2 = np.asarray(inputs["L3_ffn_b2"], np.float32)

    selg = np.zeros((128, 16 * 128), np.float32)
    selr = np.zeros((128, 16 * 128), np.float32)
    for g in range(16):
        for p in range(128):
            selg[8 * g + p // 16, g * 128 + p] = 1.0
            selr[p, g * 128 + 8 * g + p // 16] = 1.0
    selbc = np.zeros((96, 256), np.float32)
    for p in range(128):
        selbc[64 + p % 16, p] = 1.0
        selbc[80 + p % 16, 128 + p] = 1.0
    mwT = np.ascontiguousarray(maskw.T.astype(np.float32))
    mws = np.ascontiguousarray(mwT.reshape(1, 14 * T))
    xpw_arr = chunk128(np.ascontiguousarray(xpw_w.T))

    A3 = _get("A3", build_A3)
    im = []
    for c in range(NCORE):
        dsl = slice(c * 128, (c + 1) * 128)
        csl = slice(c * 256, (c + 1) * 256)
        w1sl = w1[:, :, csl]
        w1c = np.ascontiguousarray(
            w1sl.reshape(14, 8, 128, 2, 128).transpose(2, 0, 1, 3, 4)
            .reshape(128, 28672))
        w2sl = w2[:, csl, :]
        w2c = np.ascontiguousarray(
            w2sl.reshape(14, 2, 128, 8, 128).transpose(2, 0, 1, 3, 4)
            .reshape(128, 28672))
        b1sl = b1[:, csl]
        b1c = np.ascontiguousarray(
            b1sl.reshape(14, 2, 128).transpose(2, 0, 1).reshape(128, 28))
        im.append({
            "xnf": xn3,
            "xnd": np.ascontiguousarray(xn3[dsl]),
            "xpw": xpw_arr,
            "dtw": np.ascontiguousarray(dtw_w[dsl].T),
            "dtb": np.ascontiguousarray(dtb_w[dsl].reshape(128, 1)),
            "alg": np.ascontiguousarray(np.tile(A_log, 8).reshape(128, 1)),
            "dvc": np.ascontiguousarray(Dv[dsl].reshape(128, 1)),
            "selg": selg,
            "selbc": selbc,
            "selr": selr,
            "mws": mws,
            "mw14": mwT,
            "w1c": w1c,
            "w2c": w2c,
            "b1c": b1c,
            "b2f": np.ascontiguousarray(b2),
        })
    rA = run_bass_kernel_spmd(A3, im, core_ids=list(range(NCORE)))
    y3full = np.concatenate([rA.results[c]["yo3"] for c in range(NCORE)],
                            axis=0)
    mo3s = [rA.results[c]["mo3"] for c in range(NCORE)]

    B3 = _get("B3", lambda: build_B_add(1, 8, ln_out=False))
    im = []
    for c in range(NCORE):
        sl = slice(c * TSL, (c + 1) * TSL)
        tb = np.concatenate(
            [np.ascontiguousarray(m.T[sl]) for m in mo3s], axis=0)
        im.append({
            "x0": x3[sl],
            "tf": np.ascontiguousarray(y3full[:, sl].T),
            "tb": tb,
        })
    rB = run_bass_kernel_spmd(B3, im, core_ids=list(range(NCORE)))
    x4 = np.concatenate([rB.results[c]["x1o"] for c in range(NCORE)],
                        axis=0)
    return x4, {"y3": y3full, "mo3s": mo3s}


def kernel(**inputs):
    x1, xn1, _ = run_L0(inputs)
    x2, xn2, _ = run_L1(inputs, x1, xn1)
    x3, xn3, maskw, _ = run_L2(inputs, x2, xn2)
    x4, _ = run_L3(inputs, x3, xn3, maskw)
    return np.ascontiguousarray(x4.reshape(2, S, 1024).astype(np.float32))
